# version 07
# baseline (speedup 1.0000x reference)
import math
import numpy as np
import ml_dtypes
import concourse.bass as bass
import concourse.mybir as mybir
from concourse.bass_utils import run_bass_kernel_spmd

F32 = mybir.dt.float32
BF16 = mybir.dt.bfloat16
AF = mybir.ActivationFunctionType
ALU = mybir.AluOpType
AX = mybir.AxisListType

T = 2048
D = 2048
NT = 16
DC = 16
HID = 1792
NHC = 14
NEXP = 8
L0_HQ, L0_HKV, L0_DH = 32, 8, 64
L0_SCALE = 0.125
L1_H, L1_DH = 16, 128
L1_RSCALE = float(np.float32(1.0) / np.float32(math.sqrt(128.0)))

QUEUES = {"qsync": "sync", "qgpsimd": "gpsimd"}
NLANES = 6
QSEMS = {f"{q}{l}": e for q, e in QUEUES.items() for l in range(NLANES)}
ENGS = ("tensor", "vector", "scalar", "gpsimd", "sync")

SBUF_BASE = 17408
SBUF_TOP = 229120


class Buf:
    __slots__ = ("name", "writer", "readers", "fence")

    def __init__(self, name, fence=None):
        self.name = name
        self.writer = None
        self.readers = []
        self.fence = fence


class _Op:
    __slots__ = ("eng", "sem", "fn", "args", "kwargs", "inc", "idx", "val",
                 "waits", "is_dma", "tag")

    def __init__(self, eng, sem, fn, args, kwargs, is_dma, tag):
        self.eng, self.sem, self.fn = eng, sem, fn
        self.args, self.kwargs = args, kwargs
        self.inc = is_dma
        self.idx = -1
        self.val = 0
        self.waits = []
        self.is_dma = is_dma
        self.tag = tag


class Emit:
    def __init__(self):
        self.ops = {e: [] for e in ENGS}
        self.counts = {s: 0 for s in list(ENGS) + list(QSEMS)}
        self.qcount = {q: 0 for q in QUEUES}
        self.last_lane_dma = {s: None for s in QSEMS}
        self.clock = {}
        self.tag = ""

    def buf(self, name, fence=None):
        return Buf(name, fence)

    def _edge(self, op, prod, is_read=False):
        if prod is None or prod is op:
            return
        if (not prod.is_dma) and prod.sem == op.eng:
            if (not is_read) or op.eng == "tensor":
                return
        key = (op.eng, prod.sem)
        if self.clock.get(key, -1) >= prod.idx:
            return
        self.clock[key] = prod.idx
        prod.inc = True
        op.waits.append(prod)

    def _record(self, eng, sem, fn, args, kwargs, reads, writes, is_dma):
        op = _Op(eng, sem, fn, args, kwargs, is_dma, self.tag)
        op.idx = self.counts[sem]
        self.counts[sem] = op.idx + 1
        for b in writes:
            if b.fence is not None:
                for ob in b.fence:
                    self._edge(op, ob.writer)
                    for r in ob.readers:
                        self._edge(op, r)
                b.fence = None
        for b in reads:
            self._edge(op, b.writer, is_read=True)
        for b in writes:
            self._edge(op, b.writer)
            for r in b.readers:
                self._edge(op, r)
        for b in reads:
            b.readers.append(op)
        for b in writes:
            b.writer = op
            b.readers = []
        self.ops[eng].append(op)
        return op

    def op(self, eng, fn, *args, r=(), w=(), **kwargs):
        return self._record(eng, eng, fn, args, kwargs, r, w, False)

    def dma(self, queue, out, in_, r=(), w=(), **kwargs):
        lane = self.qcount[queue] % NLANES
        self.qcount[queue] += 1
        sem = f"{queue}{lane}"
        eng = QUEUES[queue]
        op = self._record(eng, sem, "dma_start",
                          (), dict(out=out, in_=in_, **kwargs), r, w, True)
        prev = self.last_lane_dma[sem]
        if prev is not None and prev not in op.waits:
            key = (eng, sem)
            if self.clock.get(key, -1) < prev.idx:
                self.clock[key] = prev.idx
                op.waits.append(prev)
        self.last_lane_dma[sem] = op
        return op

    def mm_group(self, out_ap, pairs, r=(), w=()):
        n = len(pairs)
        ops = []
        for i, (lt, rh) in enumerate(pairs):
            rr = r if i == 0 else ()
            ww = w if (i == 0 or i == n - 1) else ()
            ops.append(self._record("tensor", "tensor", "matmul",
                                    (out_ap, lt, rh),
                                    dict(start=(i == 0), stop=(i == n - 1)),
                                    rr, ww, False))
        for b in r:
            for i in range(n - 1):
                if ops[i] in b.readers:
                    b.readers.remove(ops[i])
            if ops[-1] not in b.readers:
                b.readers.append(ops[-1])
        return ops[-1]

    def finalize(self, nc, sems):
        cums = {s: 0 for s in self.counts}
        for e in ENGS:
            for op in self.ops[e]:
                s = op.sem
                if op.is_dma:
                    cums[s] += 16
                    op.val = cums[s]
                else:
                    if op.inc:
                        cums[s] += 1
                    op.val = cums[s]

        def emit_stream(engobj, ename):
            for op in self.ops[ename]:
                for prod in op.waits:
                    engobj.wait_ge(sems[prod.sem], prod.val)
                inst = getattr(engobj, op.fn)(*op.args, **op.kwargs)
                if op.inc:
                    inst.then_inc(sems[op.sem], 16 if op.is_dma else 1)
            if ename == "sync":
                for q in QSEMS:
                    if cums[q] > 0:
                        engobj.wait_ge(sems[q], cums[q])

        with nc.Block() as block:
            block.tensor(lambda t: emit_stream(t, "tensor"))
            block.vector(lambda v: emit_stream(v, "vector"))
            block.scalar(lambda s: emit_stream(s, "scalar"))
            block.gpsimd(lambda g: emit_stream(g, "gpsimd"))
            block.sync(lambda s: emit_stream(s, "sync"))


class Arena:
    def __init__(self, nc, em, base, limit):
        self.nc, self.em = nc, em
        self.base, self.limit = base, limit
        self.off = base
        self.n = 0
        self.cur = []
        self.pending = []

    def alloc(self, name, shape, dtype):
        nbytes = shape[1] * (2 if dtype == BF16 else 4)
        nbytes = (nbytes + 31) & ~31
        assert self.off + nbytes <= self.limit, \
            f"SBUF arena overflow at {name}: {self.off}+{nbytes}>{self.limit}"
        t = self.nc.alloc_sbuf_tensor_at(f"{name}_{self.n}", shape, dtype,
                                         offset=self.off)
        self.n += 1
        self.off += nbytes
        b = self.em.buf(name, fence=self.pending if self.pending else None)
        self.cur.append(b)
        return t, b

    def reset(self, to=None):
        self.off = self.base if to is None else to
        self.pending = self.pending + self.cur
        self.cur = []


def build_program(debug_taps=(), phases=("ln0", "p1", "p2", "ln1", "p4", "p5")):
    nc = bass.Bass("TRN2", target_bir_lowering=False, debug=False)
    em = Emit()

    dt = lambda n, s, k="ExternalInput", d=F32: nc.dram_tensor(n, s, d, kind=k).ap()
    x_h = dt("x", [T, D])
    wqTs = dt("wqTs", [16, 128, DC * 128])
    wkTs = dt("wkTs", [4, 128, DC * 128])
    wvTs = dt("wvTs", [4, 128, DC * 128])
    woTs = dt("woTs", [16, 128, DC * 128])
    w0_h = dt("w0", [D, D])
    sclbc = dt("sclbc", [128, D])
    Wqs = dt("Wqs", [8, 128, DC * 256]); Wks = dt("Wks", [8, 128, DC * 256])
    Wvs = dt("Wvs", [8, 128, DC * 256]); Wus = dt("Wus", [8, 128, DC * 256])
    Wos = dt("Wos", [4, 128, DC * 512])
    gw_h = dt("gw", [D, 8])
    w1s_h = dt("w1s", [NEXP + 1, NHC, 128, DC * 128])
    w3s_h = dt("w3s", [NEXP + 1, NHC, 128, DC * 128])
    w2s_h = dt("w2s", [NEXP + 1, 4, 128, NHC * 512])
    biasE_h = dt("biasE", [128, 8])
    cst_h = dt("cst", [128, 160])
    idb_h = dt("idb", [128, 128], d=BF16)
    idf_h = dt("idf", [128, 128])
    oh_h = dt("oh", [8, 8 * 128])
    mask01_h = dt("mask01", [128, 4 * 512], d=BF16)
    y_h = dt("y", [T, D], k="ExternalOutput")
    kind_scr = "ExternalOutput" if "scr" in debug_taps else "Internal"
    s_scr = dt("s_scr", [T, D], k=kind_scr)
    x1_scr = dt("x1_scr", [T, D], k=kind_scr)
    acc_scr = dt("acc_scr", [T, D], k=kind_scr)
    xnd = {}
    for tap in ("xnd0", "xnd1"):
        if tap in debug_taps:
            xnd[tap] = dt(tap, [128, DC * T], k="ExternalOutput", d=BF16)

    off = SBUF_BASE

    def fixed(name, shape, dtyp):
        nonlocal off
        nbytes = shape[1] * (2 if dtyp == BF16 else 4)
        nbytes = (nbytes + 31) & ~31
        t = nc.alloc_sbuf_tensor_at(name, shape, dtyp, offset=off)
        off += nbytes
        return t

    cst = fixed("cstt", [128, 160], F32)
    idb = fixed("idbt", [128, 128], BF16)
    idf = fixed("idft", [128, 128], F32)
    biasE = fixed("biasEt", [128, 8], F32)
    gw_sb = fixed("gw_sbt", [128, DC * 8], BF16)
    gates = fixed("gatest", [128, NT * 8], F32)
    st = fixed("statst", [128, 48], F32)
    st2 = fixed("stats2t", [1, 1024], F32)
    xn_ft = fixed("xn_ftt", [128, DC * T], BF16)

    arena = Arena(nc, em, (off + 31) & ~31, SBUF_TOP)

    c_neg20 = cst[:, 0:1]
    ones128 = cst[:, 1:2]
    ones_row64 = cst[0:1, 8:72]
    ones_row128 = cst[0:1, 8:136]

    pb = [nc.alloc_psum_tensor(f"pb{i}", [128, 512], F32) for i in range(7)]
    pbT = nc.alloc_psum_tensor("pbT", [128, 1024], BF16)
    Bpb = [em.buf(f"pb{i}") for i in range(7)]
    BpbT = em.buf("pbT")

    Bconst = em.buf("const")
    Bxn = em.buf("xn_ft")
    Bgates = em.buf("gates")
    Bst = em.buf("stats")
    Bst2 = em.buf("stats2")
    Bs_scr = em.buf("s_scr")
    Bx1_scr = em.buf("x1_scr")
    Bacc_scr = [em.buf(f"acc_scr{i}") for i in range(64)]

    em.tag = "init"
    em.dma("qsync", cst[:, :], cst_h[:, :], w=(Bconst,))
    em.dma("qsync", idb[:, :], idb_h[:, :], w=(Bconst,))
    em.dma("qsync", idf[:, :], idf_h[:, :], w=(Bconst,))
    em.dma("qsync", biasE[:, :], biasE_h[:, :], w=(Bconst,))
    em.dma("qgpsimd", gw_sb[:, :].rearrange("p (c e) -> p c e", c=DC),
           gw_h[:, :].rearrange("(c p) e -> p c e", p=128), w=(Bconst,))

    def gate_tile(tt, zpb, Bzpb, gz, Bgz, gst, Bst):
        sc, e_, tmp, p_, q_ = (gz[i][:, 0:8] for i in range(5))
        Bsc, Be, Btmp, Bp, Bq = Bgz
        zb = gst[:, 0:8]
        mx = gst[:, 8:9]; nm = gst[:, 9:10]; ssum = gst[:, 10:11]
        rs = gst[:, 11:12]; m_ = gst[:, 12:13]
        m1_ap = gst[:, 16:24]; m2_ap = gst[:, 24:32]
        em.op("vector", "reduce_max", mx, zpb[0:128, 0:8], axis=AX.X,
              r=(Bzpb,), w=(Bst,))
        em.op("vector", "tensor_scalar", nm, mx, -1.0, None, ALU.mult,
              r=(Bst,), w=(Bst,))
        em.op("scalar", "activation", e_, zpb[0:128, 0:8], AF.Exp,
              bias=nm, r=(Bzpb, Bst), w=(Be,))
        em.op("vector", "reduce_sum", ssum, e_, axis=AX.X, r=(Be,), w=(Bst,))
        em.op("vector", "reciprocal", rs, ssum, r=(Bst,), w=(Bst,))
        em.op("vector", "tensor_scalar", sc, e_, rs, None, ALU.mult,
              r=(Be, Bst), w=(Bsc,))
        em.op("vector", "tensor_tensor", zb, sc, biasE[:, :], ALU.add,
              r=(Bsc, Bconst, Bst), w=(Bst,))

        def top1(dst_ap):
            em.op("vector", "reduce_max", m_, zb, axis=AX.X, r=(Bst,), w=(Bst,))
            em.op("vector", "tensor_scalar", tmp, zb, m_, None, ALU.is_ge,
                  r=(Bst,), w=(Btmp,))
            em.op("vector", "tensor_copy", e_[:, 0:1], tmp[:, 0:1],
                  r=(Btmp,), w=(Be,))
            em.op("vector", "tensor_tensor", e_[:, 1:8], tmp[:, 1:8],
                  tmp[:, 0:7], ALU.add, r=(Btmp,), w=(Be,))
            em.op("vector", "tensor_copy", p_[:, 0:2], e_[:, 0:2],
                  r=(Be,), w=(Bp,))
            em.op("vector", "tensor_tensor", p_[:, 2:8], e_[:, 2:8],
                  e_[:, 0:6], ALU.add, r=(Be,), w=(Bp,))
            em.op("vector", "tensor_copy", q_[:, 0:4], p_[:, 0:4],
                  r=(Bp,), w=(Bq,))
            em.op("vector", "tensor_tensor", q_[:, 4:8], p_[:, 4:8],
                  p_[:, 0:4], ALU.add, r=(Bp,), w=(Bq,))
            em.op("vector", "tensor_tensor", q_, q_, tmp, ALU.subtract,
                  r=(Bq, Btmp), w=(Bq,))
            em.op("vector", "tensor_scalar", q_, q_, 0.0, None, ALU.is_equal,
                  r=(Bq,), w=(Bq,))
            em.op("vector", "tensor_tensor", dst_ap, tmp, q_, ALU.mult,
                  r=(Btmp, Bq), w=(Bst,))

        top1(m1_ap)
        em.op("vector", "scalar_tensor_tensor", zb, m1_ap, -1e30, zb,
              ALU.mult, ALU.add, r=(Bst,), w=(Bst,))
        top1(m2_ap)
        em.op("vector", "tensor_tensor", m1_ap, m1_ap, m2_ap, ALU.add,
              r=(Bst,), w=(Bst,))
        em.op("vector", "tensor_tensor", gates[:, tt * 8:(tt + 1) * 8],
              sc, m1_ap, ALU.mult, r=(Bsc, Bst), w=(Bgates,))

    def phase_ln(src_h, Bsrc, layer, tap=None):
        em.tag = f"ln{layer}"
        arena.reset()
        xt, Bxt, xnt, Bxnt, sq, Bsq = [], [], [], [], [], []
        for i in range(3):
            t_, b_ = arena.alloc("ln_x", [128, D], F32); xt.append(t_); Bxt.append(b_)
        for i in range(2):
            t_, b_ = arena.alloc("ln_xn", [128, D], BF16); xnt.append(t_); Bxnt.append(b_)
            t_, b_ = arena.alloc("ln_sq", [128, D], F32); sq.append(t_); Bsq.append(b_)
        lst, Blst = [], []
        for par in range(2):
            t_, b_ = arena.alloc("ln_st", [128, 12], F32); lst.append(t_); Blst.append(b_)

        for tt in range(NT):
            sl = tt % 2
            sl3 = tt % 3
            Bst = Blst[sl]
            s_sum = lst[sl][:, 0:1]; s_ss = lst[sl][:, 1:2]
            s_m = lst[sl][:, 2:3]; s_var = lst[sl][:, 3:4]
            s_r0 = lst[sl][:, 4:5]; s_e = lst[sl][:, 5:6]
            s_e2 = lst[sl][:, 6:7]; s_h = lst[sl][:, 7:8]
            s_rstd = lst[sl][:, 8:9]
            rd = (Bsrc,) if Bsrc else ()
            em.dma("qsync", xt[sl3][:, :], src_h[tt * 128:(tt + 1) * 128, :],
                   r=rd, w=(Bxt[sl3],))
            em.op("vector", "reduce_sum", s_sum, xt[sl3][:, :], axis=AX.X,
                  r=(Bxt[sl3],), w=(Bst,))
            em.op("scalar", "activation", sq[sl][:, :], xt[sl3][:, :],
                  AF.Square, r=(Bxt[sl3],), w=(Bsq[sl],))
            em.op("vector", "reduce_sum", s_ss, sq[sl][:, :], axis=AX.X,
                  r=(Bsq[sl],), w=(Bst,))
            v = lambda fn, *a, **k: em.op("vector", fn, *a, r=(Bst,), w=(Bst,), **k)
            v("tensor_scalar", s_m, s_sum, 1.0 / D, None, ALU.mult)
            v("tensor_scalar", s_var, s_ss, 1.0 / D, None, ALU.mult)
            v("tensor_tensor", s_e, s_m, s_m, ALU.mult)
            v("tensor_tensor", s_var, s_var, s_e, ALU.subtract)
            v("tensor_scalar", s_var, s_var, 1e-5, None, ALU.add)
            em.op("scalar", "activation", s_e, s_var, AF.Ln, r=(Bst,), w=(Bst,))
            em.op("scalar", "activation", s_r0, s_e, AF.Exp, scale=-0.5,
                  r=(Bst,), w=(Bst,))
            v("tensor_tensor", s_e, s_var, s_r0, ALU.mult)
            v("tensor_tensor", s_e2, s_e, s_r0, ALU.mult)
            v("tensor_scalar", s_h, s_e2, -0.5, 1.5, ALU.mult, ALU.add)
            v("tensor_tensor", s_rstd, s_r0, s_h, ALU.mult)
            em.op("vector", "tensor_scalar", xnt[sl][:, :], xt[sl3][:, :],
                  s_m, s_rstd, ALU.subtract, ALU.mult,
                  r=(Bxt[sl3], Bst), w=(Bxnt[sl],))
            for g8 in range(2):
                for j in range(8):
                    dc = g8 * 8 + j
                    em.op("tensor", "transpose", pbT[:, j * 128:(j + 1) * 128],
                          xnt[sl][:, dc * 128:(dc + 1) * 128], idb[:, :],
                          r=(Bxnt[sl], Bconst), w=(BpbT,))
                dst = xn_ft[:, :].rearrange("p (c n) -> p c n", c=DC)[
                    :, g8 * 8:(g8 + 1) * 8, tt * 128:(tt + 1) * 128]
                src = pbT[:, :].rearrange("p (c n) -> p c n", c=8)
                em.op("vector", "tensor_copy", dst, src, r=(BpbT,), w=(Bxn,))

        if tap is not None:
            em.dma("qsync", xnd[tap][:, :], xn_ft[:, :], r=(Bxn,))
        arena.reset()

    def phase_p1(sub=4):
        em.tag = "p1"
        arena.reset()
        k64, Bk64 = arena.alloc("k64", [64, L0_HKV * T], BF16)
        q64, Bq64 = arena.alloc("q64", [64, L0_HQ * 512], BF16)
        v_aug, Bva = arena.alloc("v_aug", [128, NT * 520], BF16)
        attn_ft, Baf = arena.alloc("attn_ft", [128, DC * 512], BF16)
        stg, Bstg, wsl, Bwsl, wosl, Bwo, ptl, Bpt = [], [], [], [], [], [], [], []
        for i in range(2):
            t_, b_ = arena.alloc("attn_stg", [64, 512], BF16); stg.append(t_); Bstg.append(b_)
            t_, b_ = arena.alloc("wsl", [128, DC * 128], BF16); wsl.append(t_); Bwsl.append(b_)
            t_, b_ = arena.alloc("wosl", [128, DC * 128], BF16); wosl.append(t_); Bwo.append(b_)
        for i in range(4):
            t_, b_ = arena.alloc("pt", [128, 512], BF16); ptl.append(t_); Bpt.append(b_)
        mask01, Bm01 = arena.alloc("mask01", [128, 4 * 512], BF16)
        s0st, Bs0l = [], []
        for i in range(4):
            t_, b_ = arena.alloc("s0st", [128, 256], F32); s0st.append(t_); Bs0l.append(b_)
        atmp, Batmp, rnrow, Brn = [], [], [], []
        for i in range(2):
            t_, b_ = arena.alloc("atmp", [64, 512], F32)
            atmp.append(t_); Batmp.append(b_)
            t_, b_ = arena.alloc("rnrow", [1, 512], F32)
            rnrow.append(t_); Brn.append(b_)
        scj = [0]

        em.dma("qsync", mask01[:, :], mask01_h[:, :], w=(Bm01,))
        em.op("vector", "memset",
              v_aug[:, :].rearrange("p (t g c) -> p t g c", t=NT, g=8)[:, :, :, 64:65],
              1.0, w=(Bva,))

        def load_wpanel(slot, Bslot, staged_h, pnl, ncols):
            em.dma("qgpsimd", slot[:, 0:DC * ncols], staged_h[pnl],
                   w=(Bslot,))

        bankctr = [0]

        def pingbank():
            bankctr[0] ^= 1
            return 5 + bankctr[0]

        for pnl in range(4):
            sl = pnl % 2
            load_wpanel(wsl[sl], Bwsl[sl], wkTs, pnl, 128)
            for gg in range(2):
                g = pnl * 2 + gg
                for tc in range(4):
                    bk = pingbank()
                    em.mm_group(
                        pb[bk][0:64, 0:512],
                        [(wsl[sl][:, dc * 128 + gg * 64: dc * 128 + gg * 64 + 64],
                          xn_ft[:, dc * T + tc * 512:dc * T + (tc + 1) * 512])
                         for dc in range(DC)],
                        r=(Bwsl[sl], Bxn), w=(Bpb[bk],))
                    em.op("scalar", "copy",
                          k64[:, g * T + tc * 512:g * T + (tc + 1) * 512],
                          pb[bk][0:64, 0:512], r=(Bpb[bk],), w=(Bk64,))
        for pnl in range(4):
            sl = pnl % 2
            load_wpanel(wsl[sl], Bwsl[sl], wvTs, pnl, 128)
            for tt in range(NT):
                bk = pingbank()
                em.mm_group(
                    pb[bk][0:128, 0:128],
                    [(xn_ft[:, dc * T + tt * 128:dc * T + (tt + 1) * 128],
                      wsl[sl][:, dc * 128:(dc + 1) * 128]) for dc in range(DC)],
                    r=(Bwsl[sl], Bxn), w=(Bpb[bk],))
                for gg in range(2):
                    g = pnl * 2 + gg
                    em.op("scalar", "copy",
                          v_aug[:, tt * 520 + g * 65: tt * 520 + g * 65 + 64],
                          pb[bk][0:128, gg * 64:gg * 64 + 64],
                          r=(Bpb[bk],), w=(Bva,))

        for qc in range(4 if sub >= 2 else 0):
            for pnl in range(16):
                sl = pnl % 2
                load_wpanel(wsl[sl], Bwsl[sl], wqTs, pnl, 128)
                for gg in range(2):
                    h = pnl * 2 + gg
                    bk = pingbank()
                    em.mm_group(
                        pb[bk][0:64, 0:512],
                        [(wsl[sl][:, dc * 128 + gg * 64: dc * 128 + gg * 64 + 64],
                          xn_ft[:, dc * T + qc * 512:dc * T + (qc + 1) * 512])
                         for dc in range(DC)],
                        r=(Bwsl[sl], Bxn), w=(Bpb[bk],))
                    em.op("vector", "tensor_scalar",
                          q64[:, h * 512:(h + 1) * 512],
                          pb[bk][0:64, 0:512], L0_SCALE, None, ALU.mult,
                          r=(Bpb[bk],), w=(Bq64,))
            nkt = 4 * qc + 4
            for hp in range(L0_HQ // 2 if sub >= 3 else 0):
                hA, hB = 2 * hp, 2 * hp + 1
                g = hA // 4
                pvA, pvB = 5, 6

                def score(h, kt, ptb, Bptb):
                    bk = scj[0] % 4
                    scj[0] += 1
                    em.op("tensor", "matmul", pb[bk][0:128, 0:512],
                          k64[:, g * T + kt * 128:g * T + (kt + 1) * 128],
                          q64[:, h * 512:(h + 1) * 512],
                          start=True, stop=True,
                          r=(Bk64, Bq64), w=(Bpb[bk],))
                    em.op("scalar", "activation", ptb[:, :],
                          pb[bk][0:128, 0:512], AF.Exp, bias=c_neg20,
                          r=(Bpb[bk], Bconst), w=(Bptb,))
                    if kt >= 4 * qc:
                        i = kt - 4 * qc
                        em.op("vector", "tensor_tensor", ptb[:, :],
                              ptb[:, :], mask01[:, i * 512:(i + 1) * 512],
                              ALU.mult, r=(Bptb, Bm01), w=(Bptb,))

                def pv(kt, pvb, ptb, Bptb):
                    em.op("tensor", "matmul", pb[pvb][0:65, 0:512],
                          v_aug[:, kt * 520 + g * 65: kt * 520 + (g + 1) * 65],
                          ptb[:, :], start=(kt == 0), stop=(kt == nkt - 1),
                          r=(Bva, Bptb), w=(Bpb[pvb],))

                for kt in range(nkt):
                    score(hA, kt, ptl[kt % 2], Bpt[kt % 2])
                    score(hB, kt, ptl[2 + kt % 2], Bpt[2 + kt % 2])
                    if kt > 0:
                        pv(kt - 1, pvA, ptl[(kt - 1) % 2], Bpt[(kt - 1) % 2])
                        pv(kt - 1, pvB, ptl[2 + (kt - 1) % 2],
                           Bpt[2 + (kt - 1) % 2])
                pv(nkt - 1, pvA, ptl[(nkt - 1) % 2], Bpt[(nkt - 1) % 2])
                pv(nkt - 1, pvB, ptl[2 + (nkt - 1) % 2],
                   Bpt[2 + (nkt - 1) % 2])

                for (h, pvb, rn_, Brn_, at_, Bat_) in (
                        (hA, pvA, rnrow[0], Brn[0], atmp[0], Batmp[0]),
                        (hB, pvB, rnrow[1], Brn[1], atmp[1], Batmp[1])):
                    em.op("vector", "reciprocal", rn_[0:1, :],
                          pb[pvb][64:65, 0:512], r=(Bpb[pvb],), w=(Brn_,))
                    em.op("tensor", "matmul", pb[4][0:64, 0:512], ones_row64,
                          rn_[0:1, :], start=True, stop=True,
                          r=(Brn_, Bconst), w=(Bpb[4],))
                    em.op("scalar", "copy", at_[:, :], pb[pvb][0:64, 0:512],
                          r=(Bpb[pvb],), w=(Bat_,))
                    if h % 2 == 0:
                        em.op("vector", "tensor_tensor",
                              attn_ft[0:64, (h // 2) * 512:(h // 2 + 1) * 512],
                              at_[:, :], pb[4][0:64, 0:512], ALU.mult,
                              r=(Bat_, Bpb[4]), w=(Baf,))
                    else:
                        ssl = (h // 2) % 2
                        em.op("vector", "tensor_tensor", stg[ssl][:, :],
                              at_[:, :], pb[4][0:64, 0:512], ALU.mult,
                              r=(Bat_, Bpb[4]), w=(Bstg[ssl],))
                        em.dma("qsync",
                               attn_ft[64:128,
                                       (h // 2) * 512:(h // 2 + 1) * 512],
                               stg[ssl][:, :], r=(Bstg[ssl],), w=(Baf,))
            for np_ in range((16 if sub >= 5 else 2) if sub >= 4 else 0):
                sl = np_ % 2
                load_wpanel(wosl[sl], Bwo[sl], woTs, np_, 128)
                for ts in range(4):
                    bk = pingbank()
                    st_i = (np_ * 4 + ts) % 4
                    em.mm_group(
                        pb[bk][0:128, 0:128],
                        [(attn_ft[:, dc * 512 + ts * 128:dc * 512 + (ts + 1) * 128],
                          wosl[sl][:, dc * 128:(dc + 1) * 128]) for dc in range(DC)],
                        r=(Bwo[sl], Baf), w=(Bpb[bk],))
                    em.op("scalar", "copy", s0st[st_i][:, 0:128],
                          pb[bk][0:128, 0:128], r=(Bpb[bk],), w=(Bs0l[st_i],))
                    em.dma("qsync",
                           s_scr[qc * 512 + ts * 128: qc * 512 + (ts + 1) * 128,
                                 np_ * 128:(np_ + 1) * 128],
                           s0st[st_i][:, 0:128], r=(Bs0l[st_i],), w=(Bs_scr,))
        arena.reset()

    def phase_p2():
        em.tag = "p2"
        arena.reset()
        w0sb, Bw0 = arena.alloc("w0sb", [128, DC * D], BF16)
        scl, Bscl = arena.alloc("scl", [128, D], F32)
        o_, Bo, o2, Bo2, s_, Bs, x_, Bx = [], [], [], [], [], [], [], []
        stt, Bstt = [], []
        for i in range(2):
            t_, b_ = arena.alloc("p2o", [128, D], F32); o_.append(t_); Bo.append(b_)
            t_, b_ = arena.alloc("p2o2", [128, D], F32); o2.append(t_); Bo2.append(b_)
            t_, b_ = arena.alloc("p2s", [128, D], F32); s_.append(t_); Bs.append(b_)
            t_, b_ = arena.alloc("p2x", [128, D], F32); x_.append(t_); Bx.append(b_)
            t_, b_ = arena.alloc("p2st", [128, 24], F32); stt.append(t_); Bstt.append(b_)

        em.dma("qsync", scl[:, :], sclbc[:, :], w=(Bscl,))
        for ch in range(DC):
            em.dma("qgpsimd", w0sb[:, ch * D:(ch + 1) * D],
                   w0_h[ch * 128:(ch + 1) * 128, :], w=(Bw0,))
        def p2_chain(tt):
            sl = tt % 2
            stp = stt[sl]
            Bst = Bstt[sl]
            ss = stp[:, 0:1]; ssm = stp[:, 1:2]; lnv = stp[:, 2:3]
            r0 = stp[:, 3:4]; e1 = stp[:, 4:5]; e2 = stp[:, 5:6]
            hh = stp[:, 6:7]; rv = stp[:, 7:8]; vn = stp[:, 8:9]
            e2x = stp[:, 9:10]; dd = stp[:, 10:11]; rd_ = stp[:, 11:12]
            th = stp[:, 12:13]; dn = stp[:, 13:14]; rdn = stp[:, 14:15]
            f1 = stp[:, 15:16]; onrm = stp[:, 16:17]; ron = stp[:, 17:18]
            cc = stp[:, 18:19]; mlt = stp[:, 19:20]
            em.op("scalar", "activation", o2[sl][:, :], o_[sl][:, :], AF.Square,
                  r=(Bo[sl],), w=(Bo2[sl],))
            em.op("vector", "reduce_sum", ss, o2[sl][:, :], axis=AX.X,
                  r=(Bo2[sl],), w=(Bst,))
            v = lambda fn, *a, **k: em.op("vector", fn, *a, r=(Bst,), w=(Bst,), **k)
            v("tensor_scalar", ssm, ss, 1e-14, None, ALU.max)
            em.op("scalar", "activation", lnv, ssm, AF.Ln, r=(Bst,), w=(Bst,))
            em.op("scalar", "activation", r0, lnv, AF.Exp, scale=-0.5,
                  r=(Bst,), w=(Bst,))
            v("tensor_tensor", e1, ssm, r0, ALU.mult)
            v("tensor_tensor", e2, e1, r0, ALU.mult)
            v("tensor_scalar", hh, e2, -0.5, 1.5, ALU.mult, ALU.add)
            v("tensor_tensor", rv, r0, hh, ALU.mult)
            v("tensor_tensor", vn, ssm, rv, ALU.mult)
            em.op("scalar", "activation", e2x, vn, AF.Exp, scale=2.0,
                  r=(Bst,), w=(Bst,))
            v("tensor_scalar", dd, e2x, 1.0, None, ALU.add)
            em.op("vector", "reciprocal", rd_, dd, r=(Bst,), w=(Bst,))
            v("tensor_scalar", th, rd_, -2.0, 1.0, ALU.mult, ALU.add)
            v("tensor_scalar", dn, vn, 1e-7, None, ALU.add)
            em.op("vector", "reciprocal", rdn, dn, r=(Bst,), w=(Bst,))
            v("tensor_tensor", f1, th, rdn, ALU.mult)
            v("tensor_tensor", onrm, f1, vn, ALU.mult)
            v("tensor_scalar", onrm, onrm, 1e-7, None, ALU.max)
            em.op("vector", "reciprocal", ron, onrm, r=(Bst,), w=(Bst,))
            v("tensor_scalar", cc, ron, 0.99, None, ALU.mult)
            v("tensor_scalar", cc, cc, 1.0, None, ALU.min)
            v("tensor_tensor", mlt, f1, cc, ALU.mult)
            em.op("vector", "tensor_scalar", o2[sl][:, :], o_[sl][:, :], mlt,
                  None, ALU.mult, r=(Bo[sl], Bst), w=(Bo2[sl],))
            em.op("scalar", "activation", o_[sl][:, :], o2[sl][:, :], AF.Silu,
                  r=(Bo2[sl],), w=(Bo[sl],))
            em.op("gpsimd", "tensor_tensor", s_[sl][:, :], s_[sl][:, :],
                  o_[sl][:, :], ALU.mult, r=(Bs[sl], Bo[sl]), w=(Bs[sl],))
            em.op("vector", "tensor_tensor", x_[sl][:, :], x_[sl][:, :],
                  s_[sl][:, :], ALU.add, r=(Bx[sl], Bs[sl]), w=(Bx[sl],))
            em.dma("qsync", x1_scr[tt * 128:(tt + 1) * 128, :], x_[sl][:, :],
                   r=(Bx[sl],), w=(Bx1_scr,))

        for tt in range(NT):
            sl = tt % 2
            em.dma("qsync", s_[sl][:, :], s_scr[tt * 128:(tt + 1) * 128, :],
                   r=(Bs_scr,), w=(Bs[sl],))
            em.dma("qsync", x_[sl][:, :], x_h[tt * 128:(tt + 1) * 128, :],
                   w=(Bx[sl],))
            for kc in range(4):
                bank = (tt * 4 + kc) % 4
                em.mm_group(
                    pb[bank][0:128, 0:512],
                    [(xn_ft[:, dc * T + tt * 128:dc * T + (tt + 1) * 128],
                      w0sb[:, dc * D + kc * 512:dc * D + kc * 512 + 512])
                     for dc in range(DC)],
                    r=(Bxn, Bw0), w=(Bpb[bank],))
                em.op("vector", "tensor_tensor",
                      o_[sl][:, kc * 512:(kc + 1) * 512],
                      pb[bank][0:128, 0:512], scl[:, kc * 512:(kc + 1) * 512],
                      ALU.mult, r=(Bpb[bank], Bscl), w=(Bo[sl],))
            if tt > 0:
                p2_chain(tt - 1)
        p2_chain(NT - 1)
        arena.reset()

    def phase_p4():
        em.tag = "p4"
        arena.reset()
        kv_sb, Bkv = arena.alloc("kv_sb", [128, D], F32)
        kv_bf, Bkvb = arena.alloc("kv_bf", [128, D], BF16)
        keep = arena.off
        wk_p, Bwk, wv_p, Bwv, kt_, Bkt, vt_, Bvt = [], [], [], [], [], [], [], []
        for i in range(2):
            t_, b_ = arena.alloc("wk_p", [128, DC * 256], BF16); wk_p.append(t_); Bwk.append(b_)
            t_, b_ = arena.alloc("wv_p", [128, DC * 256], BF16); wv_p.append(t_); Bwv.append(b_)
            t_, b_ = arena.alloc("ktile", [128, 256], BF16); kt_.append(t_); Bkt.append(b_)
            t_, b_ = arena.alloc("vtile", [128, 256], BF16); vt_.append(t_); Bvt.append(b_)

        gz4, Bgz4, gst4, Bgst4 = [], [], [], []
        for par in range(2):
            z5, Bz5 = [], []
            for i in range(5):
                t_, b_ = arena.alloc("gz4", [128, 8], F32)
                z5.append(t_); Bz5.append(b_)
            gz4.append(z5); Bgz4.append(Bz5)
            t_, b_ = arena.alloc("gst4", [128, 32], F32)
            gst4.append(t_); Bgst4.append(b_)

        em.op("vector", "memset", kv_sb[:, :], 0.0, w=(Bkv,))

        def load_np(slot, Bslot, staged_h, pnl, ncols):
            em.dma("qgpsimd", slot[:, 0:DC * ncols], staged_h[pnl],
                   w=(Bslot,))

        def emit_gate(gti):
            par = gti % 2
            em.mm_group(
                pb[6][0:128, 0:8],
                [(xn_ft[:, d2 * T + gti * 128: d2 * T + (gti + 1) * 128],
                  gw_sb[:, d2 * 8:(d2 + 1) * 8]) for d2 in range(DC)],
                r=(Bxn, Bconst), w=(Bpb[6],))
            gate_tile(gti, pb[6], Bpb[6], gz4[par], Bgz4[par],
                      gst4[par], Bgst4[par])

        def kv_op(pidx):
            nc8p = pidx // 16
            slp = pidx % 2
            ob = 4 + pidx % 2
            for hh in range(2):
                em.op("tensor", "matmul",
                      pb[ob][0:128, hh * 128:(hh + 1) * 128],
                      kt_[slp][:, hh * 128:(hh + 1) * 128],
                      vt_[slp][:, hh * 128:(hh + 1) * 128],
                      start=True, stop=True,
                      r=(Bkt[slp], Bvt[slp]), w=(Bpb[ob],))
            em.op("vector", "tensor_tensor",
                  kv_sb[:, nc8p * 256:(nc8p + 1) * 256],
                  kv_sb[:, nc8p * 256:(nc8p + 1) * 256],
                  pb[ob][0:128, 0:256], ALU.add,
                  r=(Bkv, Bpb[ob]), w=(Bkv,))

        for idx in range(128):
            nc8, tt = idx // 16, idx % 16
            sl = idx % 2
            wsl_ = nc8 % 2
            if tt == 0:
                load_np(wk_p[wsl_], Bwk[wsl_], Wks, nc8, 256)
                load_np(wv_p[wsl_], Bwv[wsl_], Wvs, nc8, 256)
            em.mm_group(
                pb[sl][0:128, 0:256],
                [(xn_ft[:, dc * T + tt * 128:dc * T + (tt + 1) * 128],
                  wk_p[wsl_][:, dc * 256:(dc + 1) * 256]) for dc in range(DC)],
                r=(Bxn, Bwk[wsl_]), w=(Bpb[sl],))
            em.op("vector", "tensor_scalar", kt_[sl][:, :],
                  pb[sl][0:128, 0:256], 0.0, L1_RSCALE, ALU.max, ALU.mult,
                  r=(Bpb[sl],), w=(Bkt[sl],))
            em.mm_group(
                pb[2 + sl][0:128, 0:256],
                [(xn_ft[:, dc * T + tt * 128:dc * T + (tt + 1) * 128],
                  wv_p[wsl_][:, dc * 256:(dc + 1) * 256]) for dc in range(DC)],
                r=(Bxn, Bwv[wsl_]), w=(Bpb[2 + sl],))
            em.op("scalar", "copy", vt_[sl][:, :], pb[2 + sl][0:128, 0:256],
                  r=(Bpb[2 + sl],), w=(Bvt[sl],))
            if idx > 0:
                kv_op(idx - 1)
            if idx % 8 == 7:
                emit_gate(idx // 8)
        kv_op(127)
        em.op("scalar", "copy", kv_bf[:, :], kv_sb[:, :], r=(Bkv,), w=(Bkvb,))

        arena.reset(keep)
        q_sb, Bq = arena.alloc("q_sb", [128, L1_H * 512], BF16)
        u_sb, Bu = arena.alloc("u_sb", [128, L1_H * 512], BF16)
        out_sb, Bo4 = arena.alloc("out_sb", [128, L1_H * 512], BF16)
        z_sb, Bz = arena.alloc("z_sb", [128, DC * 512], BF16)
        wp, Bwp = [], []
        for i in range(3):
            t_, b_ = arena.alloc("wp4", [128, DC * 512], BF16); wp.append(t_); Bwp.append(b_)
        sqs, Bsq = [], []
        for i in range(2):
            t_, b_ = arena.alloc("sqs", [128, 512], F32)
            sqs.append(t_); Bsq.append(b_)
        s1l, Bs1 = [], []
        for i in range(2):
            t_, b_ = arena.alloc("s1st", [128, 512], F32); s1l.append(t_); Bs1.append(b_)
        rn4, Brn4 = arena.alloc("rn4", [1, 512], F32)
        r1 = st2[0:1, 0:512]; r2 = st2[0:1, 512:1024]

        for qc in range(4):
            for (W_s, dstsb, Bdst, act) in ((Wqs, q_sb, Bq, "relu"),
                                            (Wus, u_sb, Bu, "silu")):
                for np8 in range(8):
                    sl = np8 % 3
                    load_np(wp[sl], Bwp[sl], W_s, np8, 256)
                    for sub in range(2):
                        nt = np8 * 2 + sub
                        bank = nt % 2
                        em.mm_group(
                            pb[bank][0:128, 0:512],
                            [(wp[sl][:, dc * 256 + sub * 128:dc * 256 + sub * 128 + 128],
                              xn_ft[:, dc * T + qc * 512:dc * T + (qc + 1) * 512])
                             for dc in range(DC)],
                            r=(Bwp[sl], Bxn), w=(Bpb[bank],))
                        if act == "relu":
                            em.op("vector", "tensor_scalar",
                                  dstsb[:, nt * 512:(nt + 1) * 512],
                                  pb[bank][0:128, 0:512], 0.0, L1_RSCALE,
                                  ALU.max, ALU.mult,
                                  r=(Bpb[bank],), w=(Bdst,))
                        else:
                            em.op("scalar", "activation",
                                  dstsb[:, nt * 512:(nt + 1) * 512],
                                  pb[bank][0:128, 0:512], AF.Silu,
                                  r=(Bpb[bank],), w=(Bdst,))
            for h in range(L1_H + 2):
                if h < L1_H:
                    bank = h % 3
                    em.op("tensor", "matmul", pb[bank][0:128, 0:512],
                          kv_bf[:, h * 128:(h + 1) * 128],
                          q_sb[:, h * 512:(h + 1) * 512], start=True, stop=True,
                          r=(Bkvb, Bq), w=(Bpb[bank],))
                    em.op("scalar", "activation", sqs[h % 2][:, :],
                          pb[bank][0:128, 0:512],
                          AF.Square, r=(Bpb[bank],), w=(Bsq[h % 2],))
                    em.op("scalar", "copy", out_sb[:, h * 512:(h + 1) * 512],
                          pb[bank][0:128, 0:512], r=(Bpb[bank],), w=(Bo4,))
                if 1 <= h:
                    h1 = h - 1
                    nb = 4 + h1 % 2
                    r1h = st2[0:1, (h1 % 2) * 512:(h1 % 2) * 512 + 512]
                    em.op("tensor", "matmul", pb[nb][0:1, 0:512], ones128,
                          sqs[h1 % 2][:, :], start=True, stop=True,
                          r=(Bsq[h1 % 2], Bconst), w=(Bpb[nb],))
                    em.op("vector", "tensor_scalar", r1h, pb[nb][0:1, 0:512],
                          1.0 / L1_DH, 1e-24, ALU.mult, ALU.max,
                          r=(Bpb[nb],), w=(Bst2,))
                    em.op("scalar", "activation", r1h, r1h, AF.Ln,
                          r=(Bst2,), w=(Bst2,))
                    em.op("scalar", "activation", r1h, r1h, AF.Exp, scale=-0.5,
                          r=(Bst2,), w=(Bst2,))
                if 2 <= h:
                    h2 = h - 2
                    nb2 = 4 + h2 % 2
                    r1h2 = st2[0:1, (h2 % 2) * 512:(h2 % 2) * 512 + 512]
                    em.op("tensor", "matmul", pb[nb2][0:128, 0:512],
                          ones_row128, r1h2, start=True, stop=True,
                          r=(Bst2, Bconst), w=(Bpb[nb2],))
                    em.op("vector", "tensor_tensor",
                          z_sb[:, h2 * 512:(h2 + 1) * 512],
                          out_sb[:, h2 * 512:(h2 + 1) * 512],
                          u_sb[:, h2 * 512:(h2 + 1) * 512], ALU.mult,
                          r=(Bo4, Bu), w=(Bz,))
                    em.op("vector", "tensor_tensor",
                          z_sb[:, h2 * 512:(h2 + 1) * 512],
                          z_sb[:, h2 * 512:(h2 + 1) * 512],
                          pb[nb2][0:128, 0:512],
                          ALU.mult, r=(Bz, Bpb[nb2]), w=(Bz,))
            for np8 in range(4):
                sl = np8 % 3
                load_np(wp[sl], Bwp[sl], Wos, np8, 512)
                for ts in range(4):
                    bank = 6 if ts % 2 == 0 else 3
                    ssl = ts % 2
                    em.mm_group(
                        pb[bank][0:128, 0:512],
                        [(z_sb[:, dc * 512 + ts * 128:dc * 512 + ts * 128 + 128],
                          wp[sl][:, dc * 512:(dc + 1) * 512]) for dc in range(DC)],
                        r=(Bwp[sl], Bz), w=(Bpb[bank],))
                    em.op("scalar", "copy", s1l[ssl][:, :],
                          pb[bank][0:128, 0:512], r=(Bpb[bank],), w=(Bs1[ssl],))
                    em.dma("qsync",
                           s_scr[qc * 512 + ts * 128:qc * 512 + (ts + 1) * 128,
                                 np8 * 512:(np8 + 1) * 512],
                           s1l[ssl][:, :], r=(Bs1[ssl],), w=(Bs_scr,))
        arena.reset()

    def phase_p5():
        em.tag = "p5"
        arena.reset()
        H_sb, BH = arena.alloc("H_sb", [128, NHC * T], BF16)
        w13, Bw13 = [], []
        for i in range(4):
            t_, b_ = arena.alloc("w13", [128, DC * 128], BF16); w13.append(t_); Bw13.append(b_)
        w2sl, Bw2 = [], []
        for i in range(2):
            t_, b_ = arena.alloc("w2sl", [128, NHC * 512], BF16); w2sl.append(t_); Bw2.append(b_)
        B_sb, BB = arena.alloc("B_sb", [128, T], F32)
        NACC = 3
        acci, Bai, acco, Bao = [], [], [], []
        for i in range(NACC):
            t_, b_ = arena.alloc("acci", [128, 512], F32); acci.append(t_); Bai.append(b_)
            t_, b_ = arena.alloc("acco", [128, 512], F32); acco.append(t_); Bao.append(b_)
        a_sc, Ba = arena.alloc("a_sc", [128, 512], F32)
        sl_sc, Bsl = arena.alloc("sl_sc", [128, 512], F32)
        grows, Bgr = arena.alloc("grows", [8, T], F32)
        oh_sb, Boh = arena.alloc("oh_sb", [8, 8 * 128], F32)

        em.dma("qsync", oh_sb[:, :], oh_h[:, :], w=(Boh,))
        for tt in range(NT):
            em.op("tensor", "transpose", pb[0][0:8, 0:128],
                  gates[:, tt * 8:(tt + 1) * 8], idf[:, :],
                  r=(Bgates, Bconst), w=(Bpb[0],))
            em.op("vector", "tensor_copy", grows[0:8, tt * 128:(tt + 1) * 128],
                  pb[0][0:8, 0:128], r=(Bpb[0],), w=(Bgr,))

        experts = [("shared", NEXP)] + [(e, e) for e in range(NEXP)]
        for (e, ei) in experts:
            shared = (e == "shared")
            if not shared:
                for tc in range(4):
                    em.op("tensor", "matmul", pb[6][0:128, 0:512],
                          oh_sb[:, e * 128:(e + 1) * 128],
                          grows[0:8, tc * 512:(tc + 1) * 512],
                          start=True, stop=True, r=(Boh, Bgr), w=(Bpb[6],))
                    em.op("vector", "tensor_copy",
                          B_sb[:, tc * 512:(tc + 1) * 512],
                          pb[6][0:128, 0:512], r=(Bpb[6],), w=(BB,))
            for hc in range(NHC):
                s1_ = hc % 2
                s3_ = 2 + hc % 2
                em.dma("qgpsimd", w13[s1_][:, :], w1s_h[ei, hc],
                       w=(Bw13[s1_],))
                em.dma("qgpsimd", w13[s3_][:, :], w3s_h[ei, hc],
                       w=(Bw13[s3_],))
                for tc in range(4):
                    b1 = tc % 2
                    b3 = 2 + tc % 2
                    em.mm_group(
                        pb[b1][0:128, 0:512],
                        [(w13[s1_][:, dc * 128:(dc + 1) * 128],
                          xn_ft[:, dc * T + tc * 512:dc * T + (tc + 1) * 512])
                         for dc in range(DC)],
                        r=(Bw13[s1_], Bxn), w=(Bpb[b1],))
                    em.mm_group(
                        pb[b3][0:128, 0:512],
                        [(w13[s3_][:, dc * 128:(dc + 1) * 128],
                          xn_ft[:, dc * T + tc * 512:dc * T + (tc + 1) * 512])
                         for dc in range(DC)],
                        r=(Bw13[s3_], Bxn), w=(Bpb[b3],))
                    if shared:
                        em.op("scalar", "activation", sl_sc[:, :],
                              pb[b1][0:128, 0:512], AF.Silu,
                              r=(Bpb[b1],), w=(Bsl,))
                    else:
                        em.op("vector", "tensor_tensor", a_sc[:, :],
                              pb[b1][0:128, 0:512],
                              B_sb[:, tc * 512:(tc + 1) * 512], ALU.mult,
                              r=(Bpb[b1], BB), w=(Ba,))
                        em.op("scalar", "activation", sl_sc[:, :], a_sc[:, :],
                              AF.Silu, r=(Ba,), w=(Bsl,))
                    em.op("vector", "tensor_tensor",
                          H_sb[:, hc * T + tc * 512:hc * T + (tc + 1) * 512],
                          sl_sc[:, :], pb[b3][0:128, 0:512], ALU.mult,
                          r=(Bsl, Bpb[b3]), w=(BH,))
            w2banks = (3, 4, 5, 6)
            iters = [(nc4, ts) for nc4 in range(4) for ts in range(NT)]
            last = (not shared) and e == NEXP - 1
            if last:
                s_ch = [B_sb[:, 0:512], B_sb[:, 512:1024]]
                x1_ch = [B_sb[:, 1024:1536], B_sb[:, 1536:2048]]
                Bs_ch = [em.buf(f"lsch{i}", fence=[BB]) for i in range(2)]
                Bx1_ch = [em.buf(f"lx1ch{i}", fence=[BB]) for i in range(2)]
                fgt = [a_sc, sl_sc]
                Bfgt = [em.buf("lfg0", fence=[Ba]),
                        em.buf("lfg1", fence=[Bsl])]

                def sx_load(i):
                    if i >= 64:
                        return
                    nc4i, tsi = iters[i]
                    s2 = i % 2
                    em.dma("qsync", s_ch[s2],
                           s_scr[tsi * 128:(tsi + 1) * 128,
                                 nc4i * 512:(nc4i + 1) * 512],
                           r=(Bs_scr,), w=(Bs_ch[s2],))
                    em.dma("qsync", x1_ch[s2],
                           x1_scr[tsi * 128:(tsi + 1) * 128,
                                  nc4i * 512:(nc4i + 1) * 512],
                           r=(Bx1_scr,), w=(Bx1_ch[s2],))
                sx_load(0)
                sx_load(1)

            def acci_load(i):
                if shared or i >= 64:
                    return
                nc4i, tsi = iters[i]
                sli = i % NACC
                em.dma("qsync", acci[sli][:, :],
                       acc_scr[tsi * 128:(tsi + 1) * 128,
                               nc4i * 512:(nc4i + 1) * 512],
                       r=(Bacc_scr[nc4i * 16 + tsi],), w=(Bai[sli],))

            if not shared:
                for j in range(NACC):
                    acci_load(j)
            for i, (nc4, ts) in enumerate(iters):
                if i % 16 == 0:
                    sw = nc4 % 2
                    em.dma("qgpsimd", w2sl[sw][:, :], w2s_h[ei, nc4],
                           w=(Bw2[sw],))
                sl = i % NACC
                bank = w2banks[i % 4]
                swc = nc4 % 2
                em.mm_group(
                    pb[bank][0:128, 0:512],
                    [(H_sb[:, hc * T + ts * 128:hc * T + ts * 128 + 128],
                      w2sl[swc][:, hc * 512:(hc + 1) * 512])
                     for hc in range(NHC)],
                    r=(BH, Bw2[swc]), w=(Bpb[bank],))
                if shared:
                    em.op("scalar", "copy", acco[sl][:, :],
                          pb[bank][0:128, 0:512], r=(Bpb[bank],),
                          w=(Bao[sl],))
                else:
                    em.op("vector", "scalar_tensor_tensor", acco[sl][:, :],
                          pb[bank][0:128, 0:512],
                          gates[:, ts * 8 + e: ts * 8 + e + 1],
                          acci[sl][:, :], ALU.mult, ALU.add,
                          r=(Bpb[bank], Bgates, Bai[sl]), w=(Bao[sl],))
                if last:
                    s2 = i % 2
                    em.op("scalar", "activation", fgt[s2][:, :],
                          acco[sl][:, :], AF.Silu,
                          r=(Bao[sl],), w=(Bfgt[s2],))
                    em.op("vector", "tensor_tensor", fgt[s2][:, :],
                          fgt[s2][:, :], s_ch[s2], ALU.mult,
                          r=(Bfgt[s2], Bs_ch[s2]), w=(Bfgt[s2],))
                    em.op("vector", "tensor_tensor", x1_ch[s2], x1_ch[s2],
                          fgt[s2][:, :], ALU.add,
                          r=(Bx1_ch[s2], Bfgt[s2]), w=(Bx1_ch[s2],))
                    em.dma("qsync",
                           y_h[ts * 128:(ts + 1) * 128,
                               nc4 * 512:(nc4 + 1) * 512],
                           x1_ch[s2], r=(Bx1_ch[s2],))
                    sx_load(i + 2)
                else:
                    em.dma("qsync",
                           acc_scr[ts * 128:(ts + 1) * 128,
                                   nc4 * 512:(nc4 + 1) * 512],
                           acco[sl][:, :], r=(Bao[sl],),
                           w=(Bacc_scr[nc4 * 16 + ts],))
                acci_load(i + NACC)


    if "ln0" in phases:
        phase_ln(x_h, None, layer=0,
                 tap="xnd0" if "xnd0" in debug_taps else None)
    if "p1" in phases:
        import os as _os
        phase_p1(sub=int(_os.environ.get("P1SUB", "5")))
    if "p2" in phases:
        phase_p2()
    if "ln1" in phases:
        phase_ln(x1_scr, Bx1_scr, layer=1,
                 tap="xnd1" if "xnd1" in debug_taps else None)
    if "p4" in phases:
        phase_p4()
    if "p5" in phases:
        phase_p5()
    if "p5" not in phases:
        Bdummy = em.buf("ydummy")
        for tt in range(NT):
            pass

    from contextlib import ExitStack
    es = ExitStack()
    sems = {}
    for s in list(ENGS) + list(QSEMS):
        sems[s] = es.enter_context(nc.semaphore(f"sem_{s}"))
    em.finalize(nc, sems)
    es.close()
    return nc


def _host_consts():
    cst = np.zeros((128, 160), np.float32)
    cst[:, 0] = -20.0
    cst[:, 1] = 1.0
    cst[:, 8:136] = 1.0
    idb = np.eye(128, dtype=ml_dtypes.bfloat16)
    idf = np.eye(128, dtype=np.float32)
    oh = np.zeros((8, 8 * 128), np.float32)
    for e in range(8):
        oh[e, e * 128:(e + 1) * 128] = 1.0
    mask01 = np.zeros((128, 4 * 512), ml_dtypes.bfloat16)
    kk = np.arange(128)[:, None]
    q = np.arange(512)[None, :]
    for i in range(4):
        mask01[:, i * 512:(i + 1) * 512] = ((128 * i + kk) <= q).astype(
            ml_dtypes.bfloat16)
    return cst, idb, idf, oh, mask01


def _stage_panels(mT, ncols):
    Dn, X = mT.shape
    npan = X // ncols
    return np.ascontiguousarray(
        mT.reshape(16, 128, npan, ncols).transpose(2, 1, 0, 3)
          .reshape(npan, 128, 16 * ncols))


def _stage_w13(w):
    return np.ascontiguousarray(
        w.reshape(16, 128, 14, 128).transpose(2, 1, 0, 3).reshape(14, 128, 2048))


def _stage_w2(w):
    return np.ascontiguousarray(
        w.reshape(14, 128, 4, 512).transpose(2, 1, 0, 3).reshape(4, 128, 14 * 512))


def host_inputs(inputs):
    f32 = lambda a: np.ascontiguousarray(np.asarray(a), dtype=np.float32)
    x = f32(inputs["L0_attn_x"])
    scl = f32(inputs["L0_ffn_scale"])
    biasE = f32(inputs["L1_ffn_expert_bias"])
    cst, idb, idf, oh, mask01 = _host_consts()
    w1 = f32(inputs["L1_ffn_w1"]); w2 = f32(inputs["L1_ffn_w2"])
    w3 = f32(inputs["L1_ffn_w3"])
    shared = {
        "wqTs": _stage_panels(f32(inputs["L0_attn_wq"]).T, 128),
        "wkTs": _stage_panels(f32(inputs["L0_attn_wk"]).T, 128),
        "wvTs": _stage_panels(f32(inputs["L0_attn_wv"]).T, 128),
        "woTs": _stage_panels(f32(inputs["L0_attn_wo"]).T, 128),
        "w0": f32(inputs["L0_ffn_weight"]),
        "sclbc": np.ascontiguousarray(np.broadcast_to(scl[None, :], (128, D))),
        "Wqs": _stage_panels(f32(inputs["L1_attn_Wq"]), 256),
        "Wks": _stage_panels(f32(inputs["L1_attn_Wk"]), 256),
        "Wvs": _stage_panels(f32(inputs["L1_attn_Wv"]), 256),
        "Wus": _stage_panels(f32(inputs["L1_attn_Wu"]), 256),
        "Wos": _stage_panels(f32(inputs["L1_attn_Wo"]), 512),
        "gw": f32(inputs["L1_ffn_gate_w"]),
        "w1s": np.stack([_stage_w13(w1[e]) for e in range(NEXP)]
                        + [_stage_w13(f32(inputs["L1_ffn_sw1"]))]),
        "w3s": np.stack([_stage_w13(w3[e]) for e in range(NEXP)]
                        + [_stage_w13(f32(inputs["L1_ffn_sw3"]))]),
        "w2s": np.stack([_stage_w2(w2[e]) for e in range(NEXP)]
                        + [_stage_w2(f32(inputs["L1_ffn_sw2"]))]),
        "biasE": np.ascontiguousarray(np.broadcast_to(biasE[None, :], (128, 8))),
        "cst": cst, "idb": idb, "idf": idf, "oh": oh, "mask01": mask01,
    }
    return x, shared


_NC_CACHE = {}


def kernel(**inputs):
    x, shared = host_inputs(inputs)
    if "nc" not in _NC_CACHE:
        _NC_CACHE["nc"] = build_program()
    nc = _NC_CACHE["nc"]
    in_maps = [dict(shared, x=np.ascontiguousarray(x[c])) for c in range(8)]
    res = run_bass_kernel_spmd(nc, in_maps, core_ids=list(range(8)))
    return np.stack([res.results[c]["y"] for c in range(8)], axis=0)


# version 09
# speedup vs baseline: 1.0012x; 1.0012x over previous
import math
import numpy as np
import ml_dtypes
import concourse.bass as bass
import concourse.mybir as mybir
from concourse.bass_utils import run_bass_kernel_spmd

F32 = mybir.dt.float32
BF16 = mybir.dt.bfloat16
AF = mybir.ActivationFunctionType
ALU = mybir.AluOpType
AX = mybir.AxisListType

T = 2048
D = 2048
NT = 16
DC = 16
HID = 1792
NHC = 14
NEXP = 8
L0_HQ, L0_HKV, L0_DH = 32, 8, 64
L0_SCALE = 0.125
L1_H, L1_DH = 16, 128
L1_RSCALE = float(np.float32(1.0) / np.float32(math.sqrt(128.0)))

QUEUES = {"qsync": "sync", "qgpsimd": "gpsimd"}
NLANES = 6
QSEMS = {f"{q}{l}": e for q, e in QUEUES.items() for l in range(NLANES)}
ENGS = ("tensor", "vector", "scalar", "gpsimd", "sync")

SBUF_BASE = 17408
SBUF_TOP = 229120


class Buf:
    __slots__ = ("name", "writer", "readers", "fence")

    def __init__(self, name, fence=None):
        self.name = name
        self.writer = None
        self.readers = []
        self.fence = fence


class _Op:
    __slots__ = ("eng", "sem", "fn", "args", "kwargs", "inc", "idx", "val",
                 "waits", "is_dma", "tag")

    def __init__(self, eng, sem, fn, args, kwargs, is_dma, tag):
        self.eng, self.sem, self.fn = eng, sem, fn
        self.args, self.kwargs = args, kwargs
        self.inc = is_dma
        self.idx = -1
        self.val = 0
        self.waits = []
        self.is_dma = is_dma
        self.tag = tag


class Emit:
    def __init__(self):
        self.ops = {e: [] for e in ENGS}
        self.counts = {s: 0 for s in list(ENGS) + list(QSEMS)}
        self.qcount = {q: 0 for q in QUEUES}
        self.last_lane_dma = {s: None for s in QSEMS}
        self.clock = {}
        self.tag = ""

    def buf(self, name, fence=None):
        return Buf(name, fence)

    def _edge(self, op, prod, is_read=False):
        if prod is None or prod is op:
            return
        if (not prod.is_dma) and prod.sem == op.eng:
            if (not is_read) or op.eng == "tensor":
                return
        key = (op.eng, prod.sem)
        if self.clock.get(key, -1) >= prod.idx:
            return
        self.clock[key] = prod.idx
        prod.inc = True
        op.waits.append(prod)

    def _record(self, eng, sem, fn, args, kwargs, reads, writes, is_dma):
        op = _Op(eng, sem, fn, args, kwargs, is_dma, self.tag)
        op.idx = self.counts[sem]
        self.counts[sem] = op.idx + 1
        for b in writes:
            if b.fence is not None:
                for ob in b.fence:
                    self._edge(op, ob.writer)
                    for r in ob.readers:
                        self._edge(op, r)
                b.fence = None
        for b in reads:
            self._edge(op, b.writer, is_read=True)
        for b in writes:
            self._edge(op, b.writer)
            for r in b.readers:
                self._edge(op, r)
        for b in reads:
            b.readers.append(op)
        for b in writes:
            b.writer = op
            b.readers = []
        self.ops[eng].append(op)
        return op

    def op(self, eng, fn, *args, r=(), w=(), **kwargs):
        return self._record(eng, eng, fn, args, kwargs, r, w, False)

    def dma(self, queue, out, in_, r=(), w=(), **kwargs):
        lane = self.qcount[queue] % NLANES
        self.qcount[queue] += 1
        sem = f"{queue}{lane}"
        eng = QUEUES[queue]
        op = self._record(eng, sem, "dma_start",
                          (), dict(out=out, in_=in_, **kwargs), r, w, True)
        prev = self.last_lane_dma[sem]
        if prev is not None and prev not in op.waits:
            key = (eng, sem)
            if self.clock.get(key, -1) < prev.idx:
                self.clock[key] = prev.idx
                op.waits.append(prev)
        self.last_lane_dma[sem] = op
        return op

    def mm_group(self, out_ap, pairs, r=(), w=()):
        n = len(pairs)
        ops = []
        for i, (lt, rh) in enumerate(pairs):
            rr = r if i == 0 else ()
            ww = w if (i == 0 or i == n - 1) else ()
            ops.append(self._record("tensor", "tensor", "matmul",
                                    (out_ap, lt, rh),
                                    dict(start=(i == 0), stop=(i == n - 1)),
                                    rr, ww, False))
        for b in r:
            for i in range(n - 1):
                if ops[i] in b.readers:
                    b.readers.remove(ops[i])
            if ops[-1] not in b.readers:
                b.readers.append(ops[-1])
        return ops[-1]

    def finalize(self, nc, sems):
        cums = {s: 0 for s in self.counts}
        for e in ENGS:
            for op in self.ops[e]:
                s = op.sem
                if op.is_dma:
                    cums[s] += 16
                    op.val = cums[s]
                else:
                    if op.inc:
                        cums[s] += 1
                    op.val = cums[s]

        def emit_stream(engobj, ename):
            for op in self.ops[ename]:
                for prod in op.waits:
                    engobj.wait_ge(sems[prod.sem], prod.val)
                inst = getattr(engobj, op.fn)(*op.args, **op.kwargs)
                if op.inc:
                    inst.then_inc(sems[op.sem], 16 if op.is_dma else 1)
            if ename == "sync":
                for q in QSEMS:
                    if cums[q] > 0:
                        engobj.wait_ge(sems[q], cums[q])

        with nc.Block() as block:
            block.tensor(lambda t: emit_stream(t, "tensor"))
            block.vector(lambda v: emit_stream(v, "vector"))
            block.scalar(lambda s: emit_stream(s, "scalar"))
            block.gpsimd(lambda g: emit_stream(g, "gpsimd"))
            block.sync(lambda s: emit_stream(s, "sync"))


class Arena:
    def __init__(self, nc, em, base, limit):
        self.nc, self.em = nc, em
        self.base, self.limit = base, limit
        self.off = base
        self.n = 0
        self.cur = []
        self.pending = []

    def alloc(self, name, shape, dtype):
        nbytes = shape[1] * (2 if dtype == BF16 else 4)
        nbytes = (nbytes + 31) & ~31
        assert self.off + nbytes <= self.limit, \
            f"SBUF arena overflow at {name}: {self.off}+{nbytes}>{self.limit}"
        t = self.nc.alloc_sbuf_tensor_at(f"{name}_{self.n}", shape, dtype,
                                         offset=self.off)
        self.n += 1
        self.off += nbytes
        b = self.em.buf(name, fence=self.pending if self.pending else None)
        self.cur.append(b)
        return t, b

    def reset(self, to=None):
        self.off = self.base if to is None else to
        self.pending = self.pending + self.cur
        self.cur = []


def build_program(debug_taps=(), phases=("ln0", "p1", "p2", "ln1", "p4", "p5")):
    nc = bass.Bass("TRN2", target_bir_lowering=False, debug=False)
    em = Emit()

    dt = lambda n, s, k="ExternalInput", d=F32: nc.dram_tensor(n, s, d, kind=k).ap()
    x_h = dt("x", [T, D])
    wqTs = dt("wqTs", [16, 128, DC * 128])
    wkTs = dt("wkTs", [4, 128, DC * 128])
    wvTs = dt("wvTs", [4, 128, DC * 128])
    woTs = dt("woTs", [16, 128, DC * 128])
    w0_h = dt("w0", [D, D])
    sclbc = dt("sclbc", [128, D])
    Wqs = dt("Wqs", [8, 128, DC * 256]); Wks = dt("Wks", [8, 128, DC * 256])
    Wvs = dt("Wvs", [8, 128, DC * 256]); Wus = dt("Wus", [8, 128, DC * 256])
    Wos = dt("Wos", [4, 128, DC * 512])
    gw_h = dt("gw", [D, 8])
    w1s_h = dt("w1s", [NEXP + 1, NHC, 128, DC * 128])
    w3s_h = dt("w3s", [NEXP + 1, NHC, 128, DC * 128])
    w2s_h = dt("w2s", [NEXP + 1, 4, 128, NHC * 512])
    biasE_h = dt("biasE", [128, 8])
    cst_h = dt("cst", [128, 160])
    idb_h = dt("idb", [128, 128], d=BF16)
    idf_h = dt("idf", [128, 128])
    oh_h = dt("oh", [8, 8 * 128])
    mask01_h = dt("mask01", [128, 4 * 512], d=BF16)
    y_h = dt("y", [T, D], k="ExternalOutput")
    kind_scr = "ExternalOutput" if "scr" in debug_taps else "Internal"
    s_scr = dt("s_scr", [T, D], k=kind_scr)
    x1_scr = dt("x1_scr", [T, D], k=kind_scr)
    acc_scr = dt("acc_scr", [T, D], k=kind_scr)
    xnd = {}
    for tap in ("xnd0", "xnd1"):
        if tap in debug_taps:
            xnd[tap] = dt(tap, [128, DC * T], k="ExternalOutput", d=BF16)

    off = SBUF_BASE

    def fixed(name, shape, dtyp):
        nonlocal off
        nbytes = shape[1] * (2 if dtyp == BF16 else 4)
        nbytes = (nbytes + 31) & ~31
        t = nc.alloc_sbuf_tensor_at(name, shape, dtyp, offset=off)
        off += nbytes
        return t

    cst = fixed("cstt", [128, 160], F32)
    idb = fixed("idbt", [128, 128], BF16)
    idf = fixed("idft", [128, 128], F32)
    biasE = fixed("biasEt", [128, 8], F32)
    gw_sb = fixed("gw_sbt", [128, DC * 8], BF16)
    gates = fixed("gatest", [128, NT * 8], F32)
    st = fixed("statst", [128, 48], F32)
    st2 = fixed("stats2t", [1, 1024], F32)
    xn_ft = fixed("xn_ftt", [128, DC * T], BF16)

    arena = Arena(nc, em, (off + 31) & ~31, SBUF_TOP)

    c_neg20 = cst[:, 0:1]
    ones128 = cst[:, 1:2]
    ones_row64 = cst[0:1, 8:72]
    ones_row128 = cst[0:1, 8:136]

    pb = [nc.alloc_psum_tensor(f"pb{i}", [128, 512], F32) for i in range(7)]
    pbT = nc.alloc_psum_tensor("pbT", [128, 1024], BF16)
    Bpb = [em.buf(f"pb{i}") for i in range(7)]
    BpbT = em.buf("pbT")

    Bconst = em.buf("const")
    Bxn = em.buf("xn_ft")
    Bgates = em.buf("gates")
    Bst = em.buf("stats")
    Bst2 = em.buf("stats2")
    Bs_scr = em.buf("s_scr")
    Bx1_scr = em.buf("x1_scr")
    Bacc_scr = [em.buf(f"acc_scr{i}") for i in range(64)]

    em.tag = "init"
    em.dma("qsync", cst[:, :], cst_h[:, :], w=(Bconst,))
    em.dma("qsync", idb[:, :], idb_h[:, :], w=(Bconst,))
    em.dma("qsync", idf[:, :], idf_h[:, :], w=(Bconst,))
    em.dma("qsync", biasE[:, :], biasE_h[:, :], w=(Bconst,))
    em.dma("qgpsimd", gw_sb[:, :].rearrange("p (c e) -> p c e", c=DC),
           gw_h[:, :].rearrange("(c p) e -> p c e", p=128), w=(Bconst,))

    def gate_tile(tt, zpb, Bzpb, gz, Bgz, gst, Bst):
        sc, e_, tmp, p_, q_ = (gz[i][:, 0:8] for i in range(5))
        Bsc, Be, Btmp, Bp, Bq = Bgz
        zb = gst[:, 0:8]
        mx = gst[:, 8:9]; nm = gst[:, 9:10]; ssum = gst[:, 10:11]
        rs = gst[:, 11:12]; m_ = gst[:, 12:13]
        m1_ap = gst[:, 16:24]; m2_ap = gst[:, 24:32]
        em.op("vector", "reduce_max", mx, zpb[0:128, 0:8], axis=AX.X,
              r=(Bzpb,), w=(Bst,))
        em.op("vector", "tensor_scalar", nm, mx, -1.0, None, ALU.mult,
              r=(Bst,), w=(Bst,))
        em.op("scalar", "activation", e_, zpb[0:128, 0:8], AF.Exp,
              bias=nm, r=(Bzpb, Bst), w=(Be,))
        em.op("vector", "reduce_sum", ssum, e_, axis=AX.X, r=(Be,), w=(Bst,))
        em.op("vector", "reciprocal", rs, ssum, r=(Bst,), w=(Bst,))
        em.op("vector", "tensor_scalar", sc, e_, rs, None, ALU.mult,
              r=(Be, Bst), w=(Bsc,))
        em.op("vector", "tensor_tensor", zb, sc, biasE[:, :], ALU.add,
              r=(Bsc, Bconst, Bst), w=(Bst,))

        def top1(dst_ap):
            em.op("vector", "reduce_max", m_, zb, axis=AX.X, r=(Bst,), w=(Bst,))
            em.op("vector", "tensor_scalar", tmp, zb, m_, None, ALU.is_ge,
                  r=(Bst,), w=(Btmp,))
            em.op("vector", "tensor_copy", e_[:, 0:1], tmp[:, 0:1],
                  r=(Btmp,), w=(Be,))
            em.op("vector", "tensor_tensor", e_[:, 1:8], tmp[:, 1:8],
                  tmp[:, 0:7], ALU.add, r=(Btmp,), w=(Be,))
            em.op("vector", "tensor_copy", p_[:, 0:2], e_[:, 0:2],
                  r=(Be,), w=(Bp,))
            em.op("vector", "tensor_tensor", p_[:, 2:8], e_[:, 2:8],
                  e_[:, 0:6], ALU.add, r=(Be,), w=(Bp,))
            em.op("vector", "tensor_copy", q_[:, 0:4], p_[:, 0:4],
                  r=(Bp,), w=(Bq,))
            em.op("vector", "tensor_tensor", q_[:, 4:8], p_[:, 4:8],
                  p_[:, 0:4], ALU.add, r=(Bp,), w=(Bq,))
            em.op("vector", "tensor_tensor", q_, q_, tmp, ALU.subtract,
                  r=(Bq, Btmp), w=(Bq,))
            em.op("vector", "tensor_scalar", q_, q_, 0.0, None, ALU.is_equal,
                  r=(Bq,), w=(Bq,))
            em.op("vector", "tensor_tensor", dst_ap, tmp, q_, ALU.mult,
                  r=(Btmp, Bq), w=(Bst,))

        top1(m1_ap)
        em.op("vector", "scalar_tensor_tensor", zb, m1_ap, -1e30, zb,
              ALU.mult, ALU.add, r=(Bst,), w=(Bst,))
        top1(m2_ap)
        em.op("vector", "tensor_tensor", m1_ap, m1_ap, m2_ap, ALU.add,
              r=(Bst,), w=(Bst,))
        em.op("vector", "tensor_tensor", gates[:, tt * 8:(tt + 1) * 8],
              sc, m1_ap, ALU.mult, r=(Bsc, Bst), w=(Bgates,))

    def phase_ln(src_h, Bsrc, layer, tap=None):
        em.tag = f"ln{layer}"
        arena.reset()
        xt, Bxt, xnt, Bxnt, sq, Bsq = [], [], [], [], [], []
        for i in range(3):
            t_, b_ = arena.alloc("ln_x", [128, D], F32); xt.append(t_); Bxt.append(b_)
        for i in range(2):
            t_, b_ = arena.alloc("ln_xn", [128, D], BF16); xnt.append(t_); Bxnt.append(b_)
            t_, b_ = arena.alloc("ln_sq", [128, D], F32); sq.append(t_); Bsq.append(b_)
        lst, Blst = [], []
        for par in range(2):
            t_, b_ = arena.alloc("ln_st", [128, 12], F32); lst.append(t_); Blst.append(b_)

        for tt in range(NT):
            sl = tt % 2
            sl3 = tt % 3
            Bst = Blst[sl]
            s_sum = lst[sl][:, 0:1]; s_ss = lst[sl][:, 1:2]
            s_m = lst[sl][:, 2:3]; s_var = lst[sl][:, 3:4]
            s_r0 = lst[sl][:, 4:5]; s_e = lst[sl][:, 5:6]
            s_e2 = lst[sl][:, 6:7]; s_h = lst[sl][:, 7:8]
            s_rstd = lst[sl][:, 8:9]
            rd = (Bsrc,) if Bsrc else ()
            em.dma("qsync", xt[sl3][:, :], src_h[tt * 128:(tt + 1) * 128, :],
                   r=rd, w=(Bxt[sl3],))
            em.op("vector", "reduce_sum", s_sum, xt[sl3][:, :], axis=AX.X,
                  r=(Bxt[sl3],), w=(Bst,))
            em.op("scalar", "activation", sq[sl][:, :], xt[sl3][:, :],
                  AF.Square, r=(Bxt[sl3],), w=(Bsq[sl],))
            em.op("vector", "reduce_sum", s_ss, sq[sl][:, :], axis=AX.X,
                  r=(Bsq[sl],), w=(Bst,))
            v = lambda fn, *a, **k: em.op("vector", fn, *a, r=(Bst,), w=(Bst,), **k)
            v("tensor_scalar", s_m, s_sum, 1.0 / D, None, ALU.mult)
            v("tensor_scalar", s_var, s_ss, 1.0 / D, None, ALU.mult)
            v("tensor_tensor", s_e, s_m, s_m, ALU.mult)
            v("tensor_tensor", s_var, s_var, s_e, ALU.subtract)
            v("tensor_scalar", s_var, s_var, 1e-5, None, ALU.add)
            em.op("scalar", "activation", s_e, s_var, AF.Ln, r=(Bst,), w=(Bst,))
            em.op("scalar", "activation", s_r0, s_e, AF.Exp, scale=-0.5,
                  r=(Bst,), w=(Bst,))
            v("tensor_tensor", s_e, s_var, s_r0, ALU.mult)
            v("tensor_tensor", s_e2, s_e, s_r0, ALU.mult)
            v("tensor_scalar", s_h, s_e2, -0.5, 1.5, ALU.mult, ALU.add)
            v("tensor_tensor", s_rstd, s_r0, s_h, ALU.mult)
            em.op("vector", "tensor_scalar", xnt[sl][:, :], xt[sl3][:, :],
                  s_m, s_rstd, ALU.subtract, ALU.mult,
                  r=(Bxt[sl3], Bst), w=(Bxnt[sl],))
            for g8 in range(2):
                for j in range(8):
                    dc = g8 * 8 + j
                    em.op("tensor", "transpose", pbT[:, j * 128:(j + 1) * 128],
                          xnt[sl][:, dc * 128:(dc + 1) * 128], idb[:, :],
                          r=(Bxnt[sl], Bconst), w=(BpbT,))
                dst = xn_ft[:, :].rearrange("p (c n) -> p c n", c=DC)[
                    :, g8 * 8:(g8 + 1) * 8, tt * 128:(tt + 1) * 128]
                src = pbT[:, :].rearrange("p (c n) -> p c n", c=8)
                em.op("vector", "tensor_copy", dst, src, r=(BpbT,), w=(Bxn,))

        if tap is not None:
            em.dma("qsync", xnd[tap][:, :], xn_ft[:, :], r=(Bxn,))
        arena.reset()

    def phase_p1(sub=4):
        em.tag = "p1"
        arena.reset()
        k64, Bk64 = arena.alloc("k64", [64, L0_HKV * T], BF16)
        q64, Bq64 = arena.alloc("q64", [64, L0_HQ * 512], BF16)
        v_aug, Bva = arena.alloc("v_aug", [128, NT * 520], BF16)
        attn_ft, Baf = arena.alloc("attn_ft", [128, DC * 512], BF16)
        stg, Bstg, wsl, Bwsl, wosl, Bwo, ptl, Bpt = [], [], [], [], [], [], [], []
        for i in range(2):
            t_, b_ = arena.alloc("attn_stg", [64, 512], BF16); stg.append(t_); Bstg.append(b_)
            t_, b_ = arena.alloc("wsl", [128, DC * 128], BF16); wsl.append(t_); Bwsl.append(b_)
            t_, b_ = arena.alloc("wosl", [128, DC * 128], BF16); wosl.append(t_); Bwo.append(b_)
        for i in range(4):
            t_, b_ = arena.alloc("pt", [128, 512], BF16); ptl.append(t_); Bpt.append(b_)
        mask01, Bm01 = arena.alloc("mask01", [128, 4 * 512], BF16)
        s0st, Bs0l = [], []
        for i in range(4):
            t_, b_ = arena.alloc("s0st", [128, 256], F32); s0st.append(t_); Bs0l.append(b_)
        atmp, Batmp, rnrow, Brn = [], [], [], []
        for i in range(2):
            t_, b_ = arena.alloc("atmp", [64, 512], F32)
            atmp.append(t_); Batmp.append(b_)
            t_, b_ = arena.alloc("rnrow", [1, 512], F32)
            rnrow.append(t_); Brn.append(b_)
        scj = [0]

        em.dma("qsync", mask01[:, :], mask01_h[:, :], w=(Bm01,))
        em.op("vector", "memset",
              v_aug[:, :].rearrange("p (t g c) -> p t g c", t=NT, g=8)[:, :, :, 64:65],
              1.0, w=(Bva,))

        def load_wpanel(slot, Bslot, staged_h, pnl, ncols):
            em.dma("qgpsimd", slot[:, 0:DC * ncols], staged_h[pnl],
                   w=(Bslot,))

        bankctr = [0]

        def pingbank():
            bankctr[0] = (bankctr[0] + 1) % 3
            return 4 + bankctr[0]

        for pnl in range(4):
            sl = pnl % 2
            load_wpanel(wsl[sl], Bwsl[sl], wkTs, pnl, 128)
            for gg in range(2):
                g = pnl * 2 + gg
                for tc in range(4):
                    bk = pingbank()
                    em.mm_group(
                        pb[bk][0:64, 0:512],
                        [(wsl[sl][:, dc * 128 + gg * 64: dc * 128 + gg * 64 + 64],
                          xn_ft[:, dc * T + tc * 512:dc * T + (tc + 1) * 512])
                         for dc in range(DC)],
                        r=(Bwsl[sl], Bxn), w=(Bpb[bk],))
                    em.op("scalar", "copy",
                          k64[:, g * T + tc * 512:g * T + (tc + 1) * 512],
                          pb[bk][0:64, 0:512], r=(Bpb[bk],), w=(Bk64,))
        for pnl in range(4):
            sl = pnl % 2
            load_wpanel(wsl[sl], Bwsl[sl], wvTs, pnl, 128)
            for tt in range(NT):
                bk = pingbank()
                em.mm_group(
                    pb[bk][0:128, 0:128],
                    [(xn_ft[:, dc * T + tt * 128:dc * T + (tt + 1) * 128],
                      wsl[sl][:, dc * 128:(dc + 1) * 128]) for dc in range(DC)],
                    r=(Bwsl[sl], Bxn), w=(Bpb[bk],))
                for gg in range(2):
                    g = pnl * 2 + gg
                    em.op("scalar", "copy",
                          v_aug[:, tt * 520 + g * 65: tt * 520 + g * 65 + 64],
                          pb[bk][0:128, gg * 64:gg * 64 + 64],
                          r=(Bpb[bk],), w=(Bva,))

        for qc in range(4 if sub >= 2 else 0):
            for pnl in range(16):
                sl = pnl % 2
                load_wpanel(wsl[sl], Bwsl[sl], wqTs, pnl, 128)
                for gg in range(2):
                    h = pnl * 2 + gg
                    bk = pingbank()
                    em.mm_group(
                        pb[bk][0:64, 0:512],
                        [(wsl[sl][:, dc * 128 + gg * 64: dc * 128 + gg * 64 + 64],
                          xn_ft[:, dc * T + qc * 512:dc * T + (qc + 1) * 512])
                         for dc in range(DC)],
                        r=(Bwsl[sl], Bxn), w=(Bpb[bk],))
                    em.op("vector", "tensor_scalar",
                          q64[:, h * 512:(h + 1) * 512],
                          pb[bk][0:64, 0:512], L0_SCALE, None, ALU.mult,
                          r=(Bpb[bk],), w=(Bq64,))
            nkt = 4 * qc + 4
            for hp in range(L0_HQ // 2 if sub >= 3 else 0):
                hA, hB = 2 * hp, 2 * hp + 1
                g = hA // 4
                pvA, pvB = 5, 6

                def score(h, kt, ptb, Bptb):
                    bk = scj[0] % 4
                    scj[0] += 1
                    em.op("tensor", "matmul", pb[bk][0:128, 0:512],
                          k64[:, g * T + kt * 128:g * T + (kt + 1) * 128],
                          q64[:, h * 512:(h + 1) * 512],
                          start=True, stop=True,
                          r=(Bk64, Bq64), w=(Bpb[bk],))
                    em.op("scalar", "activation", ptb[:, :],
                          pb[bk][0:128, 0:512], AF.Exp, bias=c_neg20,
                          r=(Bpb[bk], Bconst), w=(Bptb,))
                    if kt >= 4 * qc:
                        i = kt - 4 * qc
                        em.op("vector", "tensor_tensor", ptb[:, :],
                              ptb[:, :], mask01[:, i * 512:(i + 1) * 512],
                              ALU.mult, r=(Bptb, Bm01), w=(Bptb,))

                def pv(kt, pvb, ptb, Bptb):
                    em.op("tensor", "matmul", pb[pvb][0:65, 0:512],
                          v_aug[:, kt * 520 + g * 65: kt * 520 + (g + 1) * 65],
                          ptb[:, :], start=(kt == 0), stop=(kt == nkt - 1),
                          r=(Bva, Bptb), w=(Bpb[pvb],))

                for kt in range(nkt):
                    score(hA, kt, ptl[kt % 2], Bpt[kt % 2])
                    score(hB, kt, ptl[2 + kt % 2], Bpt[2 + kt % 2])
                    if kt > 0:
                        pv(kt - 1, pvA, ptl[(kt - 1) % 2], Bpt[(kt - 1) % 2])
                        pv(kt - 1, pvB, ptl[2 + (kt - 1) % 2],
                           Bpt[2 + (kt - 1) % 2])
                pv(nkt - 1, pvA, ptl[(nkt - 1) % 2], Bpt[(nkt - 1) % 2])
                pv(nkt - 1, pvB, ptl[2 + (nkt - 1) % 2],
                   Bpt[2 + (nkt - 1) % 2])

                for (h, pvb, rn_, Brn_, at_, Bat_) in (
                        (hA, pvA, rnrow[0], Brn[0], atmp[0], Batmp[0]),
                        (hB, pvB, rnrow[1], Brn[1], atmp[1], Batmp[1])):
                    em.op("vector", "reciprocal", rn_[0:1, :],
                          pb[pvb][64:65, 0:512], r=(Bpb[pvb],), w=(Brn_,))
                    em.op("tensor", "matmul", pb[4][0:64, 0:512], ones_row64,
                          rn_[0:1, :], start=True, stop=True,
                          r=(Brn_, Bconst), w=(Bpb[4],))
                    em.op("scalar", "copy", at_[:, :], pb[pvb][0:64, 0:512],
                          r=(Bpb[pvb],), w=(Bat_,))
                    if h % 2 == 0:
                        em.op("vector", "tensor_tensor",
                              attn_ft[0:64, (h // 2) * 512:(h // 2 + 1) * 512],
                              at_[:, :], pb[4][0:64, 0:512], ALU.mult,
                              r=(Bat_, Bpb[4]), w=(Baf,))
                    else:
                        ssl = (h // 2) % 2
                        em.op("vector", "tensor_tensor", stg[ssl][:, :],
                              at_[:, :], pb[4][0:64, 0:512], ALU.mult,
                              r=(Bat_, Bpb[4]), w=(Bstg[ssl],))
                        em.dma("qsync",
                               attn_ft[64:128,
                                       (h // 2) * 512:(h // 2 + 1) * 512],
                               stg[ssl][:, :], r=(Bstg[ssl],), w=(Baf,))
            for np_ in range((16 if sub >= 5 else 2) if sub >= 4 else 0):
                sl = np_ % 2
                load_wpanel(wosl[sl], Bwo[sl], woTs, np_, 128)
                for ts in range(4):
                    bk = pingbank()
                    st_i = (np_ * 4 + ts) % 4
                    em.mm_group(
                        pb[bk][0:128, 0:128],
                        [(attn_ft[:, dc * 512 + ts * 128:dc * 512 + (ts + 1) * 128],
                          wosl[sl][:, dc * 128:(dc + 1) * 128]) for dc in range(DC)],
                        r=(Bwo[sl], Baf), w=(Bpb[bk],))
                    em.op("scalar", "copy", s0st[st_i][:, 0:128],
                          pb[bk][0:128, 0:128], r=(Bpb[bk],), w=(Bs0l[st_i],))
                    em.dma("qsync",
                           s_scr[qc * 512 + ts * 128: qc * 512 + (ts + 1) * 128,
                                 np_ * 128:(np_ + 1) * 128],
                           s0st[st_i][:, 0:128], r=(Bs0l[st_i],), w=(Bs_scr,))
        arena.reset()

    def phase_p2():
        em.tag = "p2"
        arena.reset()
        w0sb, Bw0 = arena.alloc("w0sb", [128, DC * D], BF16)
        scl, Bscl = arena.alloc("scl", [128, D], F32)
        o_, Bo, o2, Bo2, s_, Bs, x_, Bx = [], [], [], [], [], [], [], []
        stt, Bstt = [], []
        for i in range(2):
            t_, b_ = arena.alloc("p2o", [128, D], F32); o_.append(t_); Bo.append(b_)
            t_, b_ = arena.alloc("p2o2", [128, D], F32); o2.append(t_); Bo2.append(b_)
            t_, b_ = arena.alloc("p2s", [128, D], F32); s_.append(t_); Bs.append(b_)
            t_, b_ = arena.alloc("p2x", [128, D], F32); x_.append(t_); Bx.append(b_)
            t_, b_ = arena.alloc("p2st", [128, 24], F32); stt.append(t_); Bstt.append(b_)

        em.dma("qsync", scl[:, :], sclbc[:, :], w=(Bscl,))
        for ch in range(DC):
            stg0 = x_[ch % 2]
            Bstg0 = Bx[ch % 2]
            em.dma("qsync", stg0[:, :], w0_h[ch * 128:(ch + 1) * 128, :],
                   w=(Bstg0,))
            if ch % 2 == 0:
                em.op("vector", "tensor_copy", w0sb[:, ch * D:(ch + 1) * D],
                      stg0[:, :], r=(Bstg0,), w=(Bw0,))
            else:
                em.op("scalar", "copy", w0sb[:, ch * D:(ch + 1) * D],
                      stg0[:, :], r=(Bstg0,), w=(Bw0,))
        def p2_chain(tt):
            sl = tt % 2
            stp = stt[sl]
            Bst = Bstt[sl]
            ss = stp[:, 0:1]; ssm = stp[:, 1:2]; lnv = stp[:, 2:3]
            r0 = stp[:, 3:4]; e1 = stp[:, 4:5]; e2 = stp[:, 5:6]
            hh = stp[:, 6:7]; rv = stp[:, 7:8]; vn = stp[:, 8:9]
            e2x = stp[:, 9:10]; dd = stp[:, 10:11]; rd_ = stp[:, 11:12]
            th = stp[:, 12:13]; dn = stp[:, 13:14]; rdn = stp[:, 14:15]
            f1 = stp[:, 15:16]; onrm = stp[:, 16:17]; ron = stp[:, 17:18]
            cc = stp[:, 18:19]; mlt = stp[:, 19:20]
            em.op("scalar", "activation", o2[sl][:, :], o_[sl][:, :], AF.Square,
                  r=(Bo[sl],), w=(Bo2[sl],))
            em.op("vector", "reduce_sum", ss, o2[sl][:, :], axis=AX.X,
                  r=(Bo2[sl],), w=(Bst,))
            v = lambda fn, *a, **k: em.op("vector", fn, *a, r=(Bst,), w=(Bst,), **k)
            v("tensor_scalar", ssm, ss, 1e-14, None, ALU.max)
            em.op("scalar", "activation", lnv, ssm, AF.Ln, r=(Bst,), w=(Bst,))
            em.op("scalar", "activation", r0, lnv, AF.Exp, scale=-0.5,
                  r=(Bst,), w=(Bst,))
            v("tensor_tensor", e1, ssm, r0, ALU.mult)
            v("tensor_tensor", e2, e1, r0, ALU.mult)
            v("tensor_scalar", hh, e2, -0.5, 1.5, ALU.mult, ALU.add)
            v("tensor_tensor", rv, r0, hh, ALU.mult)
            v("tensor_tensor", vn, ssm, rv, ALU.mult)
            em.op("scalar", "activation", e2x, vn, AF.Exp, scale=2.0,
                  r=(Bst,), w=(Bst,))
            v("tensor_scalar", dd, e2x, 1.0, None, ALU.add)
            em.op("vector", "reciprocal", rd_, dd, r=(Bst,), w=(Bst,))
            v("tensor_scalar", th, rd_, -2.0, 1.0, ALU.mult, ALU.add)
            v("tensor_scalar", dn, vn, 1e-7, None, ALU.add)
            em.op("vector", "reciprocal", rdn, dn, r=(Bst,), w=(Bst,))
            v("tensor_tensor", f1, th, rdn, ALU.mult)
            v("tensor_tensor", onrm, f1, vn, ALU.mult)
            v("tensor_scalar", onrm, onrm, 1e-7, None, ALU.max)
            em.op("vector", "reciprocal", ron, onrm, r=(Bst,), w=(Bst,))
            v("tensor_scalar", cc, ron, 0.99, None, ALU.mult)
            v("tensor_scalar", cc, cc, 1.0, None, ALU.min)
            v("tensor_tensor", mlt, f1, cc, ALU.mult)
            em.op("vector", "tensor_scalar", o2[sl][:, :], o_[sl][:, :], mlt,
                  None, ALU.mult, r=(Bo[sl], Bst), w=(Bo2[sl],))
            em.op("scalar", "activation", o_[sl][:, :], o2[sl][:, :], AF.Silu,
                  r=(Bo2[sl],), w=(Bo[sl],))
            em.op("gpsimd", "tensor_tensor", s_[sl][:, :], s_[sl][:, :],
                  o_[sl][:, :], ALU.mult, r=(Bs[sl], Bo[sl]), w=(Bs[sl],))
            em.op("vector", "tensor_tensor", x_[sl][:, :], x_[sl][:, :],
                  s_[sl][:, :], ALU.add, r=(Bx[sl], Bs[sl]), w=(Bx[sl],))
            em.dma("qsync", x1_scr[tt * 128:(tt + 1) * 128, :], x_[sl][:, :],
                   r=(Bx[sl],), w=(Bx1_scr,))

        for tt in range(NT):
            sl = tt % 2
            em.dma("qsync", s_[sl][:, :], s_scr[tt * 128:(tt + 1) * 128, :],
                   r=(Bs_scr,), w=(Bs[sl],))
            em.dma("qsync", x_[sl][:, :], x_h[tt * 128:(tt + 1) * 128, :],
                   w=(Bx[sl],))
            for kc in range(4):
                bank = (tt * 4 + kc) % 4
                em.mm_group(
                    pb[bank][0:128, 0:512],
                    [(xn_ft[:, dc * T + tt * 128:dc * T + (tt + 1) * 128],
                      w0sb[:, dc * D + kc * 512:dc * D + kc * 512 + 512])
                     for dc in range(DC)],
                    r=(Bxn, Bw0), w=(Bpb[bank],))
                em.op("vector", "tensor_tensor",
                      o_[sl][:, kc * 512:(kc + 1) * 512],
                      pb[bank][0:128, 0:512], scl[:, kc * 512:(kc + 1) * 512],
                      ALU.mult, r=(Bpb[bank], Bscl), w=(Bo[sl],))
            if tt > 0:
                p2_chain(tt - 1)
        p2_chain(NT - 1)
        arena.reset()

    def phase_p4():
        em.tag = "p4"
        arena.reset()
        kv_sb, Bkv = arena.alloc("kv_sb", [128, D], F32)
        kv_bf, Bkvb = arena.alloc("kv_bf", [128, D], BF16)
        keep = arena.off
        wk_p, Bwk, wv_p, Bwv, kt_, Bkt, vt_, Bvt = [], [], [], [], [], [], [], []
        for i in range(2):
            t_, b_ = arena.alloc("wk_p", [128, DC * 256], BF16); wk_p.append(t_); Bwk.append(b_)
            t_, b_ = arena.alloc("wv_p", [128, DC * 256], BF16); wv_p.append(t_); Bwv.append(b_)
            t_, b_ = arena.alloc("ktile", [128, 256], BF16); kt_.append(t_); Bkt.append(b_)
            t_, b_ = arena.alloc("vtile", [128, 256], BF16); vt_.append(t_); Bvt.append(b_)

        gz4, Bgz4, gst4, Bgst4 = [], [], [], []
        for par in range(2):
            z5, Bz5 = [], []
            for i in range(5):
                t_, b_ = arena.alloc("gz4", [128, 8], F32)
                z5.append(t_); Bz5.append(b_)
            gz4.append(z5); Bgz4.append(Bz5)
            t_, b_ = arena.alloc("gst4", [128, 32], F32)
            gst4.append(t_); Bgst4.append(b_)

        em.op("vector", "memset", kv_sb[:, :], 0.0, w=(Bkv,))

        def load_np(slot, Bslot, staged_h, pnl, ncols):
            em.dma("qgpsimd", slot[:, 0:DC * ncols], staged_h[pnl],
                   w=(Bslot,))

        def emit_gate(gti):
            par = gti % 2
            em.mm_group(
                pb[6][0:128, 0:8],
                [(xn_ft[:, d2 * T + gti * 128: d2 * T + (gti + 1) * 128],
                  gw_sb[:, d2 * 8:(d2 + 1) * 8]) for d2 in range(DC)],
                r=(Bxn, Bconst), w=(Bpb[6],))
            gate_tile(gti, pb[6], Bpb[6], gz4[par], Bgz4[par],
                      gst4[par], Bgst4[par])

        def kv_op(pidx):
            nc8p = pidx // 16
            slp = pidx % 2
            ob = 4 + pidx % 2
            for hh in range(2):
                em.op("tensor", "matmul",
                      pb[ob][0:128, hh * 128:(hh + 1) * 128],
                      kt_[slp][:, hh * 128:(hh + 1) * 128],
                      vt_[slp][:, hh * 128:(hh + 1) * 128],
                      start=True, stop=True,
                      r=(Bkt[slp], Bvt[slp]), w=(Bpb[ob],))
            em.op("vector", "tensor_tensor",
                  kv_sb[:, nc8p * 256:(nc8p + 1) * 256],
                  kv_sb[:, nc8p * 256:(nc8p + 1) * 256],
                  pb[ob][0:128, 0:256], ALU.add,
                  r=(Bkv, Bpb[ob]), w=(Bkv,))

        for idx in range(128):
            nc8, tt = idx // 16, idx % 16
            sl = idx % 2
            wsl_ = nc8 % 2
            if tt == 0:
                load_np(wk_p[wsl_], Bwk[wsl_], Wks, nc8, 256)
                load_np(wv_p[wsl_], Bwv[wsl_], Wvs, nc8, 256)
            em.mm_group(
                pb[sl][0:128, 0:256],
                [(xn_ft[:, dc * T + tt * 128:dc * T + (tt + 1) * 128],
                  wk_p[wsl_][:, dc * 256:(dc + 1) * 256]) for dc in range(DC)],
                r=(Bxn, Bwk[wsl_]), w=(Bpb[sl],))
            em.op("vector", "tensor_scalar", kt_[sl][:, :],
                  pb[sl][0:128, 0:256], 0.0, L1_RSCALE, ALU.max, ALU.mult,
                  r=(Bpb[sl],), w=(Bkt[sl],))
            em.mm_group(
                pb[2 + sl][0:128, 0:256],
                [(xn_ft[:, dc * T + tt * 128:dc * T + (tt + 1) * 128],
                  wv_p[wsl_][:, dc * 256:(dc + 1) * 256]) for dc in range(DC)],
                r=(Bxn, Bwv[wsl_]), w=(Bpb[2 + sl],))
            em.op("scalar", "copy", vt_[sl][:, :], pb[2 + sl][0:128, 0:256],
                  r=(Bpb[2 + sl],), w=(Bvt[sl],))
            if idx > 0:
                kv_op(idx - 1)
            if idx % 8 == 7:
                emit_gate(idx // 8)
        kv_op(127)
        em.op("scalar", "copy", kv_bf[:, :], kv_sb[:, :], r=(Bkv,), w=(Bkvb,))

        arena.reset(keep)
        q_sb, _Bq0 = arena.alloc("q_sb", [128, L1_H * 512], BF16)
        u_sb, _Bu0 = arena.alloc("u_sb", [128, L1_H * 512], BF16)
        Bq = [em.buf(f"qsl{i}", fence=[_Bq0]) for i in range(L1_H)]
        Bu = [em.buf(f"usl{i}", fence=[_Bu0]) for i in range(L1_H)]
        out_sb, Bo4 = arena.alloc("out_sb", [128, L1_H * 512], BF16)
        z_sb, Bz = arena.alloc("z_sb", [128, DC * 512], BF16)
        wp, Bwp = [], []
        for i in range(3):
            t_, b_ = arena.alloc("wp4", [128, DC * 512], BF16); wp.append(t_); Bwp.append(b_)
        sqs, Bsq = [], []
        for i in range(2):
            t_, b_ = arena.alloc("sqs", [128, 512], F32)
            sqs.append(t_); Bsq.append(b_)
        s1l, Bs1 = [], []
        for i in range(2):
            t_, b_ = arena.alloc("s1st", [128, 512], F32); s1l.append(t_); Bs1.append(b_)
        rn4, Brn4 = arena.alloc("rn4", [1, 512], F32)
        r1 = st2[0:1, 0:512]; r2 = st2[0:1, 512:1024]

        for qc in range(4):
            def head_step(h):
                if h < L1_H:
                    bank = (2, 3, 6)[h % 3]
                    em.op("tensor", "matmul", pb[bank][0:128, 0:512],
                          kv_bf[:, h * 128:(h + 1) * 128],
                          q_sb[:, h * 512:(h + 1) * 512], start=True, stop=True,
                          r=(Bkvb, Bq[h]), w=(Bpb[bank],))
                    em.op("scalar", "activation", sqs[h % 2][:, :],
                          pb[bank][0:128, 0:512],
                          AF.Square, r=(Bpb[bank],), w=(Bsq[h % 2],))
                    em.op("scalar", "copy", out_sb[:, h * 512:(h + 1) * 512],
                          pb[bank][0:128, 0:512], r=(Bpb[bank],), w=(Bo4,))
                if 1 <= h < L1_H + 1:
                    h1 = h - 1
                    nb = 4 + h1 % 2
                    r1h = st2[0:1, (h1 % 2) * 512:(h1 % 2) * 512 + 512]
                    em.op("tensor", "matmul", pb[nb][0:1, 0:512], ones128,
                          sqs[h1 % 2][:, :], start=True, stop=True,
                          r=(Bsq[h1 % 2], Bconst), w=(Bpb[nb],))
                    em.op("vector", "tensor_scalar", r1h, pb[nb][0:1, 0:512],
                          1.0 / L1_DH, 1e-24, ALU.mult, ALU.max,
                          r=(Bpb[nb],), w=(Bst2,))
                    em.op("scalar", "activation", r1h, r1h, AF.Ln,
                          r=(Bst2,), w=(Bst2,))
                    em.op("scalar", "activation", r1h, r1h, AF.Exp, scale=-0.5,
                          r=(Bst2,), w=(Bst2,))
                if 2 <= h:
                    h2 = h - 2
                    nb2 = 4 + h2 % 2
                    r1h2 = st2[0:1, (h2 % 2) * 512:(h2 % 2) * 512 + 512]
                    em.op("tensor", "matmul", pb[nb2][0:128, 0:512],
                          ones_row128, r1h2, start=True, stop=True,
                          r=(Bst2, Bconst), w=(Bpb[nb2],))
                    em.op("vector", "tensor_tensor",
                          z_sb[:, h2 * 512:(h2 + 1) * 512],
                          out_sb[:, h2 * 512:(h2 + 1) * 512],
                          u_sb[:, h2 * 512:(h2 + 1) * 512], ALU.mult,
                          r=(Bo4, Bu[h2]), w=(Bz,))
                    em.op("vector", "tensor_tensor",
                          z_sb[:, h2 * 512:(h2 + 1) * 512],
                          z_sb[:, h2 * 512:(h2 + 1) * 512],
                          pb[nb2][0:128, 0:512],
                          ALU.mult, r=(Bz, Bpb[nb2]), w=(Bz,))

            step = [0]
            for (W_s, dstsb, Bdst, act) in ((Wqs, q_sb, Bq, "relu"),
                                            (Wus, u_sb, Bu, "silu")):
                for np8 in range(8):
                    sl = np8 % 3
                    load_np(wp[sl], Bwp[sl], W_s, np8, 256)
                    for sub in range(2):
                        nt = np8 * 2 + sub
                        bank = nt % 2
                        em.mm_group(
                            pb[bank][0:128, 0:512],
                            [(wp[sl][:, dc * 256 + sub * 128:dc * 256 + sub * 128 + 128],
                              xn_ft[:, dc * T + qc * 512:dc * T + (qc + 1) * 512])
                             for dc in range(DC)],
                            r=(Bwp[sl], Bxn), w=(Bpb[bank],))
                        if act == "relu":
                            em.op("vector", "tensor_scalar",
                                  dstsb[:, nt * 512:(nt + 1) * 512],
                                  pb[bank][0:128, 0:512], 0.0, L1_RSCALE,
                                  ALU.max, ALU.mult,
                                  r=(Bpb[bank],), w=(Bdst[nt],))
                        else:
                            em.op("scalar", "activation",
                                  dstsb[:, nt * 512:(nt + 1) * 512],
                                  pb[bank][0:128, 0:512], AF.Silu,
                                  r=(Bpb[bank],), w=(Bdst[nt],))
            while step[0] < L1_H + 2:
                head_step(step[0]); step[0] += 1
            for np8 in range(4):
                sl = np8 % 3
                load_np(wp[sl], Bwp[sl], Wos, np8, 512)
                for ts in range(4):
                    bank = 6 if ts % 2 == 0 else 3
                    ssl = ts % 2
                    em.mm_group(
                        pb[bank][0:128, 0:512],
                        [(z_sb[:, dc * 512 + ts * 128:dc * 512 + ts * 128 + 128],
                          wp[sl][:, dc * 512:(dc + 1) * 512]) for dc in range(DC)],
                        r=(Bwp[sl], Bz), w=(Bpb[bank],))
                    em.op("scalar", "copy", s1l[ssl][:, :],
                          pb[bank][0:128, 0:512], r=(Bpb[bank],), w=(Bs1[ssl],))
                    em.dma("qsync",
                           s_scr[qc * 512 + ts * 128:qc * 512 + (ts + 1) * 128,
                                 np8 * 512:(np8 + 1) * 512],
                           s1l[ssl][:, :], r=(Bs1[ssl],), w=(Bs_scr,))
        arena.reset()

    def phase_p5():
        em.tag = "p5"
        arena.reset()
        H_sb, BH = arena.alloc("H_sb", [128, NHC * T], BF16)
        w13, Bw13 = [], []
        for i in range(4):
            t_, b_ = arena.alloc("w13", [128, DC * 128], BF16); w13.append(t_); Bw13.append(b_)
        w2sl, Bw2 = [], []
        for i in range(2):
            t_, b_ = arena.alloc("w2sl", [128, NHC * 512], BF16); w2sl.append(t_); Bw2.append(b_)
        B_sb, BB = arena.alloc("B_sb", [128, T], F32)
        NACC = 3
        acci, Bai, acco, Bao = [], [], [], []
        for i in range(NACC):
            t_, b_ = arena.alloc("acci", [128, 512], F32); acci.append(t_); Bai.append(b_)
            t_, b_ = arena.alloc("acco", [128, 512], F32); acco.append(t_); Bao.append(b_)
        a_sc, Ba = arena.alloc("a_sc", [128, 512], F32)
        sl_sc, Bsl = arena.alloc("sl_sc", [128, 512], F32)
        grows, Bgr = arena.alloc("grows", [8, T], F32)
        oh_sb, Boh = arena.alloc("oh_sb", [8, 8 * 128], F32)

        em.dma("qsync", oh_sb[:, :], oh_h[:, :], w=(Boh,))
        for tt in range(NT):
            em.op("tensor", "transpose", pb[0][0:8, 0:128],
                  gates[:, tt * 8:(tt + 1) * 8], idf[:, :],
                  r=(Bgates, Bconst), w=(Bpb[0],))
            em.op("vector", "tensor_copy", grows[0:8, tt * 128:(tt + 1) * 128],
                  pb[0][0:8, 0:128], r=(Bpb[0],), w=(Bgr,))

        experts = [("shared", NEXP)] + [(e, e) for e in range(NEXP)]
        for (e, ei) in experts:
            shared = (e == "shared")
            if not shared:
                for tc in range(4):
                    em.op("tensor", "matmul", pb[6][0:128, 0:512],
                          oh_sb[:, e * 128:(e + 1) * 128],
                          grows[0:8, tc * 512:(tc + 1) * 512],
                          start=True, stop=True, r=(Boh, Bgr), w=(Bpb[6],))
                    em.op("vector", "tensor_copy",
                          B_sb[:, tc * 512:(tc + 1) * 512],
                          pb[6][0:128, 0:512], r=(Bpb[6],), w=(BB,))
            for hc in range(NHC):
                s1_ = hc % 2
                s3_ = 2 + hc % 2
                em.dma("qgpsimd", w13[s1_][:, :], w1s_h[ei, hc],
                       w=(Bw13[s1_],))
                em.dma("qgpsimd", w13[s3_][:, :], w3s_h[ei, hc],
                       w=(Bw13[s3_],))
                for tc in range(4):
                    b1 = tc % 2
                    b3 = 2 + tc % 2
                    em.mm_group(
                        pb[b1][0:128, 0:512],
                        [(w13[s1_][:, dc * 128:(dc + 1) * 128],
                          xn_ft[:, dc * T + tc * 512:dc * T + (tc + 1) * 512])
                         for dc in range(DC)],
                        r=(Bw13[s1_], Bxn), w=(Bpb[b1],))
                    em.mm_group(
                        pb[b3][0:128, 0:512],
                        [(w13[s3_][:, dc * 128:(dc + 1) * 128],
                          xn_ft[:, dc * T + tc * 512:dc * T + (tc + 1) * 512])
                         for dc in range(DC)],
                        r=(Bw13[s3_], Bxn), w=(Bpb[b3],))
                    if shared:
                        em.op("scalar", "activation", sl_sc[:, :],
                              pb[b1][0:128, 0:512], AF.Silu,
                              r=(Bpb[b1],), w=(Bsl,))
                    else:
                        em.op("vector", "tensor_tensor", a_sc[:, :],
                              pb[b1][0:128, 0:512],
                              B_sb[:, tc * 512:(tc + 1) * 512], ALU.mult,
                              r=(Bpb[b1], BB), w=(Ba,))
                        em.op("scalar", "activation", sl_sc[:, :], a_sc[:, :],
                              AF.Silu, r=(Ba,), w=(Bsl,))
                    em.op("vector", "tensor_tensor",
                          H_sb[:, hc * T + tc * 512:hc * T + (tc + 1) * 512],
                          sl_sc[:, :], pb[b3][0:128, 0:512], ALU.mult,
                          r=(Bsl, Bpb[b3]), w=(BH,))
            w2banks = (3, 4, 5, 6)
            iters = [(nc4, ts) for nc4 in range(4) for ts in range(NT)]
            last = (not shared) and e == NEXP - 1
            if last:
                s_ch = [B_sb[:, 0:512], B_sb[:, 512:1024]]
                x1_ch = [B_sb[:, 1024:1536], B_sb[:, 1536:2048]]
                Bs_ch = [em.buf(f"lsch{i}", fence=[BB]) for i in range(2)]
                Bx1_ch = [em.buf(f"lx1ch{i}", fence=[BB]) for i in range(2)]
                fgt = [a_sc, sl_sc]
                Bfgt = [em.buf("lfg0", fence=[Ba]),
                        em.buf("lfg1", fence=[Bsl])]

                def sx_load(i):
                    if i >= 64:
                        return
                    nc4i, tsi = iters[i]
                    s2 = i % 2
                    em.dma("qsync", s_ch[s2],
                           s_scr[tsi * 128:(tsi + 1) * 128,
                                 nc4i * 512:(nc4i + 1) * 512],
                           r=(Bs_scr,), w=(Bs_ch[s2],))
                    em.dma("qsync", x1_ch[s2],
                           x1_scr[tsi * 128:(tsi + 1) * 128,
                                  nc4i * 512:(nc4i + 1) * 512],
                           r=(Bx1_scr,), w=(Bx1_ch[s2],))
                sx_load(0)
                sx_load(1)

            def acci_load(i):
                if shared or i >= 64:
                    return
                nc4i, tsi = iters[i]
                sli = i % NACC
                em.dma("qsync", acci[sli][:, :],
                       acc_scr[tsi * 128:(tsi + 1) * 128,
                               nc4i * 512:(nc4i + 1) * 512],
                       r=(Bacc_scr[nc4i * 16 + tsi],), w=(Bai[sli],))

            if not shared:
                for j in range(NACC):
                    acci_load(j)
            for i, (nc4, ts) in enumerate(iters):
                if i % 16 == 0:
                    sw = nc4 % 2
                    em.dma("qgpsimd", w2sl[sw][:, :], w2s_h[ei, nc4],
                           w=(Bw2[sw],))
                sl = i % NACC
                bank = w2banks[i % 4]
                swc = nc4 % 2
                em.mm_group(
                    pb[bank][0:128, 0:512],
                    [(H_sb[:, hc * T + ts * 128:hc * T + ts * 128 + 128],
                      w2sl[swc][:, hc * 512:(hc + 1) * 512])
                     for hc in range(NHC)],
                    r=(BH, Bw2[swc]), w=(Bpb[bank],))
                if shared:
                    em.op("scalar", "copy", acco[sl][:, :],
                          pb[bank][0:128, 0:512], r=(Bpb[bank],),
                          w=(Bao[sl],))
                else:
                    em.op("vector", "scalar_tensor_tensor", acco[sl][:, :],
                          pb[bank][0:128, 0:512],
                          gates[:, ts * 8 + e: ts * 8 + e + 1],
                          acci[sl][:, :], ALU.mult, ALU.add,
                          r=(Bpb[bank], Bgates, Bai[sl]), w=(Bao[sl],))
                if last:
                    s2 = i % 2
                    em.op("scalar", "activation", fgt[s2][:, :],
                          acco[sl][:, :], AF.Silu,
                          r=(Bao[sl],), w=(Bfgt[s2],))
                    em.op("vector", "tensor_tensor", fgt[s2][:, :],
                          fgt[s2][:, :], s_ch[s2], ALU.mult,
                          r=(Bfgt[s2], Bs_ch[s2]), w=(Bfgt[s2],))
                    em.op("vector", "tensor_tensor", x1_ch[s2], x1_ch[s2],
                          fgt[s2][:, :], ALU.add,
                          r=(Bx1_ch[s2], Bfgt[s2]), w=(Bx1_ch[s2],))
                    em.dma("qsync",
                           y_h[ts * 128:(ts + 1) * 128,
                               nc4 * 512:(nc4 + 1) * 512],
                           x1_ch[s2], r=(Bx1_ch[s2],))
                    sx_load(i + 2)
                else:
                    em.dma("qsync",
                           acc_scr[ts * 128:(ts + 1) * 128,
                                   nc4 * 512:(nc4 + 1) * 512],
                           acco[sl][:, :], r=(Bao[sl],),
                           w=(Bacc_scr[nc4 * 16 + ts],))
                acci_load(i + NACC)


    if "ln0" in phases:
        phase_ln(x_h, None, layer=0,
                 tap="xnd0" if "xnd0" in debug_taps else None)
    if "p1" in phases:
        import os as _os
        phase_p1(sub=int(_os.environ.get("P1SUB", "5")))
    if "p2" in phases:
        phase_p2()
    if "ln1" in phases:
        phase_ln(x1_scr, Bx1_scr, layer=1,
                 tap="xnd1" if "xnd1" in debug_taps else None)
    if "p4" in phases:
        phase_p4()
    if "p5" in phases:
        phase_p5()
    if "p5" not in phases:
        Bdummy = em.buf("ydummy")
        for tt in range(NT):
            pass

    from contextlib import ExitStack
    es = ExitStack()
    sems = {}
    for s in list(ENGS) + list(QSEMS):
        sems[s] = es.enter_context(nc.semaphore(f"sem_{s}"))
    em.finalize(nc, sems)
    es.close()
    return nc


def _host_consts():
    cst = np.zeros((128, 160), np.float32)
    cst[:, 0] = -20.0
    cst[:, 1] = 1.0
    cst[:, 8:136] = 1.0
    idb = np.eye(128, dtype=ml_dtypes.bfloat16)
    idf = np.eye(128, dtype=np.float32)
    oh = np.zeros((8, 8 * 128), np.float32)
    for e in range(8):
        oh[e, e * 128:(e + 1) * 128] = 1.0
    mask01 = np.zeros((128, 4 * 512), ml_dtypes.bfloat16)
    kk = np.arange(128)[:, None]
    q = np.arange(512)[None, :]
    for i in range(4):
        mask01[:, i * 512:(i + 1) * 512] = ((128 * i + kk) <= q).astype(
            ml_dtypes.bfloat16)
    return cst, idb, idf, oh, mask01


def _stage_panels(mT, ncols):
    Dn, X = mT.shape
    npan = X // ncols
    return np.ascontiguousarray(
        mT.reshape(16, 128, npan, ncols).transpose(2, 1, 0, 3)
          .reshape(npan, 128, 16 * ncols))


def _stage_w13(w):
    return np.ascontiguousarray(
        w.reshape(16, 128, 14, 128).transpose(2, 1, 0, 3).reshape(14, 128, 2048))


def _stage_w2(w):
    return np.ascontiguousarray(
        w.reshape(14, 128, 4, 512).transpose(2, 1, 0, 3).reshape(4, 128, 14 * 512))


def host_inputs(inputs):
    f32 = lambda a: np.ascontiguousarray(np.asarray(a), dtype=np.float32)
    x = f32(inputs["L0_attn_x"])
    scl = f32(inputs["L0_ffn_scale"])
    biasE = f32(inputs["L1_ffn_expert_bias"])
    cst, idb, idf, oh, mask01 = _host_consts()
    w1 = f32(inputs["L1_ffn_w1"]); w2 = f32(inputs["L1_ffn_w2"])
    w3 = f32(inputs["L1_ffn_w3"])
    shared = {
        "wqTs": _stage_panels(f32(inputs["L0_attn_wq"]).T, 128),
        "wkTs": _stage_panels(f32(inputs["L0_attn_wk"]).T, 128),
        "wvTs": _stage_panels(f32(inputs["L0_attn_wv"]).T, 128),
        "woTs": _stage_panels(f32(inputs["L0_attn_wo"]).T, 128),
        "w0": f32(inputs["L0_ffn_weight"]),
        "sclbc": np.ascontiguousarray(np.broadcast_to(scl[None, :], (128, D))),
        "Wqs": _stage_panels(f32(inputs["L1_attn_Wq"]), 256),
        "Wks": _stage_panels(f32(inputs["L1_attn_Wk"]), 256),
        "Wvs": _stage_panels(f32(inputs["L1_attn_Wv"]), 256),
        "Wus": _stage_panels(f32(inputs["L1_attn_Wu"]), 256),
        "Wos": _stage_panels(f32(inputs["L1_attn_Wo"]), 512),
        "gw": f32(inputs["L1_ffn_gate_w"]),
        "w1s": np.stack([_stage_w13(w1[e]) for e in range(NEXP)]
                        + [_stage_w13(f32(inputs["L1_ffn_sw1"]))]),
        "w3s": np.stack([_stage_w13(w3[e]) for e in range(NEXP)]
                        + [_stage_w13(f32(inputs["L1_ffn_sw3"]))]),
        "w2s": np.stack([_stage_w2(w2[e]) for e in range(NEXP)]
                        + [_stage_w2(f32(inputs["L1_ffn_sw2"]))]),
        "biasE": np.ascontiguousarray(np.broadcast_to(biasE[None, :], (128, 8))),
        "cst": cst, "idb": idb, "idf": idf, "oh": oh, "mask01": mask01,
    }
    return x, shared


_NC_CACHE = {}


def kernel(**inputs):
    x, shared = host_inputs(inputs)
    if "nc" not in _NC_CACHE:
        _NC_CACHE["nc"] = build_program()
    nc = _NC_CACHE["nc"]
    in_maps = [dict(shared, x=np.ascontiguousarray(x[c])) for c in range(8)]
    res = run_bass_kernel_spmd(nc, in_maps, core_ids=list(range(8)))
    return np.stack([res.results[c]["y"] for c in range(8)], axis=0)


# version 10
# speedup vs baseline: 1.0350x; 1.0338x over previous
import math
import numpy as np
import ml_dtypes
import concourse.bass as bass
import concourse.mybir as mybir
from concourse.bass_utils import run_bass_kernel_spmd

F32 = mybir.dt.float32
BF16 = mybir.dt.bfloat16
AF = mybir.ActivationFunctionType
ALU = mybir.AluOpType
AX = mybir.AxisListType

T = 2048
D = 2048
NT = 16
DC = 16
HID = 1792
NHC = 14
NEXP = 8
L0_HQ, L0_HKV, L0_DH = 32, 8, 64
L0_SCALE = 0.125
L1_H, L1_DH = 16, 128
L1_RSCALE = float(np.float32(1.0) / np.float32(math.sqrt(128.0)))

QUEUES = {"qsync": "sync", "qgpsimd": "gpsimd"}
NLANES = 6
QSEMS = {f"{q}{l}": e for q, e in QUEUES.items() for l in range(NLANES)}
ENGS = ("tensor", "vector", "scalar", "gpsimd", "sync")

SBUF_BASE = 17408
SBUF_TOP = 229120


class Buf:
    __slots__ = ("name", "writer", "readers", "fence")

    def __init__(self, name, fence=None):
        self.name = name
        self.writer = None
        self.readers = []
        self.fence = fence


class _Op:
    __slots__ = ("eng", "sem", "fn", "args", "kwargs", "inc", "idx", "val",
                 "waits", "is_dma", "tag")

    def __init__(self, eng, sem, fn, args, kwargs, is_dma, tag):
        self.eng, self.sem, self.fn = eng, sem, fn
        self.args, self.kwargs = args, kwargs
        self.inc = is_dma
        self.idx = -1
        self.val = 0
        self.waits = []
        self.is_dma = is_dma
        self.tag = tag


class Emit:
    def __init__(self):
        self.ops = {e: [] for e in ENGS}
        self.counts = {s: 0 for s in list(ENGS) + list(QSEMS)}
        self.qcount = {q: 0 for q in QUEUES}
        self.last_lane_dma = {s: None for s in QSEMS}
        self.clock = {}
        self.tag = ""

    def buf(self, name, fence=None):
        return Buf(name, fence)

    def _edge(self, op, prod, is_read=False):
        if prod is None or prod is op:
            return
        if (not prod.is_dma) and prod.sem == op.eng:
            if (not is_read) or op.eng == "tensor":
                return
        key = (op.eng, prod.sem)
        if self.clock.get(key, -1) >= prod.idx:
            return
        self.clock[key] = prod.idx
        prod.inc = True
        op.waits.append(prod)

    def _record(self, eng, sem, fn, args, kwargs, reads, writes, is_dma):
        op = _Op(eng, sem, fn, args, kwargs, is_dma, self.tag)
        op.idx = self.counts[sem]
        self.counts[sem] = op.idx + 1
        for b in writes:
            if b.fence is not None:
                for ob in b.fence:
                    self._edge(op, ob.writer)
                    for r in ob.readers:
                        self._edge(op, r)
                b.fence = None
        for b in reads:
            self._edge(op, b.writer, is_read=True)
        for b in writes:
            self._edge(op, b.writer)
            for r in b.readers:
                self._edge(op, r)
        for b in reads:
            b.readers.append(op)
        for b in writes:
            b.writer = op
            b.readers = []
        self.ops[eng].append(op)
        return op

    def op(self, eng, fn, *args, r=(), w=(), **kwargs):
        return self._record(eng, eng, fn, args, kwargs, r, w, False)

    def dma(self, queue, out, in_, r=(), w=(), **kwargs):
        lane = self.qcount[queue] % NLANES
        self.qcount[queue] += 1
        sem = f"{queue}{lane}"
        eng = QUEUES[queue]
        op = self._record(eng, sem, "dma_start",
                          (), dict(out=out, in_=in_, **kwargs), r, w, True)
        prev = self.last_lane_dma[sem]
        if prev is not None and prev not in op.waits:
            key = (eng, sem)
            if self.clock.get(key, -1) < prev.idx:
                self.clock[key] = prev.idx
                op.waits.append(prev)
        self.last_lane_dma[sem] = op
        return op

    def mm_group(self, out_ap, pairs, r=(), w=()):
        n = len(pairs)
        ops = []
        for i, (lt, rh) in enumerate(pairs):
            rr = r if i == 0 else ()
            ww = w if (i == 0 or i == n - 1) else ()
            ops.append(self._record("tensor", "tensor", "matmul",
                                    (out_ap, lt, rh),
                                    dict(start=(i == 0), stop=(i == n - 1)),
                                    rr, ww, False))
        for b in r:
            for i in range(n - 1):
                if ops[i] in b.readers:
                    b.readers.remove(ops[i])
            if ops[-1] not in b.readers:
                b.readers.append(ops[-1])
        return ops[-1]

    def finalize(self, nc, sems):
        cums = {s: 0 for s in self.counts}
        for e in ENGS:
            for op in self.ops[e]:
                s = op.sem
                if op.is_dma:
                    cums[s] += 16
                    op.val = cums[s]
                else:
                    if op.inc:
                        cums[s] += 1
                    op.val = cums[s]

        def emit_stream(engobj, ename):
            for op in self.ops[ename]:
                for prod in op.waits:
                    engobj.wait_ge(sems[prod.sem], prod.val)
                inst = getattr(engobj, op.fn)(*op.args, **op.kwargs)
                if op.inc:
                    inst.then_inc(sems[op.sem], 16 if op.is_dma else 1)
            if ename == "sync":
                for q in QSEMS:
                    if cums[q] > 0:
                        engobj.wait_ge(sems[q], cums[q])

        with nc.Block() as block:
            block.tensor(lambda t: emit_stream(t, "tensor"))
            block.vector(lambda v: emit_stream(v, "vector"))
            block.scalar(lambda s: emit_stream(s, "scalar"))
            block.gpsimd(lambda g: emit_stream(g, "gpsimd"))
            block.sync(lambda s: emit_stream(s, "sync"))


class Arena:
    def __init__(self, nc, em, base, limit):
        self.nc, self.em = nc, em
        self.base, self.limit = base, limit
        self.off = base
        self.n = 0
        self.cur = []
        self.pending = []

    def alloc(self, name, shape, dtype):
        nbytes = shape[1] * (2 if dtype == BF16 else 4)
        nbytes = (nbytes + 31) & ~31
        assert self.off + nbytes <= self.limit, \
            f"SBUF arena overflow at {name}: {self.off}+{nbytes}>{self.limit}"
        t = self.nc.alloc_sbuf_tensor_at(f"{name}_{self.n}", shape, dtype,
                                         offset=self.off)
        self.n += 1
        self.off += nbytes
        b = self.em.buf(name, fence=self.pending if self.pending else None)
        self.cur.append(b)
        return t, b

    def reset(self, to=None):
        self.off = self.base if to is None else to
        self.pending = self.pending + self.cur
        self.cur = []


def build_program(debug_taps=(), phases=("ln0", "p1", "p2", "ln1", "p4", "p5")):
    nc = bass.Bass("TRN2", target_bir_lowering=False, debug=False)
    em = Emit()

    dt = lambda n, s, k="ExternalInput", d=F32: nc.dram_tensor(n, s, d, kind=k).ap()
    x_h = dt("x", [T, D])
    wqTs = dt("wqTs", [16, 128, DC * 128])
    wkTs = dt("wkTs", [4, 128, DC * 128])
    wvTs = dt("wvTs", [4, 128, DC * 128])
    woTs = dt("woTs", [16, 128, DC * 128])
    w0_h = dt("w0", [D, D])
    sclbc = dt("sclbc", [128, D])
    Wqs = dt("Wqs", [8, 128, DC * 256]); Wks = dt("Wks", [8, 128, DC * 256])
    Wvs = dt("Wvs", [8, 128, DC * 256]); Wus = dt("Wus", [8, 128, DC * 256])
    Wos = dt("Wos", [4, 128, DC * 512])
    gw_h = dt("gw", [D, 8])
    w1s_h = dt("w1s", [NEXP + 1, NHC, 128, DC * 128])
    w3s_h = dt("w3s", [NEXP + 1, NHC, 128, DC * 128])
    w2s_h = dt("w2s", [NEXP + 1, 4, 128, NHC * 512])
    biasE_h = dt("biasE", [128, 8])
    cst_h = dt("cst", [128, 160])
    idb_h = dt("idb", [128, 128], d=BF16)
    idf_h = dt("idf", [128, 128])
    oh_h = dt("oh", [8, 8 * 128])
    mask01_h = dt("mask01", [128, 4 * 512], d=BF16)
    y_h = dt("y", [T, D], k="ExternalOutput")
    kind_scr = "ExternalOutput" if "scr" in debug_taps else "Internal"
    s_scr = dt("s_scr", [T, D], k=kind_scr)
    x1_scr = dt("x1_scr", [T, D], k=kind_scr)
    acc_scr = dt("acc_scr", [T, D], k=kind_scr)
    xnd = {}
    for tap in ("xnd0", "xnd1"):
        if tap in debug_taps:
            xnd[tap] = dt(tap, [128, DC * T], k="ExternalOutput", d=BF16)

    off = SBUF_BASE

    def fixed(name, shape, dtyp):
        nonlocal off
        nbytes = shape[1] * (2 if dtyp == BF16 else 4)
        nbytes = (nbytes + 31) & ~31
        t = nc.alloc_sbuf_tensor_at(name, shape, dtyp, offset=off)
        off += nbytes
        return t

    cst = fixed("cstt", [128, 160], F32)
    idb = fixed("idbt", [128, 128], BF16)
    idf = fixed("idft", [128, 128], F32)
    biasE = fixed("biasEt", [128, 8], F32)
    gw_sb = fixed("gw_sbt", [128, DC * 8], BF16)
    gates = fixed("gatest", [128, NT * 8], F32)
    st = fixed("statst", [128, 48], F32)
    st2 = fixed("stats2t", [1, 1024], F32)
    xn_ft = fixed("xn_ftt", [128, DC * T], BF16)

    arena = Arena(nc, em, (off + 31) & ~31, SBUF_TOP)

    c_neg20 = cst[:, 0:1]
    ones128 = cst[:, 1:2]
    ones_row64 = cst[0:1, 8:72]
    ones_row128 = cst[0:1, 8:136]

    pb = [nc.alloc_psum_tensor(f"pb{i}", [128, 512], F32) for i in range(7)]
    pbT = nc.alloc_psum_tensor("pbT", [128, 1024], BF16)
    Bpb = [em.buf(f"pb{i}") for i in range(7)]
    BpbT = em.buf("pbT")

    Bconst = em.buf("const")
    Bxn = em.buf("xn_ft")
    Bgates = em.buf("gates")
    Bst = em.buf("stats")
    Bst2 = em.buf("stats2")
    Bs_scr = em.buf("s_scr")
    Bx1_scr = em.buf("x1_scr")
    Bacc_scr = [em.buf(f"acc_scr{i}") for i in range(64)]

    em.tag = "init"
    em.dma("qsync", cst[:, :], cst_h[:, :], w=(Bconst,))
    em.dma("qsync", idb[:, :], idb_h[:, :], w=(Bconst,))
    em.dma("qsync", idf[:, :], idf_h[:, :], w=(Bconst,))
    em.dma("qsync", biasE[:, :], biasE_h[:, :], w=(Bconst,))
    em.dma("qgpsimd", gw_sb[:, :].rearrange("p (c e) -> p c e", c=DC),
           gw_h[:, :].rearrange("(c p) e -> p c e", p=128), w=(Bconst,))

    def gate_tile(tt, zpb, Bzpb, gz, Bgz, gst, Bst):
        sc, e_, tmp, p_, q_ = (gz[i][:, 0:8] for i in range(5))
        Bsc, Be, Btmp, Bp, Bq = Bgz
        zb = gst[:, 0:8]
        mx = gst[:, 8:9]; nm = gst[:, 9:10]; ssum = gst[:, 10:11]
        rs = gst[:, 11:12]; m_ = gst[:, 12:13]
        m1_ap = gst[:, 16:24]; m2_ap = gst[:, 24:32]
        em.op("vector", "reduce_max", mx, zpb[0:128, 0:8], axis=AX.X,
              r=(Bzpb,), w=(Bst,))
        em.op("vector", "tensor_scalar", nm, mx, -1.0, None, ALU.mult,
              r=(Bst,), w=(Bst,))
        em.op("scalar", "activation", e_, zpb[0:128, 0:8], AF.Exp,
              bias=nm, r=(Bzpb, Bst), w=(Be,))
        em.op("vector", "reduce_sum", ssum, e_, axis=AX.X, r=(Be,), w=(Bst,))
        em.op("vector", "reciprocal", rs, ssum, r=(Bst,), w=(Bst,))
        em.op("vector", "tensor_scalar", sc, e_, rs, None, ALU.mult,
              r=(Be, Bst), w=(Bsc,))
        em.op("vector", "tensor_tensor", zb, sc, biasE[:, :], ALU.add,
              r=(Bsc, Bconst, Bst), w=(Bst,))

        def top1(dst_ap):
            em.op("vector", "reduce_max", m_, zb, axis=AX.X, r=(Bst,), w=(Bst,))
            em.op("vector", "tensor_scalar", tmp, zb, m_, None, ALU.is_ge,
                  r=(Bst,), w=(Btmp,))
            em.op("vector", "tensor_copy", e_[:, 0:1], tmp[:, 0:1],
                  r=(Btmp,), w=(Be,))
            em.op("vector", "tensor_tensor", e_[:, 1:8], tmp[:, 1:8],
                  tmp[:, 0:7], ALU.add, r=(Btmp,), w=(Be,))
            em.op("vector", "tensor_copy", p_[:, 0:2], e_[:, 0:2],
                  r=(Be,), w=(Bp,))
            em.op("vector", "tensor_tensor", p_[:, 2:8], e_[:, 2:8],
                  e_[:, 0:6], ALU.add, r=(Be,), w=(Bp,))
            em.op("vector", "tensor_copy", q_[:, 0:4], p_[:, 0:4],
                  r=(Bp,), w=(Bq,))
            em.op("vector", "tensor_tensor", q_[:, 4:8], p_[:, 4:8],
                  p_[:, 0:4], ALU.add, r=(Bp,), w=(Bq,))
            em.op("vector", "tensor_tensor", q_, q_, tmp, ALU.subtract,
                  r=(Bq, Btmp), w=(Bq,))
            em.op("vector", "tensor_scalar", q_, q_, 0.0, None, ALU.is_equal,
                  r=(Bq,), w=(Bq,))
            em.op("vector", "tensor_tensor", dst_ap, tmp, q_, ALU.mult,
                  r=(Btmp, Bq), w=(Bst,))

        top1(m1_ap)
        em.op("vector", "scalar_tensor_tensor", zb, m1_ap, -1e30, zb,
              ALU.mult, ALU.add, r=(Bst,), w=(Bst,))
        top1(m2_ap)
        em.op("vector", "tensor_tensor", m1_ap, m1_ap, m2_ap, ALU.add,
              r=(Bst,), w=(Bst,))
        em.op("vector", "tensor_tensor", gates[:, tt * 8:(tt + 1) * 8],
              sc, m1_ap, ALU.mult, r=(Bsc, Bst), w=(Bgates,))

    def phase_ln(src_h, Bsrc, layer, tap=None):
        em.tag = f"ln{layer}"
        arena.reset()
        xt, Bxt, xnt, Bxnt, sq, Bsq = [], [], [], [], [], []
        for i in range(3):
            t_, b_ = arena.alloc("ln_x", [128, D], F32); xt.append(t_); Bxt.append(b_)
        for i in range(2):
            t_, b_ = arena.alloc("ln_xn", [128, D], BF16); xnt.append(t_); Bxnt.append(b_)
            t_, b_ = arena.alloc("ln_sq", [128, D], F32); sq.append(t_); Bsq.append(b_)
        lst, Blst = [], []
        for par in range(2):
            t_, b_ = arena.alloc("ln_st", [128, 12], F32); lst.append(t_); Blst.append(b_)

        for tt in range(NT):
            sl = tt % 2
            sl3 = tt % 3
            Bst = Blst[sl]
            s_sum = lst[sl][:, 0:1]; s_ss = lst[sl][:, 1:2]
            s_m = lst[sl][:, 2:3]; s_var = lst[sl][:, 3:4]
            s_r0 = lst[sl][:, 4:5]; s_e = lst[sl][:, 5:6]
            s_e2 = lst[sl][:, 6:7]; s_h = lst[sl][:, 7:8]
            s_rstd = lst[sl][:, 8:9]
            rd = (Bsrc,) if Bsrc else ()
            em.dma("qsync", xt[sl3][:, :], src_h[tt * 128:(tt + 1) * 128, :],
                   r=rd, w=(Bxt[sl3],))
            em.op("vector", "reduce_sum", s_sum, xt[sl3][:, :], axis=AX.X,
                  r=(Bxt[sl3],), w=(Bst,))
            em.op("scalar", "activation", sq[sl][:, :], xt[sl3][:, :],
                  AF.Square, r=(Bxt[sl3],), w=(Bsq[sl],))
            em.op("vector", "reduce_sum", s_ss, sq[sl][:, :], axis=AX.X,
                  r=(Bsq[sl],), w=(Bst,))
            v = lambda fn, *a, **k: em.op("vector", fn, *a, r=(Bst,), w=(Bst,), **k)
            v("tensor_scalar", s_m, s_sum, 1.0 / D, None, ALU.mult)
            v("tensor_scalar", s_var, s_ss, 1.0 / D, None, ALU.mult)
            v("tensor_tensor", s_e, s_m, s_m, ALU.mult)
            v("tensor_tensor", s_var, s_var, s_e, ALU.subtract)
            v("tensor_scalar", s_var, s_var, 1e-5, None, ALU.add)
            em.op("scalar", "activation", s_e, s_var, AF.Ln, r=(Bst,), w=(Bst,))
            em.op("scalar", "activation", s_r0, s_e, AF.Exp, scale=-0.5,
                  r=(Bst,), w=(Bst,))
            v("tensor_tensor", s_e, s_var, s_r0, ALU.mult)
            v("tensor_tensor", s_e2, s_e, s_r0, ALU.mult)
            v("tensor_scalar", s_h, s_e2, -0.5, 1.5, ALU.mult, ALU.add)
            v("tensor_tensor", s_rstd, s_r0, s_h, ALU.mult)
            em.op("vector", "tensor_scalar", xnt[sl][:, :], xt[sl3][:, :],
                  s_m, s_rstd, ALU.subtract, ALU.mult,
                  r=(Bxt[sl3], Bst), w=(Bxnt[sl],))
            for g8 in range(2):
                for j in range(8):
                    dc = g8 * 8 + j
                    em.op("tensor", "transpose", pbT[:, j * 128:(j + 1) * 128],
                          xnt[sl][:, dc * 128:(dc + 1) * 128], idb[:, :],
                          r=(Bxnt[sl], Bconst), w=(BpbT,))
                dst = xn_ft[:, :].rearrange("p (c n) -> p c n", c=DC)[
                    :, g8 * 8:(g8 + 1) * 8, tt * 128:(tt + 1) * 128]
                src = pbT[:, :].rearrange("p (c n) -> p c n", c=8)
                em.op("vector", "tensor_copy", dst, src, r=(BpbT,), w=(Bxn,))

        if tap is not None:
            em.dma("qsync", xnd[tap][:, :], xn_ft[:, :], r=(Bxn,))
        arena.reset()

    def phase_p1(sub=4):
        em.tag = "p1"
        arena.reset()
        k64, Bk64 = arena.alloc("k64", [64, L0_HKV * T], BF16)
        q64, Bq64 = arena.alloc("q64", [64, L0_HQ * 512], BF16)
        v_aug, Bva = arena.alloc("v_aug", [128, NT * 520], BF16)
        attn_ft, Baf = arena.alloc("attn_ft", [128, DC * 512], BF16)
        stg, Bstg, wsl, Bwsl, wosl, Bwo, ptl, Bpt = [], [], [], [], [], [], [], []
        for i in range(2):
            t_, b_ = arena.alloc("attn_stg", [64, 512], BF16); stg.append(t_); Bstg.append(b_)
            t_, b_ = arena.alloc("wsl", [128, DC * 128], BF16); wsl.append(t_); Bwsl.append(b_)
            t_, b_ = arena.alloc("wosl", [128, DC * 128], BF16); wosl.append(t_); Bwo.append(b_)
        for i in range(6):
            t_, b_ = arena.alloc("pt", [128, 512], BF16); ptl.append(t_); Bpt.append(b_)
        mask01, Bm01 = arena.alloc("mask01", [128, 4 * 512], BF16)
        s0st, Bs0l = [], []
        for i in range(4):
            t_, b_ = arena.alloc("s0st", [128, 256], F32); s0st.append(t_); Bs0l.append(b_)
        atmp, Batmp, rnrow, Brn = [], [], [], []
        for i in range(2):
            t_, b_ = arena.alloc("atmp", [64, 512], F32)
            atmp.append(t_); Batmp.append(b_)
            t_, b_ = arena.alloc("rnrow", [1, 512], F32)
            rnrow.append(t_); Brn.append(b_)
        scj = [0]

        em.dma("qsync", mask01[:, :], mask01_h[:, :], w=(Bm01,))
        em.op("vector", "memset",
              v_aug[:, :].rearrange("p (t g c) -> p t g c", t=NT, g=8)[:, :, :, 64:65],
              1.0, w=(Bva,))

        def load_wpanel(slot, Bslot, staged_h, pnl, ncols):
            em.dma("qgpsimd", slot[:, 0:DC * ncols], staged_h[pnl],
                   w=(Bslot,))

        bankctr = [0]

        def pingbank():
            bankctr[0] = (bankctr[0] + 1) % 3
            return 4 + bankctr[0]

        for pnl in range(4):
            sl = pnl % 2
            load_wpanel(wsl[sl], Bwsl[sl], wkTs, pnl, 128)
            for gg in range(2):
                g = pnl * 2 + gg
                for tc in range(4):
                    bk = pingbank()
                    em.mm_group(
                        pb[bk][0:64, 0:512],
                        [(wsl[sl][:, dc * 128 + gg * 64: dc * 128 + gg * 64 + 64],
                          xn_ft[:, dc * T + tc * 512:dc * T + (tc + 1) * 512])
                         for dc in range(DC)],
                        r=(Bwsl[sl], Bxn), w=(Bpb[bk],))
                    em.op("scalar", "copy",
                          k64[:, g * T + tc * 512:g * T + (tc + 1) * 512],
                          pb[bk][0:64, 0:512], r=(Bpb[bk],), w=(Bk64,))
        for pnl in range(4):
            sl = pnl % 2
            load_wpanel(wsl[sl], Bwsl[sl], wvTs, pnl, 128)
            for tt in range(NT):
                bk = pingbank()
                em.mm_group(
                    pb[bk][0:128, 0:128],
                    [(xn_ft[:, dc * T + tt * 128:dc * T + (tt + 1) * 128],
                      wsl[sl][:, dc * 128:(dc + 1) * 128]) for dc in range(DC)],
                    r=(Bwsl[sl], Bxn), w=(Bpb[bk],))
                for gg in range(2):
                    g = pnl * 2 + gg
                    em.op("scalar", "copy",
                          v_aug[:, tt * 520 + g * 65: tt * 520 + g * 65 + 64],
                          pb[bk][0:128, gg * 64:gg * 64 + 64],
                          r=(Bpb[bk],), w=(Bva,))

        for qc in range(4 if sub >= 2 else 0):
            for pnl in range(16):
                sl = pnl % 2
                load_wpanel(wsl[sl], Bwsl[sl], wqTs, pnl, 128)
                for gg in range(2):
                    h = pnl * 2 + gg
                    bk = pingbank()
                    em.mm_group(
                        pb[bk][0:64, 0:512],
                        [(wsl[sl][:, dc * 128 + gg * 64: dc * 128 + gg * 64 + 64],
                          xn_ft[:, dc * T + qc * 512:dc * T + (qc + 1) * 512])
                         for dc in range(DC)],
                        r=(Bwsl[sl], Bxn), w=(Bpb[bk],))
                    em.op("vector", "tensor_scalar",
                          q64[:, h * 512:(h + 1) * 512],
                          pb[bk][0:64, 0:512], L0_SCALE, None, ALU.mult,
                          r=(Bpb[bk],), w=(Bq64,))
            nkt = 4 * qc + 4
            for hp in range(L0_HQ // 2 if sub >= 3 else 0):
                hA, hB = 2 * hp, 2 * hp + 1
                g = hA // 4
                pvA, pvB = 5, 6

                def score(h, kt, ptb, Bptb):
                    bk = scj[0] % 4
                    scj[0] += 1
                    em.op("tensor", "matmul", pb[bk][0:128, 0:512],
                          k64[:, g * T + kt * 128:g * T + (kt + 1) * 128],
                          q64[:, h * 512:(h + 1) * 512],
                          start=True, stop=True,
                          r=(Bk64, Bq64), w=(Bpb[bk],))
                    em.op("scalar", "activation", ptb[:, :],
                          pb[bk][0:128, 0:512], AF.Exp, bias=c_neg20,
                          r=(Bpb[bk], Bconst), w=(Bptb,))
                    if kt >= 4 * qc:
                        i = kt - 4 * qc
                        em.op("vector", "tensor_tensor", ptb[:, :],
                              ptb[:, :], mask01[:, i * 512:(i + 1) * 512],
                              ALU.mult, r=(Bptb, Bm01), w=(Bptb,))

                def pv(kt, pvb, ptb, Bptb):
                    em.op("tensor", "matmul", pb[pvb][0:65, 0:512],
                          v_aug[:, kt * 520 + g * 65: kt * 520 + (g + 1) * 65],
                          ptb[:, :], start=(kt == 0), stop=(kt == nkt - 1),
                          r=(Bva, Bptb), w=(Bpb[pvb],))

                for kt in range(nkt):
                    score(hA, kt, ptl[kt % 3], Bpt[kt % 3])
                    score(hB, kt, ptl[3 + kt % 3], Bpt[3 + kt % 3])
                    if kt > 1:
                        pv(kt - 2, pvA, ptl[(kt - 2) % 3], Bpt[(kt - 2) % 3])
                        pv(kt - 2, pvB, ptl[3 + (kt - 2) % 3],
                           Bpt[3 + (kt - 2) % 3])
                for ktl in (nkt - 2, nkt - 1):
                    pv(ktl, pvA, ptl[ktl % 3], Bpt[ktl % 3])
                    pv(ktl, pvB, ptl[3 + ktl % 3], Bpt[3 + ktl % 3])

                for (h, pvb, rn_, Brn_, at_, Bat_) in (
                        (hA, pvA, rnrow[0], Brn[0], atmp[0], Batmp[0]),
                        (hB, pvB, rnrow[1], Brn[1], atmp[1], Batmp[1])):
                    em.op("vector", "reciprocal", rn_[0:1, :],
                          pb[pvb][64:65, 0:512], r=(Bpb[pvb],), w=(Brn_,))
                    em.op("tensor", "matmul", pb[4][0:64, 0:512], ones_row64,
                          rn_[0:1, :], start=True, stop=True,
                          r=(Brn_, Bconst), w=(Bpb[4],))
                    em.op("scalar", "copy", at_[:, :], pb[pvb][0:64, 0:512],
                          r=(Bpb[pvb],), w=(Bat_,))
                    if h % 2 == 0:
                        em.op("vector", "tensor_tensor",
                              attn_ft[0:64, (h // 2) * 512:(h // 2 + 1) * 512],
                              at_[:, :], pb[4][0:64, 0:512], ALU.mult,
                              r=(Bat_, Bpb[4]), w=(Baf,))
                    else:
                        ssl = (h // 2) % 2
                        em.op("vector", "tensor_tensor", stg[ssl][:, :],
                              at_[:, :], pb[4][0:64, 0:512], ALU.mult,
                              r=(Bat_, Bpb[4]), w=(Bstg[ssl],))
                        em.dma("qsync",
                               attn_ft[64:128,
                                       (h // 2) * 512:(h // 2 + 1) * 512],
                               stg[ssl][:, :], r=(Bstg[ssl],), w=(Baf,))
            for np_ in range((16 if sub >= 5 else 2) if sub >= 4 else 0):
                sl = np_ % 2
                load_wpanel(wosl[sl], Bwo[sl], woTs, np_, 128)
                for ts in range(4):
                    bk = pingbank()
                    st_i = (np_ * 4 + ts) % 4
                    em.mm_group(
                        pb[bk][0:128, 0:128],
                        [(attn_ft[:, dc * 512 + ts * 128:dc * 512 + (ts + 1) * 128],
                          wosl[sl][:, dc * 128:(dc + 1) * 128]) for dc in range(DC)],
                        r=(Bwo[sl], Baf), w=(Bpb[bk],))
                    em.op("scalar", "copy", s0st[st_i][:, 0:128],
                          pb[bk][0:128, 0:128], r=(Bpb[bk],), w=(Bs0l[st_i],))
                    em.dma("qsync",
                           s_scr[qc * 512 + ts * 128: qc * 512 + (ts + 1) * 128,
                                 np_ * 128:(np_ + 1) * 128],
                           s0st[st_i][:, 0:128], r=(Bs0l[st_i],), w=(Bs_scr,))
        arena.reset()

    def phase_p2():
        em.tag = "p2"
        arena.reset()
        w0sb, Bw0 = arena.alloc("w0sb", [128, DC * D], BF16)
        scl, Bscl = arena.alloc("scl", [128, D], F32)
        o_, Bo, o2, Bo2, s_, Bs, x_, Bx = [], [], [], [], [], [], [], []
        stt, Bstt = [], []
        for i in range(2):
            t_, b_ = arena.alloc("p2o", [128, D], F32); o_.append(t_); Bo.append(b_)
            t_, b_ = arena.alloc("p2o2", [128, D], F32); o2.append(t_); Bo2.append(b_)
            t_, b_ = arena.alloc("p2s", [128, D], F32); s_.append(t_); Bs.append(b_)
            t_, b_ = arena.alloc("p2x", [128, D], F32); x_.append(t_); Bx.append(b_)
            t_, b_ = arena.alloc("p2st", [128, 24], F32); stt.append(t_); Bstt.append(b_)

        em.dma("qsync", scl[:, :], sclbc[:, :], w=(Bscl,))
        for ch in range(DC):
            stg0 = x_[ch % 2]
            Bstg0 = Bx[ch % 2]
            em.dma("qsync", stg0[:, :], w0_h[ch * 128:(ch + 1) * 128, :],
                   w=(Bstg0,))
            if ch % 2 == 0:
                em.op("vector", "tensor_copy", w0sb[:, ch * D:(ch + 1) * D],
                      stg0[:, :], r=(Bstg0,), w=(Bw0,))
            else:
                em.op("scalar", "copy", w0sb[:, ch * D:(ch + 1) * D],
                      stg0[:, :], r=(Bstg0,), w=(Bw0,))
        def p2_chain(tt):
            sl = tt % 2
            stp = stt[sl]
            Bst = Bstt[sl]
            ss = stp[:, 0:1]; ssm = stp[:, 1:2]; lnv = stp[:, 2:3]
            r0 = stp[:, 3:4]; e1 = stp[:, 4:5]; e2 = stp[:, 5:6]
            hh = stp[:, 6:7]; rv = stp[:, 7:8]; vn = stp[:, 8:9]
            e2x = stp[:, 9:10]; dd = stp[:, 10:11]; rd_ = stp[:, 11:12]
            th = stp[:, 12:13]; dn = stp[:, 13:14]; rdn = stp[:, 14:15]
            f1 = stp[:, 15:16]; onrm = stp[:, 16:17]; ron = stp[:, 17:18]
            cc = stp[:, 18:19]; mlt = stp[:, 19:20]
            em.op("scalar", "activation", o2[sl][:, :], o_[sl][:, :], AF.Square,
                  r=(Bo[sl],), w=(Bo2[sl],))
            em.op("vector", "reduce_sum", ss, o2[sl][:, :], axis=AX.X,
                  r=(Bo2[sl],), w=(Bst,))
            v = lambda fn, *a, **k: em.op("vector", fn, *a, r=(Bst,), w=(Bst,), **k)
            v("tensor_scalar", ssm, ss, 1e-14, None, ALU.max)
            em.op("scalar", "activation", lnv, ssm, AF.Ln, r=(Bst,), w=(Bst,))
            em.op("scalar", "activation", r0, lnv, AF.Exp, scale=-0.5,
                  r=(Bst,), w=(Bst,))
            v("tensor_tensor", e1, ssm, r0, ALU.mult)
            v("tensor_tensor", e2, e1, r0, ALU.mult)
            v("tensor_scalar", hh, e2, -0.5, 1.5, ALU.mult, ALU.add)
            v("tensor_tensor", rv, r0, hh, ALU.mult)
            v("tensor_tensor", vn, ssm, rv, ALU.mult)
            em.op("scalar", "activation", e2x, vn, AF.Exp, scale=2.0,
                  r=(Bst,), w=(Bst,))
            v("tensor_scalar", dd, e2x, 1.0, None, ALU.add)
            em.op("vector", "reciprocal", rd_, dd, r=(Bst,), w=(Bst,))
            v("tensor_scalar", th, rd_, -2.0, 1.0, ALU.mult, ALU.add)
            v("tensor_scalar", dn, vn, 1e-7, None, ALU.add)
            em.op("vector", "reciprocal", rdn, dn, r=(Bst,), w=(Bst,))
            v("tensor_tensor", f1, th, rdn, ALU.mult)
            v("tensor_tensor", onrm, f1, vn, ALU.mult)
            v("tensor_scalar", onrm, onrm, 1e-7, None, ALU.max)
            em.op("vector", "reciprocal", ron, onrm, r=(Bst,), w=(Bst,))
            v("tensor_scalar", cc, ron, 0.99, None, ALU.mult)
            v("tensor_scalar", cc, cc, 1.0, None, ALU.min)
            v("tensor_tensor", mlt, f1, cc, ALU.mult)
            em.op("vector", "tensor_scalar", o2[sl][:, :], o_[sl][:, :], mlt,
                  None, ALU.mult, r=(Bo[sl], Bst), w=(Bo2[sl],))
            em.op("scalar", "activation", o_[sl][:, :], o2[sl][:, :], AF.Silu,
                  r=(Bo2[sl],), w=(Bo[sl],))
            em.op("gpsimd", "tensor_tensor", s_[sl][:, :], s_[sl][:, :],
                  o_[sl][:, :], ALU.mult, r=(Bs[sl], Bo[sl]), w=(Bs[sl],))
            em.op("vector", "tensor_tensor", x_[sl][:, :], x_[sl][:, :],
                  s_[sl][:, :], ALU.add, r=(Bx[sl], Bs[sl]), w=(Bx[sl],))
            em.dma("qsync", x1_scr[tt * 128:(tt + 1) * 128, :], x_[sl][:, :],
                   r=(Bx[sl],), w=(Bx1_scr,))

        for tt in range(NT):
            sl = tt % 2
            em.dma("qsync", s_[sl][:, :], s_scr[tt * 128:(tt + 1) * 128, :],
                   r=(Bs_scr,), w=(Bs[sl],))
            em.dma("qsync", x_[sl][:, :], x_h[tt * 128:(tt + 1) * 128, :],
                   w=(Bx[sl],))
            for kc in range(4):
                bank = (tt * 4 + kc) % 4
                em.mm_group(
                    pb[bank][0:128, 0:512],
                    [(xn_ft[:, dc * T + tt * 128:dc * T + (tt + 1) * 128],
                      w0sb[:, dc * D + kc * 512:dc * D + kc * 512 + 512])
                     for dc in range(DC)],
                    r=(Bxn, Bw0), w=(Bpb[bank],))
                em.op("vector", "tensor_tensor",
                      o_[sl][:, kc * 512:(kc + 1) * 512],
                      pb[bank][0:128, 0:512], scl[:, kc * 512:(kc + 1) * 512],
                      ALU.mult, r=(Bpb[bank], Bscl), w=(Bo[sl],))
            if tt > 0:
                p2_chain(tt - 1)
        p2_chain(NT - 1)
        arena.reset()

    def phase_p4():
        em.tag = "p4"
        arena.reset()
        kv_sb, Bkv = arena.alloc("kv_sb", [128, D], F32)
        kv_bf, Bkvb = arena.alloc("kv_bf", [128, D], BF16)
        keep = arena.off
        wk_p, Bwk, wv_p, Bwv, kt_, Bkt, vt_, Bvt = [], [], [], [], [], [], [], []
        for i in range(2):
            t_, b_ = arena.alloc("wk_p", [128, DC * 256], BF16); wk_p.append(t_); Bwk.append(b_)
            t_, b_ = arena.alloc("wv_p", [128, DC * 256], BF16); wv_p.append(t_); Bwv.append(b_)
            t_, b_ = arena.alloc("ktile", [128, 256], BF16); kt_.append(t_); Bkt.append(b_)
            t_, b_ = arena.alloc("vtile", [128, 256], BF16); vt_.append(t_); Bvt.append(b_)

        gz4, Bgz4, gst4, Bgst4 = [], [], [], []
        for par in range(2):
            z5, Bz5 = [], []
            for i in range(5):
                t_, b_ = arena.alloc("gz4", [128, 8], F32)
                z5.append(t_); Bz5.append(b_)
            gz4.append(z5); Bgz4.append(Bz5)
            t_, b_ = arena.alloc("gst4", [128, 32], F32)
            gst4.append(t_); Bgst4.append(b_)

        em.op("vector", "memset", kv_sb[:, :], 0.0, w=(Bkv,))

        def load_np(slot, Bslot, staged_h, pnl, ncols):
            em.dma("qgpsimd", slot[:, 0:DC * ncols], staged_h[pnl],
                   w=(Bslot,))

        def emit_gate(gti):
            par = gti % 2
            em.mm_group(
                pb[6][0:128, 0:8],
                [(xn_ft[:, d2 * T + gti * 128: d2 * T + (gti + 1) * 128],
                  gw_sb[:, d2 * 8:(d2 + 1) * 8]) for d2 in range(DC)],
                r=(Bxn, Bconst), w=(Bpb[6],))
            gate_tile(gti, pb[6], Bpb[6], gz4[par], Bgz4[par],
                      gst4[par], Bgst4[par])

        def kv_op(pidx):
            nc8p = pidx // 16
            slp = pidx % 2
            ob = 4 + pidx % 2
            for hh in range(2):
                em.op("tensor", "matmul",
                      pb[ob][0:128, hh * 128:(hh + 1) * 128],
                      kt_[slp][:, hh * 128:(hh + 1) * 128],
                      vt_[slp][:, hh * 128:(hh + 1) * 128],
                      start=True, stop=True,
                      r=(Bkt[slp], Bvt[slp]), w=(Bpb[ob],))
            em.op("vector", "tensor_tensor",
                  kv_sb[:, nc8p * 256:(nc8p + 1) * 256],
                  kv_sb[:, nc8p * 256:(nc8p + 1) * 256],
                  pb[ob][0:128, 0:256], ALU.add,
                  r=(Bkv, Bpb[ob]), w=(Bkv,))

        for idx in range(128):
            nc8, tt = idx // 16, idx % 16
            sl = idx % 2
            wsl_ = nc8 % 2
            if tt == 0:
                load_np(wk_p[wsl_], Bwk[wsl_], Wks, nc8, 256)
                load_np(wv_p[wsl_], Bwv[wsl_], Wvs, nc8, 256)
            em.mm_group(
                pb[sl][0:128, 0:256],
                [(xn_ft[:, dc * T + tt * 128:dc * T + (tt + 1) * 128],
                  wk_p[wsl_][:, dc * 256:(dc + 1) * 256]) for dc in range(DC)],
                r=(Bxn, Bwk[wsl_]), w=(Bpb[sl],))
            em.op("vector", "tensor_scalar", kt_[sl][:, :],
                  pb[sl][0:128, 0:256], 0.0, L1_RSCALE, ALU.max, ALU.mult,
                  r=(Bpb[sl],), w=(Bkt[sl],))
            em.mm_group(
                pb[2 + sl][0:128, 0:256],
                [(xn_ft[:, dc * T + tt * 128:dc * T + (tt + 1) * 128],
                  wv_p[wsl_][:, dc * 256:(dc + 1) * 256]) for dc in range(DC)],
                r=(Bxn, Bwv[wsl_]), w=(Bpb[2 + sl],))
            em.op("scalar", "copy", vt_[sl][:, :], pb[2 + sl][0:128, 0:256],
                  r=(Bpb[2 + sl],), w=(Bvt[sl],))
            if idx > 0:
                kv_op(idx - 1)
            if idx % 8 == 7:
                emit_gate(idx // 8)
        kv_op(127)
        em.op("scalar", "copy", kv_bf[:, :], kv_sb[:, :], r=(Bkv,), w=(Bkvb,))

        arena.reset(keep)
        q_sb, _Bq0 = arena.alloc("q_sb", [128, L1_H * 512], BF16)
        u_sb, _Bu0 = arena.alloc("u_sb", [128, L1_H * 512], BF16)
        Bq = [em.buf(f"qsl{i}", fence=[_Bq0]) for i in range(L1_H)]
        Bu = [em.buf(f"usl{i}", fence=[_Bu0]) for i in range(L1_H)]
        out_sb, Bo4 = arena.alloc("out_sb", [128, L1_H * 512], BF16)
        z_sb, Bz = arena.alloc("z_sb", [128, DC * 512], BF16)
        wp, Bwp = [], []
        for i in range(3):
            t_, b_ = arena.alloc("wp4", [128, DC * 512], BF16); wp.append(t_); Bwp.append(b_)
        sqs, Bsq = [], []
        for i in range(2):
            t_, b_ = arena.alloc("sqs", [128, 512], F32)
            sqs.append(t_); Bsq.append(b_)
        s1l, Bs1 = [], []
        for i in range(2):
            t_, b_ = arena.alloc("s1st", [128, 512], F32); s1l.append(t_); Bs1.append(b_)
        rn4, Brn4 = arena.alloc("rn4", [1, 512], F32)
        r1 = st2[0:1, 0:512]; r2 = st2[0:1, 512:1024]

        for qc in range(4):
            def head_step(h):
                if h < L1_H:
                    bank = (2, 3, 6)[h % 3]
                    em.op("tensor", "matmul", pb[bank][0:128, 0:512],
                          kv_bf[:, h * 128:(h + 1) * 128],
                          q_sb[:, h * 512:(h + 1) * 512], start=True, stop=True,
                          r=(Bkvb, Bq[h]), w=(Bpb[bank],))
                    em.op("scalar", "activation", sqs[h % 2][:, :],
                          pb[bank][0:128, 0:512],
                          AF.Square, r=(Bpb[bank],), w=(Bsq[h % 2],))
                    em.op("scalar", "copy", out_sb[:, h * 512:(h + 1) * 512],
                          pb[bank][0:128, 0:512], r=(Bpb[bank],), w=(Bo4,))
                if 1 <= h < L1_H + 1:
                    h1 = h - 1
                    nb = 4 + h1 % 2
                    r1h = st2[0:1, (h1 % 2) * 512:(h1 % 2) * 512 + 512]
                    em.op("tensor", "matmul", pb[nb][0:1, 0:512], ones128,
                          sqs[h1 % 2][:, :], start=True, stop=True,
                          r=(Bsq[h1 % 2], Bconst), w=(Bpb[nb],))
                    em.op("vector", "tensor_scalar", r1h, pb[nb][0:1, 0:512],
                          1.0 / L1_DH, 1e-24, ALU.mult, ALU.max,
                          r=(Bpb[nb],), w=(Bst2,))
                    em.op("scalar", "activation", r1h, r1h, AF.Ln,
                          r=(Bst2,), w=(Bst2,))
                    em.op("scalar", "activation", r1h, r1h, AF.Exp, scale=-0.5,
                          r=(Bst2,), w=(Bst2,))
                if 2 <= h:
                    h2 = h - 2
                    nb2 = 4 + h2 % 2
                    r1h2 = st2[0:1, (h2 % 2) * 512:(h2 % 2) * 512 + 512]
                    em.op("tensor", "matmul", pb[nb2][0:128, 0:512],
                          ones_row128, r1h2, start=True, stop=True,
                          r=(Bst2, Bconst), w=(Bpb[nb2],))
                    em.op("vector", "tensor_tensor",
                          z_sb[:, h2 * 512:(h2 + 1) * 512],
                          out_sb[:, h2 * 512:(h2 + 1) * 512],
                          u_sb[:, h2 * 512:(h2 + 1) * 512], ALU.mult,
                          r=(Bo4, Bu[h2]), w=(Bz,))
                    em.op("vector", "tensor_tensor",
                          z_sb[:, h2 * 512:(h2 + 1) * 512],
                          z_sb[:, h2 * 512:(h2 + 1) * 512],
                          pb[nb2][0:128, 0:512],
                          ALU.mult, r=(Bz, Bpb[nb2]), w=(Bz,))

            step = [0]
            for (W_s, dstsb, Bdst, act) in ((Wqs, q_sb, Bq, "relu"),
                                            (Wus, u_sb, Bu, "silu")):
                for np8 in range(8):
                    sl = np8 % 3
                    load_np(wp[sl], Bwp[sl], W_s, np8, 256)
                    for sub in range(2):
                        nt = np8 * 2 + sub
                        bank = nt % 2
                        em.mm_group(
                            pb[bank][0:128, 0:512],
                            [(wp[sl][:, dc * 256 + sub * 128:dc * 256 + sub * 128 + 128],
                              xn_ft[:, dc * T + qc * 512:dc * T + (qc + 1) * 512])
                             for dc in range(DC)],
                            r=(Bwp[sl], Bxn), w=(Bpb[bank],))
                        if act == "relu":
                            em.op("vector", "tensor_scalar",
                                  dstsb[:, nt * 512:(nt + 1) * 512],
                                  pb[bank][0:128, 0:512], 0.0, L1_RSCALE,
                                  ALU.max, ALU.mult,
                                  r=(Bpb[bank],), w=(Bdst[nt],))
                        else:
                            em.op("scalar", "activation",
                                  dstsb[:, nt * 512:(nt + 1) * 512],
                                  pb[bank][0:128, 0:512], AF.Silu,
                                  r=(Bpb[bank],), w=(Bdst[nt],))
            while step[0] < L1_H + 2:
                head_step(step[0]); step[0] += 1
            for np8 in range(4):
                sl = np8 % 3
                load_np(wp[sl], Bwp[sl], Wos, np8, 512)
                for ts in range(4):
                    bank = 6 if ts % 2 == 0 else 3
                    ssl = ts % 2
                    em.mm_group(
                        pb[bank][0:128, 0:512],
                        [(z_sb[:, dc * 512 + ts * 128:dc * 512 + ts * 128 + 128],
                          wp[sl][:, dc * 512:(dc + 1) * 512]) for dc in range(DC)],
                        r=(Bwp[sl], Bz), w=(Bpb[bank],))
                    em.op("scalar", "copy", s1l[ssl][:, :],
                          pb[bank][0:128, 0:512], r=(Bpb[bank],), w=(Bs1[ssl],))
                    em.dma("qsync",
                           s_scr[qc * 512 + ts * 128:qc * 512 + (ts + 1) * 128,
                                 np8 * 512:(np8 + 1) * 512],
                           s1l[ssl][:, :], r=(Bs1[ssl],), w=(Bs_scr,))
        arena.reset()

    def phase_p5():
        em.tag = "p5"
        arena.reset()
        H_sb, BH = arena.alloc("H_sb", [128, NHC * T], BF16)
        w13, Bw13 = [], []
        for i in range(4):
            t_, b_ = arena.alloc("w13", [128, DC * 128], BF16); w13.append(t_); Bw13.append(b_)
        w2sl, Bw2 = [], []
        for i in range(2):
            t_, b_ = arena.alloc("w2sl", [128, NHC * 512], BF16); w2sl.append(t_); Bw2.append(b_)
        B_sb, BB = arena.alloc("B_sb", [128, T], F32)
        NACC = 3
        acci, Bai, acco, Bao = [], [], [], []
        for i in range(NACC):
            t_, b_ = arena.alloc("acci", [128, 512], F32); acci.append(t_); Bai.append(b_)
            t_, b_ = arena.alloc("acco", [128, 512], F32); acco.append(t_); Bao.append(b_)
        a_sc, Ba = arena.alloc("a_sc", [128, 512], F32)
        sl_sc, Bsl = arena.alloc("sl_sc", [128, 512], F32)
        grows, Bgr = arena.alloc("grows", [8, T], F32)
        oh_sb, Boh = arena.alloc("oh_sb", [8, 8 * 128], F32)

        em.dma("qsync", oh_sb[:, :], oh_h[:, :], w=(Boh,))
        for tt in range(NT):
            em.op("tensor", "transpose", pb[0][0:8, 0:128],
                  gates[:, tt * 8:(tt + 1) * 8], idf[:, :],
                  r=(Bgates, Bconst), w=(Bpb[0],))
            em.op("vector", "tensor_copy", grows[0:8, tt * 128:(tt + 1) * 128],
                  pb[0][0:8, 0:128], r=(Bpb[0],), w=(Bgr,))

        experts = [("shared", NEXP)] + [(e, e) for e in range(NEXP)]
        for (e, ei) in experts:
            shared = (e == "shared")
            if not shared:
                for tc in range(4):
                    em.op("tensor", "matmul", pb[6][0:128, 0:512],
                          oh_sb[:, e * 128:(e + 1) * 128],
                          grows[0:8, tc * 512:(tc + 1) * 512],
                          start=True, stop=True, r=(Boh, Bgr), w=(Bpb[6],))
                    em.op("vector", "tensor_copy",
                          B_sb[:, tc * 512:(tc + 1) * 512],
                          pb[6][0:128, 0:512], r=(Bpb[6],), w=(BB,))
            for hc in range(NHC):
                s1_ = hc % 2
                s3_ = 2 + hc % 2
                em.dma("qgpsimd", w13[s1_][:, :], w1s_h[ei, hc],
                       w=(Bw13[s1_],))
                em.dma("qgpsimd", w13[s3_][:, :], w3s_h[ei, hc],
                       w=(Bw13[s3_],))
                for tc in range(4):
                    b1 = tc % 2
                    b3 = 2 + tc % 2
                    em.mm_group(
                        pb[b1][0:128, 0:512],
                        [(w13[s1_][:, dc * 128:(dc + 1) * 128],
                          xn_ft[:, dc * T + tc * 512:dc * T + (tc + 1) * 512])
                         for dc in range(DC)],
                        r=(Bw13[s1_], Bxn), w=(Bpb[b1],))
                    em.mm_group(
                        pb[b3][0:128, 0:512],
                        [(w13[s3_][:, dc * 128:(dc + 1) * 128],
                          xn_ft[:, dc * T + tc * 512:dc * T + (tc + 1) * 512])
                         for dc in range(DC)],
                        r=(Bw13[s3_], Bxn), w=(Bpb[b3],))
                    if shared:
                        em.op("scalar", "activation", sl_sc[:, :],
                              pb[b1][0:128, 0:512], AF.Silu,
                              r=(Bpb[b1],), w=(Bsl,))
                    else:
                        em.op("vector", "tensor_tensor", a_sc[:, :],
                              pb[b1][0:128, 0:512],
                              B_sb[:, tc * 512:(tc + 1) * 512], ALU.mult,
                              r=(Bpb[b1], BB), w=(Ba,))
                        em.op("scalar", "activation", sl_sc[:, :], a_sc[:, :],
                              AF.Silu, r=(Ba,), w=(Bsl,))
                    em.op("vector", "tensor_tensor",
                          H_sb[:, hc * T + tc * 512:hc * T + (tc + 1) * 512],
                          sl_sc[:, :], pb[b3][0:128, 0:512], ALU.mult,
                          r=(Bsl, Bpb[b3]), w=(BH,))
            w2banks = (3, 4, 5, 6)
            iters = [(nc4, ts) for nc4 in range(4) for ts in range(NT)]
            last = (not shared) and e == NEXP - 1
            if last:
                s_ch = [B_sb[:, 0:512], B_sb[:, 512:1024]]
                x1_ch = [B_sb[:, 1024:1536], B_sb[:, 1536:2048]]
                Bs_ch = [em.buf(f"lsch{i}", fence=[BB]) for i in range(2)]
                Bx1_ch = [em.buf(f"lx1ch{i}", fence=[BB]) for i in range(2)]
                fgt = [a_sc, sl_sc]
                Bfgt = [em.buf("lfg0", fence=[Ba]),
                        em.buf("lfg1", fence=[Bsl])]

                def sx_load(i):
                    if i >= 64:
                        return
                    nc4i, tsi = iters[i]
                    s2 = i % 2
                    em.dma("qsync", s_ch[s2],
                           s_scr[tsi * 128:(tsi + 1) * 128,
                                 nc4i * 512:(nc4i + 1) * 512],
                           r=(Bs_scr,), w=(Bs_ch[s2],))
                    em.dma("qsync", x1_ch[s2],
                           x1_scr[tsi * 128:(tsi + 1) * 128,
                                  nc4i * 512:(nc4i + 1) * 512],
                           r=(Bx1_scr,), w=(Bx1_ch[s2],))
                sx_load(0)
                sx_load(1)

            def acci_load(i):
                if shared or i >= 64:
                    return
                nc4i, tsi = iters[i]
                sli = i % NACC
                em.dma("qsync", acci[sli][:, :],
                       acc_scr[tsi * 128:(tsi + 1) * 128,
                               nc4i * 512:(nc4i + 1) * 512],
                       r=(Bacc_scr[nc4i * 16 + tsi],), w=(Bai[sli],))

            if not shared:
                for j in range(NACC):
                    acci_load(j)
            for i, (nc4, ts) in enumerate(iters):
                if i % 16 == 0:
                    sw = nc4 % 2
                    em.dma("qgpsimd", w2sl[sw][:, :], w2s_h[ei, nc4],
                           w=(Bw2[sw],))
                sl = i % NACC
                bank = w2banks[i % 4]
                swc = nc4 % 2
                em.mm_group(
                    pb[bank][0:128, 0:512],
                    [(H_sb[:, hc * T + ts * 128:hc * T + ts * 128 + 128],
                      w2sl[swc][:, hc * 512:(hc + 1) * 512])
                     for hc in range(NHC)],
                    r=(BH, Bw2[swc]), w=(Bpb[bank],))
                if shared:
                    em.op("scalar", "copy", acco[sl][:, :],
                          pb[bank][0:128, 0:512], r=(Bpb[bank],),
                          w=(Bao[sl],))
                else:
                    em.op("vector", "scalar_tensor_tensor", acco[sl][:, :],
                          pb[bank][0:128, 0:512],
                          gates[:, ts * 8 + e: ts * 8 + e + 1],
                          acci[sl][:, :], ALU.mult, ALU.add,
                          r=(Bpb[bank], Bgates, Bai[sl]), w=(Bao[sl],))
                if last:
                    s2 = i % 2
                    em.op("scalar", "activation", fgt[s2][:, :],
                          acco[sl][:, :], AF.Silu,
                          r=(Bao[sl],), w=(Bfgt[s2],))
                    em.op("vector", "tensor_tensor", fgt[s2][:, :],
                          fgt[s2][:, :], s_ch[s2], ALU.mult,
                          r=(Bfgt[s2], Bs_ch[s2]), w=(Bfgt[s2],))
                    em.op("vector", "tensor_tensor", x1_ch[s2], x1_ch[s2],
                          fgt[s2][:, :], ALU.add,
                          r=(Bx1_ch[s2], Bfgt[s2]), w=(Bx1_ch[s2],))
                    em.dma("qsync",
                           y_h[ts * 128:(ts + 1) * 128,
                               nc4 * 512:(nc4 + 1) * 512],
                           x1_ch[s2], r=(Bx1_ch[s2],))
                    sx_load(i + 2)
                else:
                    em.dma("qsync",
                           acc_scr[ts * 128:(ts + 1) * 128,
                                   nc4 * 512:(nc4 + 1) * 512],
                           acco[sl][:, :], r=(Bao[sl],),
                           w=(Bacc_scr[nc4 * 16 + ts],))
                acci_load(i + NACC)


    if "ln0" in phases:
        phase_ln(x_h, None, layer=0,
                 tap="xnd0" if "xnd0" in debug_taps else None)
    if "p1" in phases:
        import os as _os
        phase_p1(sub=int(_os.environ.get("P1SUB", "5")))
    if "p2" in phases:
        phase_p2()
    if "ln1" in phases:
        phase_ln(x1_scr, Bx1_scr, layer=1,
                 tap="xnd1" if "xnd1" in debug_taps else None)
    if "p4" in phases:
        phase_p4()
    if "p5" in phases:
        phase_p5()
    if "p5" not in phases:
        Bdummy = em.buf("ydummy")
        for tt in range(NT):
            pass

    from contextlib import ExitStack
    es = ExitStack()
    sems = {}
    for s in list(ENGS) + list(QSEMS):
        sems[s] = es.enter_context(nc.semaphore(f"sem_{s}"))
    em.finalize(nc, sems)
    es.close()
    return nc


def _host_consts():
    cst = np.zeros((128, 160), np.float32)
    cst[:, 0] = -20.0
    cst[:, 1] = 1.0
    cst[:, 8:136] = 1.0
    idb = np.eye(128, dtype=ml_dtypes.bfloat16)
    idf = np.eye(128, dtype=np.float32)
    oh = np.zeros((8, 8 * 128), np.float32)
    for e in range(8):
        oh[e, e * 128:(e + 1) * 128] = 1.0
    mask01 = np.zeros((128, 4 * 512), ml_dtypes.bfloat16)
    kk = np.arange(128)[:, None]
    q = np.arange(512)[None, :]
    for i in range(4):
        mask01[:, i * 512:(i + 1) * 512] = ((128 * i + kk) <= q).astype(
            ml_dtypes.bfloat16)
    return cst, idb, idf, oh, mask01


def _stage_panels(mT, ncols):
    Dn, X = mT.shape
    npan = X // ncols
    return np.ascontiguousarray(
        mT.reshape(16, 128, npan, ncols).transpose(2, 1, 0, 3)
          .reshape(npan, 128, 16 * ncols))


def _stage_w13(w):
    return np.ascontiguousarray(
        w.reshape(16, 128, 14, 128).transpose(2, 1, 0, 3).reshape(14, 128, 2048))


def _stage_w2(w):
    return np.ascontiguousarray(
        w.reshape(14, 128, 4, 512).transpose(2, 1, 0, 3).reshape(4, 128, 14 * 512))


def host_inputs(inputs):
    f32 = lambda a: np.ascontiguousarray(np.asarray(a), dtype=np.float32)
    x = f32(inputs["L0_attn_x"])
    scl = f32(inputs["L0_ffn_scale"])
    biasE = f32(inputs["L1_ffn_expert_bias"])
    cst, idb, idf, oh, mask01 = _host_consts()
    w1 = f32(inputs["L1_ffn_w1"]); w2 = f32(inputs["L1_ffn_w2"])
    w3 = f32(inputs["L1_ffn_w3"])
    shared = {
        "wqTs": _stage_panels(f32(inputs["L0_attn_wq"]).T, 128),
        "wkTs": _stage_panels(f32(inputs["L0_attn_wk"]).T, 128),
        "wvTs": _stage_panels(f32(inputs["L0_attn_wv"]).T, 128),
        "woTs": _stage_panels(f32(inputs["L0_attn_wo"]).T, 128),
        "w0": f32(inputs["L0_ffn_weight"]),
        "sclbc": np.ascontiguousarray(np.broadcast_to(scl[None, :], (128, D))),
        "Wqs": _stage_panels(f32(inputs["L1_attn_Wq"]), 256),
        "Wks": _stage_panels(f32(inputs["L1_attn_Wk"]), 256),
        "Wvs": _stage_panels(f32(inputs["L1_attn_Wv"]), 256),
        "Wus": _stage_panels(f32(inputs["L1_attn_Wu"]), 256),
        "Wos": _stage_panels(f32(inputs["L1_attn_Wo"]), 512),
        "gw": f32(inputs["L1_ffn_gate_w"]),
        "w1s": np.stack([_stage_w13(w1[e]) for e in range(NEXP)]
                        + [_stage_w13(f32(inputs["L1_ffn_sw1"]))]),
        "w3s": np.stack([_stage_w13(w3[e]) for e in range(NEXP)]
                        + [_stage_w13(f32(inputs["L1_ffn_sw3"]))]),
        "w2s": np.stack([_stage_w2(w2[e]) for e in range(NEXP)]
                        + [_stage_w2(f32(inputs["L1_ffn_sw2"]))]),
        "biasE": np.ascontiguousarray(np.broadcast_to(biasE[None, :], (128, 8))),
        "cst": cst, "idb": idb, "idf": idf, "oh": oh, "mask01": mask01,
    }
    return x, shared


_NC_CACHE = {}


def kernel(**inputs):
    x, shared = host_inputs(inputs)
    if "nc" not in _NC_CACHE:
        _NC_CACHE["nc"] = build_program()
    nc = _NC_CACHE["nc"]
    in_maps = [dict(shared, x=np.ascontiguousarray(x[c])) for c in range(8)]
    res = run_bass_kernel_spmd(nc, in_maps, core_ids=list(range(8)))
    return np.stack([res.results[c]["y"] for c in range(8)], axis=0)


# version 11 (final)
# speedup vs baseline: 1.0483x; 1.0128x over previous
import math
import numpy as np
import ml_dtypes
import concourse.bass as bass
import concourse.mybir as mybir
from concourse.bass_utils import run_bass_kernel_spmd

F32 = mybir.dt.float32
BF16 = mybir.dt.bfloat16
AF = mybir.ActivationFunctionType
ALU = mybir.AluOpType
AX = mybir.AxisListType

T = 2048
D = 2048
NT = 16
DC = 16
HID = 1792
NHC = 14
NEXP = 8
L0_HQ, L0_HKV, L0_DH = 32, 8, 64
L0_SCALE = 0.125
L1_H, L1_DH = 16, 128
L1_RSCALE = float(np.float32(1.0) / np.float32(math.sqrt(128.0)))

QUEUES = {"qsync": "sync", "qgpsimd": "gpsimd"}
NLANES = 6
QSEMS = {f"{q}{l}": e for q, e in QUEUES.items() for l in range(NLANES)}
ENGS = ("tensor", "vector", "scalar", "gpsimd", "sync")

SBUF_BASE = 17408
SBUF_TOP = 229120


class Buf:
    __slots__ = ("name", "writer", "readers", "fence")

    def __init__(self, name, fence=None):
        self.name = name
        self.writer = None
        self.readers = []
        self.fence = fence


class _Op:
    __slots__ = ("eng", "sem", "fn", "args", "kwargs", "inc", "idx", "val",
                 "waits", "is_dma", "tag")

    def __init__(self, eng, sem, fn, args, kwargs, is_dma, tag):
        self.eng, self.sem, self.fn = eng, sem, fn
        self.args, self.kwargs = args, kwargs
        self.inc = is_dma
        self.idx = -1
        self.val = 0
        self.waits = []
        self.is_dma = is_dma
        self.tag = tag


class Emit:
    def __init__(self):
        self.ops = {e: [] for e in ENGS}
        self.counts = {s: 0 for s in list(ENGS) + list(QSEMS)}
        self.qcount = {q: 0 for q in QUEUES}
        self.last_lane_dma = {s: None for s in QSEMS}
        self.clock = {}
        self.tag = ""

    def buf(self, name, fence=None):
        return Buf(name, fence)

    def _edge(self, op, prod, is_read=False):
        if prod is None or prod is op:
            return
        if (not prod.is_dma) and prod.sem == op.eng:
            if (not is_read) or op.eng == "tensor":
                return
        key = (op.eng, prod.sem)
        if self.clock.get(key, -1) >= prod.idx:
            return
        self.clock[key] = prod.idx
        prod.inc = True
        op.waits.append(prod)

    def _record(self, eng, sem, fn, args, kwargs, reads, writes, is_dma):
        op = _Op(eng, sem, fn, args, kwargs, is_dma, self.tag)
        op.idx = self.counts[sem]
        self.counts[sem] = op.idx + 1
        for b in writes:
            if b.fence is not None:
                for ob in b.fence:
                    self._edge(op, ob.writer)
                    for r in ob.readers:
                        self._edge(op, r)
                b.fence = None
        for b in reads:
            self._edge(op, b.writer, is_read=True)
        for b in writes:
            self._edge(op, b.writer)
            for r in b.readers:
                self._edge(op, r)
        for b in reads:
            b.readers.append(op)
        for b in writes:
            b.writer = op
            b.readers = []
        self.ops[eng].append(op)
        return op

    def op(self, eng, fn, *args, r=(), w=(), **kwargs):
        return self._record(eng, eng, fn, args, kwargs, r, w, False)

    def dma(self, queue, out, in_, r=(), w=(), **kwargs):
        lane = self.qcount[queue] % NLANES
        self.qcount[queue] += 1
        sem = f"{queue}{lane}"
        eng = QUEUES[queue]
        op = self._record(eng, sem, "dma_start",
                          (), dict(out=out, in_=in_, **kwargs), r, w, True)
        prev = self.last_lane_dma[sem]
        if prev is not None and prev not in op.waits:
            key = (eng, sem)
            if self.clock.get(key, -1) < prev.idx:
                self.clock[key] = prev.idx
                op.waits.append(prev)
        self.last_lane_dma[sem] = op
        return op

    def mm_group(self, out_ap, pairs, r=(), w=()):
        n = len(pairs)
        ops = []
        for i, (lt, rh) in enumerate(pairs):
            rr = r if i == 0 else ()
            ww = w if (i == 0 or i == n - 1) else ()
            ops.append(self._record("tensor", "tensor", "matmul",
                                    (out_ap, lt, rh),
                                    dict(start=(i == 0), stop=(i == n - 1)),
                                    rr, ww, False))
        for b in r:
            for i in range(n - 1):
                if ops[i] in b.readers:
                    b.readers.remove(ops[i])
            if ops[-1] not in b.readers:
                b.readers.append(ops[-1])
        return ops[-1]

    def finalize(self, nc, sems):
        cums = {s: 0 for s in self.counts}
        for e in ENGS:
            for op in self.ops[e]:
                s = op.sem
                if op.is_dma:
                    cums[s] += 16
                    op.val = cums[s]
                else:
                    if op.inc:
                        cums[s] += 1
                    op.val = cums[s]

        def emit_stream(engobj, ename):
            for op in self.ops[ename]:
                for prod in op.waits:
                    engobj.wait_ge(sems[prod.sem], prod.val)
                inst = getattr(engobj, op.fn)(*op.args, **op.kwargs)
                if op.inc:
                    inst.then_inc(sems[op.sem], 16 if op.is_dma else 1)
            if ename == "sync":
                for q in QSEMS:
                    if cums[q] > 0:
                        engobj.wait_ge(sems[q], cums[q])

        with nc.Block() as block:
            block.tensor(lambda t: emit_stream(t, "tensor"))
            block.vector(lambda v: emit_stream(v, "vector"))
            block.scalar(lambda s: emit_stream(s, "scalar"))
            block.gpsimd(lambda g: emit_stream(g, "gpsimd"))
            block.sync(lambda s: emit_stream(s, "sync"))


class Arena:
    def __init__(self, nc, em, base, limit):
        self.nc, self.em = nc, em
        self.base, self.limit = base, limit
        self.off = base
        self.n = 0
        self.cur = []
        self.pending = []

    def alloc(self, name, shape, dtype):
        nbytes = shape[1] * (2 if dtype == BF16 else 4)
        nbytes = (nbytes + 31) & ~31
        assert self.off + nbytes <= self.limit, \
            f"SBUF arena overflow at {name}: {self.off}+{nbytes}>{self.limit}"
        t = self.nc.alloc_sbuf_tensor_at(f"{name}_{self.n}", shape, dtype,
                                         offset=self.off)
        self.n += 1
        self.off += nbytes
        b = self.em.buf(name, fence=self.pending if self.pending else None)
        self.cur.append(b)
        return t, b

    def reset(self, to=None):
        self.off = self.base if to is None else to
        self.pending = self.pending + self.cur
        self.cur = []


def build_program(debug_taps=(), phases=("ln0", "p1", "p2", "ln1", "p4", "p5")):
    nc = bass.Bass("TRN2", target_bir_lowering=False, debug=False)
    em = Emit()

    dt = lambda n, s, k="ExternalInput", d=F32: nc.dram_tensor(n, s, d, kind=k).ap()
    x_h = dt("x", [T, D])
    wqTs = dt("wqTs", [16, 128, DC * 128])
    wkTs = dt("wkTs", [4, 128, DC * 128])
    wvTs = dt("wvTs", [4, 128, DC * 128])
    woTs = dt("woTs", [16, 128, DC * 128])
    w0_h = dt("w0", [D, D])
    sclbc = dt("sclbc", [128, D])
    Wqs = dt("Wqs", [8, 128, DC * 256]); Wks = dt("Wks", [8, 128, DC * 256])
    Wvs = dt("Wvs", [8, 128, DC * 256]); Wus = dt("Wus", [8, 128, DC * 256])
    Wos = dt("Wos", [4, 128, DC * 512])
    gw_h = dt("gw", [D, 8])
    w1s_h = dt("w1s", [NEXP + 1, NHC, 128, DC * 128])
    w3s_h = dt("w3s", [NEXP + 1, NHC, 128, DC * 128])
    w2s_h = dt("w2s", [NEXP + 1, 4, 128, NHC * 512])
    biasE_h = dt("biasE", [128, 8])
    cst_h = dt("cst", [128, 160])
    idb_h = dt("idb", [128, 128], d=BF16)
    idf_h = dt("idf", [128, 128])
    oh_h = dt("oh", [8, 8 * 128])
    mask01_h = dt("mask01", [128, 4 * 512], d=BF16)
    y_h = dt("y", [T, D], k="ExternalOutput")
    kind_scr = "ExternalOutput" if "scr" in debug_taps else "Internal"
    s_scr = dt("s_scr", [T, D], k=kind_scr)
    x1_scr = dt("x1_scr", [T, D], k=kind_scr)
    acc_scr = dt("acc_scr", [T, D], k=kind_scr)
    xnd = {}
    for tap in ("xnd0", "xnd1"):
        if tap in debug_taps:
            xnd[tap] = dt(tap, [128, DC * T], k="ExternalOutput", d=BF16)

    off = SBUF_BASE

    def fixed(name, shape, dtyp):
        nonlocal off
        nbytes = shape[1] * (2 if dtyp == BF16 else 4)
        nbytes = (nbytes + 31) & ~31
        t = nc.alloc_sbuf_tensor_at(name, shape, dtyp, offset=off)
        off += nbytes
        return t

    cst = fixed("cstt", [128, 160], F32)
    idb = fixed("idbt", [128, 128], BF16)
    idf = fixed("idft", [128, 128], F32)
    biasE = fixed("biasEt", [128, 8], F32)
    gw_sb = fixed("gw_sbt", [128, DC * 8], BF16)
    gates = fixed("gatest", [128, NT * 8], F32)
    st = fixed("statst", [128, 48], F32)
    st2 = fixed("stats2t", [1, 1024], F32)
    xn_ft = fixed("xn_ftt", [128, DC * T], BF16)

    arena = Arena(nc, em, (off + 31) & ~31, SBUF_TOP)

    c_neg20 = cst[:, 0:1]
    ones128 = cst[:, 1:2]
    ones_row64 = cst[0:1, 8:72]
    ones_row128 = cst[0:1, 8:136]

    pb = [nc.alloc_psum_tensor(f"pb{i}", [128, 512], F32) for i in range(7)]
    pbT = nc.alloc_psum_tensor("pbT", [128, 1024], BF16)
    Bpb = [em.buf(f"pb{i}") for i in range(7)]
    BpbT = em.buf("pbT")

    Bconst = em.buf("const")
    Bxn = em.buf("xn_ft")
    Bgates = em.buf("gates")
    Bst = em.buf("stats")
    Bst2 = em.buf("stats2")
    Bs_scr = em.buf("s_scr")
    Bx1_scr = em.buf("x1_scr")
    Bacc_scr = [em.buf(f"acc_scr{i}") for i in range(64)]

    em.tag = "init"
    em.dma("qsync", cst[:, :], cst_h[:, :], w=(Bconst,))
    em.dma("qsync", idb[:, :], idb_h[:, :], w=(Bconst,))
    em.dma("qsync", idf[:, :], idf_h[:, :], w=(Bconst,))
    em.dma("qsync", biasE[:, :], biasE_h[:, :], w=(Bconst,))
    em.dma("qgpsimd", gw_sb[:, :].rearrange("p (c e) -> p c e", c=DC),
           gw_h[:, :].rearrange("(c p) e -> p c e", p=128), w=(Bconst,))

    def gate_tile(tt, zpb, Bzpb, gz, Bgz, gst, Bst):
        sc, e_, tmp, p_, q_ = (gz[i][:, 0:8] for i in range(5))
        Bsc, Be, Btmp, Bp, Bq = Bgz
        zb = gst[:, 0:8]
        mx = gst[:, 8:9]; nm = gst[:, 9:10]; ssum = gst[:, 10:11]
        rs = gst[:, 11:12]; m_ = gst[:, 12:13]
        m1_ap = gst[:, 16:24]; m2_ap = gst[:, 24:32]
        em.op("vector", "reduce_max", mx, zpb[0:128, 0:8], axis=AX.X,
              r=(Bzpb,), w=(Bst,))
        em.op("vector", "tensor_scalar", nm, mx, -1.0, None, ALU.mult,
              r=(Bst,), w=(Bst,))
        em.op("scalar", "activation", e_, zpb[0:128, 0:8], AF.Exp,
              bias=nm, r=(Bzpb, Bst), w=(Be,))
        em.op("vector", "reduce_sum", ssum, e_, axis=AX.X, r=(Be,), w=(Bst,))
        em.op("vector", "reciprocal", rs, ssum, r=(Bst,), w=(Bst,))
        em.op("vector", "tensor_scalar", sc, e_, rs, None, ALU.mult,
              r=(Be, Bst), w=(Bsc,))
        em.op("vector", "tensor_tensor", zb, sc, biasE[:, :], ALU.add,
              r=(Bsc, Bconst, Bst), w=(Bst,))

        def top1(dst_ap):
            em.op("vector", "reduce_max", m_, zb, axis=AX.X, r=(Bst,), w=(Bst,))
            em.op("vector", "tensor_scalar", tmp, zb, m_, None, ALU.is_ge,
                  r=(Bst,), w=(Btmp,))
            em.op("vector", "tensor_copy", e_[:, 0:1], tmp[:, 0:1],
                  r=(Btmp,), w=(Be,))
            em.op("vector", "tensor_tensor", e_[:, 1:8], tmp[:, 1:8],
                  tmp[:, 0:7], ALU.add, r=(Btmp,), w=(Be,))
            em.op("vector", "tensor_copy", p_[:, 0:2], e_[:, 0:2],
                  r=(Be,), w=(Bp,))
            em.op("vector", "tensor_tensor", p_[:, 2:8], e_[:, 2:8],
                  e_[:, 0:6], ALU.add, r=(Be,), w=(Bp,))
            em.op("vector", "tensor_copy", q_[:, 0:4], p_[:, 0:4],
                  r=(Bp,), w=(Bq,))
            em.op("vector", "tensor_tensor", q_[:, 4:8], p_[:, 4:8],
                  p_[:, 0:4], ALU.add, r=(Bp,), w=(Bq,))
            em.op("vector", "tensor_tensor", q_, q_, tmp, ALU.subtract,
                  r=(Bq, Btmp), w=(Bq,))
            em.op("vector", "tensor_scalar", q_, q_, 0.0, None, ALU.is_equal,
                  r=(Bq,), w=(Bq,))
            em.op("vector", "tensor_tensor", dst_ap, tmp, q_, ALU.mult,
                  r=(Btmp, Bq), w=(Bst,))

        top1(m1_ap)
        em.op("vector", "scalar_tensor_tensor", zb, m1_ap, -1e30, zb,
              ALU.mult, ALU.add, r=(Bst,), w=(Bst,))
        top1(m2_ap)
        em.op("vector", "tensor_tensor", m1_ap, m1_ap, m2_ap, ALU.add,
              r=(Bst,), w=(Bst,))
        em.op("vector", "tensor_tensor", gates[:, tt * 8:(tt + 1) * 8],
              sc, m1_ap, ALU.mult, r=(Bsc, Bst), w=(Bgates,))

    def phase_ln(src_h, Bsrc, layer, tap=None):
        em.tag = f"ln{layer}"
        arena.reset()
        xt, Bxt, xnt, Bxnt, sq, Bsq = [], [], [], [], [], []
        for i in range(3):
            t_, b_ = arena.alloc("ln_x", [128, D], F32); xt.append(t_); Bxt.append(b_)
        for i in range(2):
            t_, b_ = arena.alloc("ln_xn", [128, D], BF16); xnt.append(t_); Bxnt.append(b_)
            t_, b_ = arena.alloc("ln_sq", [128, D], F32); sq.append(t_); Bsq.append(b_)
        lst, Blst = [], []
        for par in range(2):
            t_, b_ = arena.alloc("ln_st", [128, 12], F32); lst.append(t_); Blst.append(b_)

        for tt in range(NT):
            sl = tt % 2
            sl3 = tt % 3
            Bst = Blst[sl]
            s_sum = lst[sl][:, 0:1]; s_ss = lst[sl][:, 1:2]
            s_m = lst[sl][:, 2:3]; s_var = lst[sl][:, 3:4]
            s_r0 = lst[sl][:, 4:5]; s_e = lst[sl][:, 5:6]
            s_e2 = lst[sl][:, 6:7]; s_h = lst[sl][:, 7:8]
            s_rstd = lst[sl][:, 8:9]
            rd = (Bsrc,) if Bsrc else ()
            em.dma("qsync", xt[sl3][:, :], src_h[tt * 128:(tt + 1) * 128, :],
                   r=rd, w=(Bxt[sl3],))
            em.op("vector", "reduce_sum", s_sum, xt[sl3][:, :], axis=AX.X,
                  r=(Bxt[sl3],), w=(Bst,))
            em.op("scalar", "activation", sq[sl][:, :], xt[sl3][:, :],
                  AF.Square, r=(Bxt[sl3],), w=(Bsq[sl],))
            em.op("vector", "reduce_sum", s_ss, sq[sl][:, :], axis=AX.X,
                  r=(Bsq[sl],), w=(Bst,))
            v = lambda fn, *a, **k: em.op("vector", fn, *a, r=(Bst,), w=(Bst,), **k)
            v("tensor_scalar", s_m, s_sum, 1.0 / D, None, ALU.mult)
            v("tensor_scalar", s_var, s_ss, 1.0 / D, None, ALU.mult)
            v("tensor_tensor", s_e, s_m, s_m, ALU.mult)
            v("tensor_tensor", s_var, s_var, s_e, ALU.subtract)
            v("tensor_scalar", s_var, s_var, 1e-5, None, ALU.add)
            em.op("scalar", "activation", s_e, s_var, AF.Ln, r=(Bst,), w=(Bst,))
            em.op("scalar", "activation", s_r0, s_e, AF.Exp, scale=-0.5,
                  r=(Bst,), w=(Bst,))
            v("tensor_tensor", s_e, s_var, s_r0, ALU.mult)
            v("tensor_tensor", s_e2, s_e, s_r0, ALU.mult)
            v("tensor_scalar", s_h, s_e2, -0.5, 1.5, ALU.mult, ALU.add)
            v("tensor_tensor", s_rstd, s_r0, s_h, ALU.mult)
            s_nmr = lst[sl][:, 9:10]
            em.op("vector", "scalar_tensor_tensor", s_nmr, s_m, -1.0, s_rstd,
                  ALU.mult, ALU.mult, r=(Bst,), w=(Bst,))
            em.op("scalar", "activation", xnt[sl][:, :], xt[sl3][:, :],
                  AF.Identity, bias=s_nmr, scale=s_rstd,
                  r=(Bxt[sl3], Bst), w=(Bxnt[sl],))
            for g8 in range(2):
                for j in range(8):
                    dc = g8 * 8 + j
                    em.op("tensor", "transpose", pbT[:, j * 128:(j + 1) * 128],
                          xnt[sl][:, dc * 128:(dc + 1) * 128], idb[:, :],
                          r=(Bxnt[sl], Bconst), w=(BpbT,))
                dst = xn_ft[:, :].rearrange("p (c n) -> p c n", c=DC)[
                    :, g8 * 8:(g8 + 1) * 8, tt * 128:(tt + 1) * 128]
                src = pbT[:, :].rearrange("p (c n) -> p c n", c=8)
                em.op("scalar", "copy", dst, src, r=(BpbT,), w=(Bxn,))

        if tap is not None:
            em.dma("qsync", xnd[tap][:, :], xn_ft[:, :], r=(Bxn,))
        arena.reset()

    def phase_p1(sub=4):
        em.tag = "p1"
        arena.reset()
        k64, Bk64 = arena.alloc("k64", [64, L0_HKV * T], BF16)
        q64, Bq64 = arena.alloc("q64", [64, L0_HQ * 512], BF16)
        v_aug, Bva = arena.alloc("v_aug", [128, NT * 520], BF16)
        attn_ft, Baf = arena.alloc("attn_ft", [128, DC * 512], BF16)
        stg, Bstg, wsl, Bwsl, wosl, Bwo, ptl, Bpt = [], [], [], [], [], [], [], []
        for i in range(2):
            t_, b_ = arena.alloc("attn_stg", [64, 512], BF16); stg.append(t_); Bstg.append(b_)
            t_, b_ = arena.alloc("wsl", [128, DC * 128], BF16); wsl.append(t_); Bwsl.append(b_)
            t_, b_ = arena.alloc("wosl", [128, DC * 128], BF16); wosl.append(t_); Bwo.append(b_)
        for i in range(6):
            t_, b_ = arena.alloc("pt", [128, 512], BF16); ptl.append(t_); Bpt.append(b_)
        mask01, Bm01 = arena.alloc("mask01", [128, 4 * 512], BF16)
        s0st, Bs0l = [], []
        for i in range(4):
            t_, b_ = arena.alloc("s0st", [128, 256], F32); s0st.append(t_); Bs0l.append(b_)
        atmp, Batmp, rnrow, Brn = [], [], [], []
        for i in range(2):
            t_, b_ = arena.alloc("atmp", [64, 512], F32)
            atmp.append(t_); Batmp.append(b_)
            t_, b_ = arena.alloc("rnrow", [1, 512], F32)
            rnrow.append(t_); Brn.append(b_)
        scj = [0]

        em.dma("qsync", mask01[:, :], mask01_h[:, :], w=(Bm01,))
        em.op("vector", "memset",
              v_aug[:, :].rearrange("p (t g c) -> p t g c", t=NT, g=8)[:, :, :, 64:65],
              1.0, w=(Bva,))

        def load_wpanel(slot, Bslot, staged_h, pnl, ncols):
            em.dma("qgpsimd", slot[:, 0:DC * ncols], staged_h[pnl],
                   w=(Bslot,))

        bankctr = [0]

        def pingbank():
            bankctr[0] = (bankctr[0] + 1) % 3
            return 4 + bankctr[0]

        for pnl in range(4):
            sl = pnl % 2
            load_wpanel(wsl[sl], Bwsl[sl], wkTs, pnl, 128)
            for gg in range(2):
                g = pnl * 2 + gg
                for tc in range(4):
                    bk = pingbank()
                    em.mm_group(
                        pb[bk][0:64, 0:512],
                        [(wsl[sl][:, dc * 128 + gg * 64: dc * 128 + gg * 64 + 64],
                          xn_ft[:, dc * T + tc * 512:dc * T + (tc + 1) * 512])
                         for dc in range(DC)],
                        r=(Bwsl[sl], Bxn), w=(Bpb[bk],))
                    em.op("scalar", "copy",
                          k64[:, g * T + tc * 512:g * T + (tc + 1) * 512],
                          pb[bk][0:64, 0:512], r=(Bpb[bk],), w=(Bk64,))
        for pnl in range(4):
            sl = pnl % 2
            load_wpanel(wsl[sl], Bwsl[sl], wvTs, pnl, 128)
            for tt in range(NT):
                bk = pingbank()
                em.mm_group(
                    pb[bk][0:128, 0:128],
                    [(xn_ft[:, dc * T + tt * 128:dc * T + (tt + 1) * 128],
                      wsl[sl][:, dc * 128:(dc + 1) * 128]) for dc in range(DC)],
                    r=(Bwsl[sl], Bxn), w=(Bpb[bk],))
                for gg in range(2):
                    g = pnl * 2 + gg
                    em.op("scalar", "copy",
                          v_aug[:, tt * 520 + g * 65: tt * 520 + g * 65 + 64],
                          pb[bk][0:128, gg * 64:gg * 64 + 64],
                          r=(Bpb[bk],), w=(Bva,))

        for qc in range(4 if sub >= 2 else 0):
            for pnl in range(16):
                sl = pnl % 2
                load_wpanel(wsl[sl], Bwsl[sl], wqTs, pnl, 128)
                for gg in range(2):
                    h = pnl * 2 + gg
                    bk = pingbank()
                    em.mm_group(
                        pb[bk][0:64, 0:512],
                        [(wsl[sl][:, dc * 128 + gg * 64: dc * 128 + gg * 64 + 64],
                          xn_ft[:, dc * T + qc * 512:dc * T + (qc + 1) * 512])
                         for dc in range(DC)],
                        r=(Bwsl[sl], Bxn), w=(Bpb[bk],))
                    em.op("vector", "tensor_scalar",
                          q64[:, h * 512:(h + 1) * 512],
                          pb[bk][0:64, 0:512], L0_SCALE, None, ALU.mult,
                          r=(Bpb[bk],), w=(Bq64,))
            nkt = 4 * qc + 4
            for hp in range(L0_HQ // 2 if sub >= 3 else 0):
                hA, hB = 2 * hp, 2 * hp + 1
                g = hA // 4
                pvA, pvB = 5, 6

                def score(h, kt, ptb, Bptb):
                    bk = scj[0] % 4
                    scj[0] += 1
                    em.op("tensor", "matmul", pb[bk][0:128, 0:512],
                          k64[:, g * T + kt * 128:g * T + (kt + 1) * 128],
                          q64[:, h * 512:(h + 1) * 512],
                          start=True, stop=True,
                          r=(Bk64, Bq64), w=(Bpb[bk],))
                    em.op("scalar", "activation", ptb[:, :],
                          pb[bk][0:128, 0:512], AF.Exp, bias=c_neg20,
                          r=(Bpb[bk], Bconst), w=(Bptb,))
                    if kt >= 4 * qc:
                        i = kt - 4 * qc
                        em.op("vector", "tensor_tensor", ptb[:, :],
                              ptb[:, :], mask01[:, i * 512:(i + 1) * 512],
                              ALU.mult, r=(Bptb, Bm01), w=(Bptb,))

                def pv(kt, pvb, ptb, Bptb):
                    em.op("tensor", "matmul", pb[pvb][0:65, 0:512],
                          v_aug[:, kt * 520 + g * 65: kt * 520 + (g + 1) * 65],
                          ptb[:, :], start=(kt == 0), stop=(kt == nkt - 1),
                          r=(Bva, Bptb), w=(Bpb[pvb],))

                for kt in range(nkt):
                    score(hA, kt, ptl[kt % 3], Bpt[kt % 3])
                    score(hB, kt, ptl[3 + kt % 3], Bpt[3 + kt % 3])
                    if kt > 1:
                        pv(kt - 2, pvA, ptl[(kt - 2) % 3], Bpt[(kt - 2) % 3])
                        pv(kt - 2, pvB, ptl[3 + (kt - 2) % 3],
                           Bpt[3 + (kt - 2) % 3])
                for ktl in (nkt - 2, nkt - 1):
                    pv(ktl, pvA, ptl[ktl % 3], Bpt[ktl % 3])
                    pv(ktl, pvB, ptl[3 + ktl % 3], Bpt[3 + ktl % 3])

                for (h, pvb, rn_, Brn_, at_, Bat_) in (
                        (hA, pvA, rnrow[0], Brn[0], atmp[0], Batmp[0]),
                        (hB, pvB, rnrow[1], Brn[1], atmp[1], Batmp[1])):
                    em.op("vector", "reciprocal", rn_[0:1, :],
                          pb[pvb][64:65, 0:512], r=(Bpb[pvb],), w=(Brn_,))
                    em.op("tensor", "matmul", pb[4][0:64, 0:512], ones_row64,
                          rn_[0:1, :], start=True, stop=True,
                          r=(Brn_, Bconst), w=(Bpb[4],))
                    em.op("scalar", "copy", at_[:, :], pb[pvb][0:64, 0:512],
                          r=(Bpb[pvb],), w=(Bat_,))
                    if h % 2 == 0:
                        em.op("vector", "tensor_tensor",
                              attn_ft[0:64, (h // 2) * 512:(h // 2 + 1) * 512],
                              at_[:, :], pb[4][0:64, 0:512], ALU.mult,
                              r=(Bat_, Bpb[4]), w=(Baf,))
                    else:
                        ssl = (h // 2) % 2
                        em.op("vector", "tensor_tensor", stg[ssl][:, :],
                              at_[:, :], pb[4][0:64, 0:512], ALU.mult,
                              r=(Bat_, Bpb[4]), w=(Bstg[ssl],))
                        em.dma("qsync",
                               attn_ft[64:128,
                                       (h // 2) * 512:(h // 2 + 1) * 512],
                               stg[ssl][:, :], r=(Bstg[ssl],), w=(Baf,))
            for np_ in range((16 if sub >= 5 else 2) if sub >= 4 else 0):
                sl = np_ % 2
                load_wpanel(wosl[sl], Bwo[sl], woTs, np_, 128)
                for ts in range(4):
                    bk = pingbank()
                    st_i = (np_ * 4 + ts) % 4
                    em.mm_group(
                        pb[bk][0:128, 0:128],
                        [(attn_ft[:, dc * 512 + ts * 128:dc * 512 + (ts + 1) * 128],
                          wosl[sl][:, dc * 128:(dc + 1) * 128]) for dc in range(DC)],
                        r=(Bwo[sl], Baf), w=(Bpb[bk],))
                    em.op("scalar", "copy", s0st[st_i][:, 0:128],
                          pb[bk][0:128, 0:128], r=(Bpb[bk],), w=(Bs0l[st_i],))
                    em.dma("qsync",
                           s_scr[qc * 512 + ts * 128: qc * 512 + (ts + 1) * 128,
                                 np_ * 128:(np_ + 1) * 128],
                           s0st[st_i][:, 0:128], r=(Bs0l[st_i],), w=(Bs_scr,))
        arena.reset()

    def phase_p2():
        em.tag = "p2"
        arena.reset()
        w0sb, Bw0 = arena.alloc("w0sb", [128, DC * D], BF16)
        scl, Bscl = arena.alloc("scl", [128, D], F32)
        o_, Bo, o2, Bo2, s_, Bs, x_, Bx = [], [], [], [], [], [], [], []
        stt, Bstt = [], []
        for i in range(2):
            t_, b_ = arena.alloc("p2o", [128, D], F32); o_.append(t_); Bo.append(b_)
            t_, b_ = arena.alloc("p2o2", [128, D], F32); o2.append(t_); Bo2.append(b_)
            t_, b_ = arena.alloc("p2s", [128, D], F32); s_.append(t_); Bs.append(b_)
            t_, b_ = arena.alloc("p2x", [128, D], F32); x_.append(t_); Bx.append(b_)
            t_, b_ = arena.alloc("p2st", [128, 24], F32); stt.append(t_); Bstt.append(b_)

        em.dma("qsync", scl[:, :], sclbc[:, :], w=(Bscl,))
        for ch in range(DC):
            stg0 = x_[ch % 2]
            Bstg0 = Bx[ch % 2]
            em.dma("qsync", stg0[:, :], w0_h[ch * 128:(ch + 1) * 128, :],
                   w=(Bstg0,))
            if ch % 2 == 0:
                em.op("vector", "tensor_copy", w0sb[:, ch * D:(ch + 1) * D],
                      stg0[:, :], r=(Bstg0,), w=(Bw0,))
            else:
                em.op("scalar", "copy", w0sb[:, ch * D:(ch + 1) * D],
                      stg0[:, :], r=(Bstg0,), w=(Bw0,))
        def p2_chain(tt):
            sl = tt % 2
            stp = stt[sl]
            Bst = Bstt[sl]
            ss = stp[:, 0:1]; ssm = stp[:, 1:2]; lnv = stp[:, 2:3]
            r0 = stp[:, 3:4]; e1 = stp[:, 4:5]; e2 = stp[:, 5:6]
            hh = stp[:, 6:7]; rv = stp[:, 7:8]; vn = stp[:, 8:9]
            e2x = stp[:, 9:10]; dd = stp[:, 10:11]; rd_ = stp[:, 11:12]
            th = stp[:, 12:13]; dn = stp[:, 13:14]; rdn = stp[:, 14:15]
            f1 = stp[:, 15:16]; onrm = stp[:, 16:17]; ron = stp[:, 17:18]
            cc = stp[:, 18:19]; mlt = stp[:, 19:20]
            em.op("scalar", "activation", o2[sl][:, :], o_[sl][:, :], AF.Square,
                  r=(Bo[sl],), w=(Bo2[sl],))
            em.op("vector", "reduce_sum", ss, o2[sl][:, :], axis=AX.X,
                  r=(Bo2[sl],), w=(Bst,))
            v = lambda fn, *a, **k: em.op("vector", fn, *a, r=(Bst,), w=(Bst,), **k)
            v("tensor_scalar", ssm, ss, 1e-14, None, ALU.max)
            em.op("scalar", "activation", lnv, ssm, AF.Ln, r=(Bst,), w=(Bst,))
            em.op("scalar", "activation", r0, lnv, AF.Exp, scale=-0.5,
                  r=(Bst,), w=(Bst,))
            v("tensor_tensor", e1, ssm, r0, ALU.mult)
            v("tensor_tensor", e2, e1, r0, ALU.mult)
            v("tensor_scalar", hh, e2, -0.5, 1.5, ALU.mult, ALU.add)
            v("tensor_tensor", rv, r0, hh, ALU.mult)
            v("tensor_tensor", vn, ssm, rv, ALU.mult)
            em.op("scalar", "activation", e2x, vn, AF.Exp, scale=2.0,
                  r=(Bst,), w=(Bst,))
            v("tensor_scalar", dd, e2x, 1.0, None, ALU.add)
            em.op("vector", "reciprocal", rd_, dd, r=(Bst,), w=(Bst,))
            v("tensor_scalar", th, rd_, -2.0, 1.0, ALU.mult, ALU.add)
            v("tensor_scalar", dn, vn, 1e-7, None, ALU.add)
            em.op("vector", "reciprocal", rdn, dn, r=(Bst,), w=(Bst,))
            v("tensor_tensor", f1, th, rdn, ALU.mult)
            v("tensor_tensor", onrm, f1, vn, ALU.mult)
            v("tensor_scalar", onrm, onrm, 1e-7, None, ALU.max)
            em.op("vector", "reciprocal", ron, onrm, r=(Bst,), w=(Bst,))
            v("tensor_scalar", cc, ron, 0.99, None, ALU.mult)
            v("tensor_scalar", cc, cc, 1.0, None, ALU.min)
            v("tensor_tensor", mlt, f1, cc, ALU.mult)
            em.op("vector", "tensor_scalar", o2[sl][:, :], o_[sl][:, :], mlt,
                  None, ALU.mult, r=(Bo[sl], Bst), w=(Bo2[sl],))
            em.op("scalar", "activation", o_[sl][:, :], o2[sl][:, :], AF.Silu,
                  r=(Bo2[sl],), w=(Bo[sl],))
            em.op("gpsimd", "tensor_tensor", s_[sl][:, :], s_[sl][:, :],
                  o_[sl][:, :], ALU.mult, r=(Bs[sl], Bo[sl]), w=(Bs[sl],))
            em.op("vector", "tensor_tensor", x_[sl][:, :], x_[sl][:, :],
                  s_[sl][:, :], ALU.add, r=(Bx[sl], Bs[sl]), w=(Bx[sl],))
            em.dma("qsync", x1_scr[tt * 128:(tt + 1) * 128, :], x_[sl][:, :],
                   r=(Bx[sl],), w=(Bx1_scr,))

        for tt in range(NT):
            sl = tt % 2
            em.dma("qsync", s_[sl][:, :], s_scr[tt * 128:(tt + 1) * 128, :],
                   r=(Bs_scr,), w=(Bs[sl],))
            em.dma("qsync", x_[sl][:, :], x_h[tt * 128:(tt + 1) * 128, :],
                   w=(Bx[sl],))
            for kc in range(4):
                bank = (tt * 4 + kc) % 4
                em.mm_group(
                    pb[bank][0:128, 0:512],
                    [(xn_ft[:, dc * T + tt * 128:dc * T + (tt + 1) * 128],
                      w0sb[:, dc * D + kc * 512:dc * D + kc * 512 + 512])
                     for dc in range(DC)],
                    r=(Bxn, Bw0), w=(Bpb[bank],))
                em.op("vector", "tensor_tensor",
                      o_[sl][:, kc * 512:(kc + 1) * 512],
                      pb[bank][0:128, 0:512], scl[:, kc * 512:(kc + 1) * 512],
                      ALU.mult, r=(Bpb[bank], Bscl), w=(Bo[sl],))
            if tt > 0:
                p2_chain(tt - 1)
        p2_chain(NT - 1)
        arena.reset()

    def phase_p4():
        em.tag = "p4"
        arena.reset()
        kv_sb, Bkv = arena.alloc("kv_sb", [128, D], F32)
        kv_bf, Bkvb = arena.alloc("kv_bf", [128, D], BF16)
        keep = arena.off
        wk_p, Bwk, wv_p, Bwv, kt_, Bkt, vt_, Bvt = [], [], [], [], [], [], [], []
        for i in range(2):
            t_, b_ = arena.alloc("wk_p", [128, DC * 256], BF16); wk_p.append(t_); Bwk.append(b_)
            t_, b_ = arena.alloc("wv_p", [128, DC * 256], BF16); wv_p.append(t_); Bwv.append(b_)
            t_, b_ = arena.alloc("ktile", [128, 256], BF16); kt_.append(t_); Bkt.append(b_)
            t_, b_ = arena.alloc("vtile", [128, 256], BF16); vt_.append(t_); Bvt.append(b_)

        gz4, Bgz4, gst4, Bgst4 = [], [], [], []
        for par in range(2):
            z5, Bz5 = [], []
            for i in range(5):
                t_, b_ = arena.alloc("gz4", [128, 8], F32)
                z5.append(t_); Bz5.append(b_)
            gz4.append(z5); Bgz4.append(Bz5)
            t_, b_ = arena.alloc("gst4", [128, 32], F32)
            gst4.append(t_); Bgst4.append(b_)

        em.op("vector", "memset", kv_sb[:, :], 0.0, w=(Bkv,))

        def load_np(slot, Bslot, staged_h, pnl, ncols):
            em.dma("qgpsimd", slot[:, 0:DC * ncols], staged_h[pnl],
                   w=(Bslot,))

        def emit_gate(gti):
            par = gti % 2
            em.mm_group(
                pb[6][0:128, 0:8],
                [(xn_ft[:, d2 * T + gti * 128: d2 * T + (gti + 1) * 128],
                  gw_sb[:, d2 * 8:(d2 + 1) * 8]) for d2 in range(DC)],
                r=(Bxn, Bconst), w=(Bpb[6],))
            gate_tile(gti, pb[6], Bpb[6], gz4[par], Bgz4[par],
                      gst4[par], Bgst4[par])

        def kv_op(pidx):
            nc8p = pidx // 16
            slp = pidx % 2
            ob = 4 + pidx % 2
            for hh in range(2):
                em.op("tensor", "matmul",
                      pb[ob][0:128, hh * 128:(hh + 1) * 128],
                      kt_[slp][:, hh * 128:(hh + 1) * 128],
                      vt_[slp][:, hh * 128:(hh + 1) * 128],
                      start=True, stop=True,
                      r=(Bkt[slp], Bvt[slp]), w=(Bpb[ob],))
            em.op("vector", "tensor_tensor",
                  kv_sb[:, nc8p * 256:(nc8p + 1) * 256],
                  kv_sb[:, nc8p * 256:(nc8p + 1) * 256],
                  pb[ob][0:128, 0:256], ALU.add,
                  r=(Bkv, Bpb[ob]), w=(Bkv,))

        for idx in range(128):
            nc8, tt = idx // 16, idx % 16
            sl = idx % 2
            wsl_ = nc8 % 2
            if tt == 0:
                load_np(wk_p[wsl_], Bwk[wsl_], Wks, nc8, 256)
                load_np(wv_p[wsl_], Bwv[wsl_], Wvs, nc8, 256)
            em.mm_group(
                pb[sl][0:128, 0:256],
                [(xn_ft[:, dc * T + tt * 128:dc * T + (tt + 1) * 128],
                  wk_p[wsl_][:, dc * 256:(dc + 1) * 256]) for dc in range(DC)],
                r=(Bxn, Bwk[wsl_]), w=(Bpb[sl],))
            em.op("vector", "tensor_scalar", kt_[sl][:, :],
                  pb[sl][0:128, 0:256], 0.0, L1_RSCALE, ALU.max, ALU.mult,
                  r=(Bpb[sl],), w=(Bkt[sl],))
            em.mm_group(
                pb[2 + sl][0:128, 0:256],
                [(xn_ft[:, dc * T + tt * 128:dc * T + (tt + 1) * 128],
                  wv_p[wsl_][:, dc * 256:(dc + 1) * 256]) for dc in range(DC)],
                r=(Bxn, Bwv[wsl_]), w=(Bpb[2 + sl],))
            em.op("scalar", "copy", vt_[sl][:, :], pb[2 + sl][0:128, 0:256],
                  r=(Bpb[2 + sl],), w=(Bvt[sl],))
            if idx > 0:
                kv_op(idx - 1)
            if idx % 8 == 7:
                emit_gate(idx // 8)
        kv_op(127)
        em.op("scalar", "copy", kv_bf[:, :], kv_sb[:, :], r=(Bkv,), w=(Bkvb,))

        arena.reset(keep)
        q_sb, _Bq0 = arena.alloc("q_sb", [128, L1_H * 512], BF16)
        u_sb, _Bu0 = arena.alloc("u_sb", [128, L1_H * 512], BF16)
        Bq = [em.buf(f"qsl{i}", fence=[_Bq0]) for i in range(L1_H)]
        Bu = [em.buf(f"usl{i}", fence=[_Bu0]) for i in range(L1_H)]
        out_sb, Bo4 = arena.alloc("out_sb", [128, L1_H * 512], BF16)
        z_sb, Bz = arena.alloc("z_sb", [128, DC * 512], BF16)
        wp, Bwp = [], []
        for i in range(3):
            t_, b_ = arena.alloc("wp4", [128, DC * 512], BF16); wp.append(t_); Bwp.append(b_)
        sqs, Bsq = [], []
        for i in range(2):
            t_, b_ = arena.alloc("sqs", [128, 512], F32)
            sqs.append(t_); Bsq.append(b_)
        s1l, Bs1 = [], []
        for i in range(2):
            t_, b_ = arena.alloc("s1st", [128, 512], F32); s1l.append(t_); Bs1.append(b_)
        rn4, Brn4 = arena.alloc("rn4", [1, 512], F32)
        r1 = st2[0:1, 0:512]; r2 = st2[0:1, 512:1024]

        for qc in range(4):
            def head_step(h):
                if h < L1_H:
                    bank = (2, 3, 6)[h % 3]
                    em.op("tensor", "matmul", pb[bank][0:128, 0:512],
                          kv_bf[:, h * 128:(h + 1) * 128],
                          q_sb[:, h * 512:(h + 1) * 512], start=True, stop=True,
                          r=(Bkvb, Bq[h]), w=(Bpb[bank],))
                    em.op("scalar", "activation", sqs[h % 2][:, :],
                          pb[bank][0:128, 0:512],
                          AF.Square, r=(Bpb[bank],), w=(Bsq[h % 2],))
                    em.op("scalar", "copy", out_sb[:, h * 512:(h + 1) * 512],
                          pb[bank][0:128, 0:512], r=(Bpb[bank],), w=(Bo4,))
                if 1 <= h < L1_H + 1:
                    h1 = h - 1
                    nb = 4 + h1 % 2
                    r1h = st2[0:1, (h1 % 2) * 512:(h1 % 2) * 512 + 512]
                    em.op("tensor", "matmul", pb[nb][0:1, 0:512], ones128,
                          sqs[h1 % 2][:, :], start=True, stop=True,
                          r=(Bsq[h1 % 2], Bconst), w=(Bpb[nb],))
                    em.op("vector", "tensor_scalar", r1h, pb[nb][0:1, 0:512],
                          1.0 / L1_DH, 1e-24, ALU.mult, ALU.max,
                          r=(Bpb[nb],), w=(Bst2,))
                    em.op("scalar", "activation", r1h, r1h, AF.Ln,
                          r=(Bst2,), w=(Bst2,))
                    em.op("scalar", "activation", r1h, r1h, AF.Exp, scale=-0.5,
                          r=(Bst2,), w=(Bst2,))
                if 2 <= h:
                    h2 = h - 2
                    nb2 = 4 + h2 % 2
                    r1h2 = st2[0:1, (h2 % 2) * 512:(h2 % 2) * 512 + 512]
                    em.op("tensor", "matmul", pb[nb2][0:128, 0:512],
                          ones_row128, r1h2, start=True, stop=True,
                          r=(Bst2, Bconst), w=(Bpb[nb2],))
                    em.op("vector", "tensor_tensor",
                          z_sb[:, h2 * 512:(h2 + 1) * 512],
                          out_sb[:, h2 * 512:(h2 + 1) * 512],
                          u_sb[:, h2 * 512:(h2 + 1) * 512], ALU.mult,
                          r=(Bo4, Bu[h2]), w=(Bz,))
                    em.op("vector", "tensor_tensor",
                          z_sb[:, h2 * 512:(h2 + 1) * 512],
                          z_sb[:, h2 * 512:(h2 + 1) * 512],
                          pb[nb2][0:128, 0:512],
                          ALU.mult, r=(Bz, Bpb[nb2]), w=(Bz,))

            step = [0]
            for (W_s, dstsb, Bdst, act) in ((Wqs, q_sb, Bq, "relu"),
                                            (Wus, u_sb, Bu, "silu")):
                for np8 in range(8):
                    sl = np8 % 3
                    load_np(wp[sl], Bwp[sl], W_s, np8, 256)
                    for sub in range(2):
                        nt = np8 * 2 + sub
                        bank = nt % 2
                        em.mm_group(
                            pb[bank][0:128, 0:512],
                            [(wp[sl][:, dc * 256 + sub * 128:dc * 256 + sub * 128 + 128],
                              xn_ft[:, dc * T + qc * 512:dc * T + (qc + 1) * 512])
                             for dc in range(DC)],
                            r=(Bwp[sl], Bxn), w=(Bpb[bank],))
                        if act == "relu":
                            em.op("vector", "tensor_scalar",
                                  dstsb[:, nt * 512:(nt + 1) * 512],
                                  pb[bank][0:128, 0:512], 0.0, L1_RSCALE,
                                  ALU.max, ALU.mult,
                                  r=(Bpb[bank],), w=(Bdst[nt],))
                        else:
                            em.op("scalar", "activation",
                                  dstsb[:, nt * 512:(nt + 1) * 512],
                                  pb[bank][0:128, 0:512], AF.Silu,
                                  r=(Bpb[bank],), w=(Bdst[nt],))
            while step[0] < L1_H + 2:
                head_step(step[0]); step[0] += 1
            for np8 in range(4):
                sl = np8 % 3
                load_np(wp[sl], Bwp[sl], Wos, np8, 512)
                for ts in range(4):
                    bank = 6 if ts % 2 == 0 else 3
                    ssl = ts % 2
                    em.mm_group(
                        pb[bank][0:128, 0:512],
                        [(z_sb[:, dc * 512 + ts * 128:dc * 512 + ts * 128 + 128],
                          wp[sl][:, dc * 512:(dc + 1) * 512]) for dc in range(DC)],
                        r=(Bwp[sl], Bz), w=(Bpb[bank],))
                    em.op("scalar", "copy", s1l[ssl][:, :],
                          pb[bank][0:128, 0:512], r=(Bpb[bank],), w=(Bs1[ssl],))
                    em.dma("qsync",
                           s_scr[qc * 512 + ts * 128:qc * 512 + (ts + 1) * 128,
                                 np8 * 512:(np8 + 1) * 512],
                           s1l[ssl][:, :], r=(Bs1[ssl],), w=(Bs_scr,))
        arena.reset()

    def phase_p5():
        em.tag = "p5"
        arena.reset()
        H_sb, BH = arena.alloc("H_sb", [128, NHC * T], BF16)
        w13, Bw13 = [], []
        for i in range(4):
            t_, b_ = arena.alloc("w13", [128, DC * 128], BF16); w13.append(t_); Bw13.append(b_)
        w2sl, Bw2 = [], []
        for i in range(2):
            t_, b_ = arena.alloc("w2sl", [128, NHC * 512], BF16); w2sl.append(t_); Bw2.append(b_)
        B_sb, BB = arena.alloc("B_sb", [128, T], F32)
        NACC = 3
        acci, Bai, acco, Bao = [], [], [], []
        for i in range(NACC):
            t_, b_ = arena.alloc("acci", [128, 512], F32); acci.append(t_); Bai.append(b_)
            t_, b_ = arena.alloc("acco", [128, 512], F32); acco.append(t_); Bao.append(b_)
        a_sc, Ba = arena.alloc("a_sc", [128, 512], F32)
        sl_sc, Bsl = arena.alloc("sl_sc", [128, 512], F32)
        grows, Bgr = arena.alloc("grows", [8, T], F32)
        oh_sb, Boh = arena.alloc("oh_sb", [8, 8 * 128], F32)

        em.dma("qsync", oh_sb[:, :], oh_h[:, :], w=(Boh,))
        for tt in range(NT):
            em.op("tensor", "transpose", pb[0][0:8, 0:128],
                  gates[:, tt * 8:(tt + 1) * 8], idf[:, :],
                  r=(Bgates, Bconst), w=(Bpb[0],))
            em.op("vector", "tensor_copy", grows[0:8, tt * 128:(tt + 1) * 128],
                  pb[0][0:8, 0:128], r=(Bpb[0],), w=(Bgr,))

        experts = [("shared", NEXP)] + [(e, e) for e in range(NEXP)]
        for (e, ei) in experts:
            shared = (e == "shared")
            if not shared:
                for tc in range(4):
                    em.op("tensor", "matmul", pb[6][0:128, 0:512],
                          oh_sb[:, e * 128:(e + 1) * 128],
                          grows[0:8, tc * 512:(tc + 1) * 512],
                          start=True, stop=True, r=(Boh, Bgr), w=(Bpb[6],))
                    em.op("vector", "tensor_copy",
                          B_sb[:, tc * 512:(tc + 1) * 512],
                          pb[6][0:128, 0:512], r=(Bpb[6],), w=(BB,))
            for hc in range(NHC):
                s1_ = hc % 2
                s3_ = 2 + hc % 2
                em.dma("qgpsimd", w13[s1_][:, :], w1s_h[ei, hc],
                       w=(Bw13[s1_],))
                em.dma("qgpsimd", w13[s3_][:, :], w3s_h[ei, hc],
                       w=(Bw13[s3_],))
                for tc in range(4):
                    b1 = tc % 2
                    b3 = 2 + tc % 2
                    em.mm_group(
                        pb[b1][0:128, 0:512],
                        [(w13[s1_][:, dc * 128:(dc + 1) * 128],
                          xn_ft[:, dc * T + tc * 512:dc * T + (tc + 1) * 512])
                         for dc in range(DC)],
                        r=(Bw13[s1_], Bxn), w=(Bpb[b1],))
                    em.mm_group(
                        pb[b3][0:128, 0:512],
                        [(w13[s3_][:, dc * 128:(dc + 1) * 128],
                          xn_ft[:, dc * T + tc * 512:dc * T + (tc + 1) * 512])
                         for dc in range(DC)],
                        r=(Bw13[s3_], Bxn), w=(Bpb[b3],))
                    if shared:
                        em.op("scalar", "activation", sl_sc[:, :],
                              pb[b1][0:128, 0:512], AF.Silu,
                              r=(Bpb[b1],), w=(Bsl,))
                    else:
                        em.op("vector", "tensor_tensor", a_sc[:, :],
                              pb[b1][0:128, 0:512],
                              B_sb[:, tc * 512:(tc + 1) * 512], ALU.mult,
                              r=(Bpb[b1], BB), w=(Ba,))
                        em.op("scalar", "activation", sl_sc[:, :], a_sc[:, :],
                              AF.Silu, r=(Ba,), w=(Bsl,))
                    em.op("vector", "tensor_tensor",
                          H_sb[:, hc * T + tc * 512:hc * T + (tc + 1) * 512],
                          sl_sc[:, :], pb[b3][0:128, 0:512], ALU.mult,
                          r=(Bsl, Bpb[b3]), w=(BH,))
            w2banks = (3, 4, 5, 6)
            iters = [(nc4, ts) for nc4 in range(4) for ts in range(NT)]
            last = (not shared) and e == NEXP - 1
            if last:
                s_ch = [B_sb[:, 0:512], B_sb[:, 512:1024]]
                x1_ch = [B_sb[:, 1024:1536], B_sb[:, 1536:2048]]
                Bs_ch = [em.buf(f"lsch{i}", fence=[BB]) for i in range(2)]
                Bx1_ch = [em.buf(f"lx1ch{i}", fence=[BB]) for i in range(2)]
                fgt = [a_sc, sl_sc]
                Bfgt = [em.buf("lfg0", fence=[Ba]),
                        em.buf("lfg1", fence=[Bsl])]

                def sx_load(i):
                    if i >= 64:
                        return
                    nc4i, tsi = iters[i]
                    s2 = i % 2
                    em.dma("qsync", s_ch[s2],
                           s_scr[tsi * 128:(tsi + 1) * 128,
                                 nc4i * 512:(nc4i + 1) * 512],
                           r=(Bs_scr,), w=(Bs_ch[s2],))
                    em.dma("qsync", x1_ch[s2],
                           x1_scr[tsi * 128:(tsi + 1) * 128,
                                  nc4i * 512:(nc4i + 1) * 512],
                           r=(Bx1_scr,), w=(Bx1_ch[s2],))
                sx_load(0)
                sx_load(1)

            def acci_load(i):
                if shared or i >= 64:
                    return
                nc4i, tsi = iters[i]
                sli = i % NACC
                em.dma("qsync", acci[sli][:, :],
                       acc_scr[tsi * 128:(tsi + 1) * 128,
                               nc4i * 512:(nc4i + 1) * 512],
                       r=(Bacc_scr[nc4i * 16 + tsi],), w=(Bai[sli],))

            if not shared:
                for j in range(NACC):
                    acci_load(j)
            for i, (nc4, ts) in enumerate(iters):
                if i % 16 == 0:
                    sw = nc4 % 2
                    em.dma("qgpsimd", w2sl[sw][:, :], w2s_h[ei, nc4],
                           w=(Bw2[sw],))
                sl = i % NACC
                bank = w2banks[i % 4]
                swc = nc4 % 2
                em.mm_group(
                    pb[bank][0:128, 0:512],
                    [(H_sb[:, hc * T + ts * 128:hc * T + ts * 128 + 128],
                      w2sl[swc][:, hc * 512:(hc + 1) * 512])
                     for hc in range(NHC)],
                    r=(BH, Bw2[swc]), w=(Bpb[bank],))
                if shared:
                    em.op("scalar", "copy", acco[sl][:, :],
                          pb[bank][0:128, 0:512], r=(Bpb[bank],),
                          w=(Bao[sl],))
                else:
                    em.op("vector", "scalar_tensor_tensor", acco[sl][:, :],
                          pb[bank][0:128, 0:512],
                          gates[:, ts * 8 + e: ts * 8 + e + 1],
                          acci[sl][:, :], ALU.mult, ALU.add,
                          r=(Bpb[bank], Bgates, Bai[sl]), w=(Bao[sl],))
                if last:
                    s2 = i % 2
                    em.op("scalar", "activation", fgt[s2][:, :],
                          acco[sl][:, :], AF.Silu,
                          r=(Bao[sl],), w=(Bfgt[s2],))
                    em.op("vector", "tensor_tensor", fgt[s2][:, :],
                          fgt[s2][:, :], s_ch[s2], ALU.mult,
                          r=(Bfgt[s2], Bs_ch[s2]), w=(Bfgt[s2],))
                    em.op("vector", "tensor_tensor", x1_ch[s2], x1_ch[s2],
                          fgt[s2][:, :], ALU.add,
                          r=(Bx1_ch[s2], Bfgt[s2]), w=(Bx1_ch[s2],))
                    em.dma("qsync",
                           y_h[ts * 128:(ts + 1) * 128,
                               nc4 * 512:(nc4 + 1) * 512],
                           x1_ch[s2], r=(Bx1_ch[s2],))
                    sx_load(i + 2)
                else:
                    em.dma("qsync",
                           acc_scr[ts * 128:(ts + 1) * 128,
                                   nc4 * 512:(nc4 + 1) * 512],
                           acco[sl][:, :], r=(Bao[sl],),
                           w=(Bacc_scr[nc4 * 16 + ts],))
                acci_load(i + NACC)


    if "ln0" in phases:
        phase_ln(x_h, None, layer=0,
                 tap="xnd0" if "xnd0" in debug_taps else None)
    if "p1" in phases:
        import os as _os
        phase_p1(sub=int(_os.environ.get("P1SUB", "5")))
    if "p2" in phases:
        phase_p2()
    if "ln1" in phases:
        phase_ln(x1_scr, Bx1_scr, layer=1,
                 tap="xnd1" if "xnd1" in debug_taps else None)
    if "p4" in phases:
        phase_p4()
    if "p5" in phases:
        phase_p5()
    if "p5" not in phases:
        Bdummy = em.buf("ydummy")
        for tt in range(NT):
            pass

    from contextlib import ExitStack
    es = ExitStack()
    sems = {}
    for s in list(ENGS) + list(QSEMS):
        sems[s] = es.enter_context(nc.semaphore(f"sem_{s}"))
    em.finalize(nc, sems)
    es.close()
    return nc


def _host_consts():
    cst = np.zeros((128, 160), np.float32)
    cst[:, 0] = -20.0
    cst[:, 1] = 1.0
    cst[:, 8:136] = 1.0
    idb = np.eye(128, dtype=ml_dtypes.bfloat16)
    idf = np.eye(128, dtype=np.float32)
    oh = np.zeros((8, 8 * 128), np.float32)
    for e in range(8):
        oh[e, e * 128:(e + 1) * 128] = 1.0
    mask01 = np.zeros((128, 4 * 512), ml_dtypes.bfloat16)
    kk = np.arange(128)[:, None]
    q = np.arange(512)[None, :]
    for i in range(4):
        mask01[:, i * 512:(i + 1) * 512] = ((128 * i + kk) <= q).astype(
            ml_dtypes.bfloat16)
    return cst, idb, idf, oh, mask01


def _stage_panels(mT, ncols):
    Dn, X = mT.shape
    npan = X // ncols
    return np.ascontiguousarray(
        mT.reshape(16, 128, npan, ncols).transpose(2, 1, 0, 3)
          .reshape(npan, 128, 16 * ncols))


def _stage_w13(w):
    return np.ascontiguousarray(
        w.reshape(16, 128, 14, 128).transpose(2, 1, 0, 3).reshape(14, 128, 2048))


def _stage_w2(w):
    return np.ascontiguousarray(
        w.reshape(14, 128, 4, 512).transpose(2, 1, 0, 3).reshape(4, 128, 14 * 512))


def host_inputs(inputs):
    f32 = lambda a: np.ascontiguousarray(np.asarray(a), dtype=np.float32)
    x = f32(inputs["L0_attn_x"])
    scl = f32(inputs["L0_ffn_scale"])
    biasE = f32(inputs["L1_ffn_expert_bias"])
    cst, idb, idf, oh, mask01 = _host_consts()
    w1 = f32(inputs["L1_ffn_w1"]); w2 = f32(inputs["L1_ffn_w2"])
    w3 = f32(inputs["L1_ffn_w3"])
    shared = {
        "wqTs": _stage_panels(f32(inputs["L0_attn_wq"]).T, 128),
        "wkTs": _stage_panels(f32(inputs["L0_attn_wk"]).T, 128),
        "wvTs": _stage_panels(f32(inputs["L0_attn_wv"]).T, 128),
        "woTs": _stage_panels(f32(inputs["L0_attn_wo"]).T, 128),
        "w0": f32(inputs["L0_ffn_weight"]),
        "sclbc": np.ascontiguousarray(np.broadcast_to(scl[None, :], (128, D))),
        "Wqs": _stage_panels(f32(inputs["L1_attn_Wq"]), 256),
        "Wks": _stage_panels(f32(inputs["L1_attn_Wk"]), 256),
        "Wvs": _stage_panels(f32(inputs["L1_attn_Wv"]), 256),
        "Wus": _stage_panels(f32(inputs["L1_attn_Wu"]), 256),
        "Wos": _stage_panels(f32(inputs["L1_attn_Wo"]), 512),
        "gw": f32(inputs["L1_ffn_gate_w"]),
        "w1s": np.stack([_stage_w13(w1[e]) for e in range(NEXP)]
                        + [_stage_w13(f32(inputs["L1_ffn_sw1"]))]),
        "w3s": np.stack([_stage_w13(w3[e]) for e in range(NEXP)]
                        + [_stage_w13(f32(inputs["L1_ffn_sw3"]))]),
        "w2s": np.stack([_stage_w2(w2[e]) for e in range(NEXP)]
                        + [_stage_w2(f32(inputs["L1_ffn_sw2"]))]),
        "biasE": np.ascontiguousarray(np.broadcast_to(biasE[None, :], (128, 8))),
        "cst": cst, "idb": idb, "idf": idf, "oh": oh, "mask01": mask01,
    }
    return x, shared


_NC_CACHE = {}


def kernel(**inputs):
    x, shared = host_inputs(inputs)
    if "nc" not in _NC_CACHE:
        _NC_CACHE["nc"] = build_program()
    nc = _NC_CACHE["nc"]
    in_maps = [dict(shared, x=np.ascontiguousarray(x[c])) for c in range(8)]
    res = run_bass_kernel_spmd(nc, in_maps, core_ids=list(range(8)))
    return np.stack([res.results[c]["y"] for c in range(8)], axis=0)
